# version 02
# baseline (speedup 1.0000x reference)
import numpy as np
import ml_dtypes

import concourse.bass as bass
import concourse.mybir as mybir
from concourse.bass_utils import run_bass_kernel_spmd

F32 = mybir.dt.float32
BF16 = mybir.dt.bfloat16
I32 = mybir.dt.int32
AF = mybir.ActivationFunctionType
ALU = mybir.AluOpType
AX = mybir.AxisListType
BF = ml_dtypes.bfloat16

B, S, D, T = 2, 8192, 1024, 2048
H, HD = 16, 64
E8, F1, H3 = 8, 4096, 3072
NCORE = 8
DC = D // 128
BLK = 2048 + 64
GROUPS = [[0, 1, 2, 3], [4, 5, 6, 7]]
L0SCALE = float(1.0 / np.sqrt(S))
L1SCALE = float(1.0 / np.sqrt(HD))
EPS = 1e-5
INVD = 1.0 / D
Q127 = 1.0 / 127.0

LAUNCH_PLAN = [["l0a", "l0f", "l1a", "l1f"]]
DEBUG_PHASE = 99
DEBUG_DUMP = []
DETECT_RACES = True


def _sl(s, n):
    return slice(s, s + n)


class Prog:
    ENGS = ("tensor", "vector", "scalar", "gpsimd", "sync")

    def __init__(self, nc):
        self.nc = nc
        self.q = {e: [] for e in self.ENGS}
        self.sems = {}
        self.cnt = {}

    def sem(self, name):
        if name not in self.sems:
            self.sems[name] = self.nc.alloc_semaphore(name)
            self.cnt[name] = 0
        return self.sems[name]

    def emit(self, eng, fn, inc=None):
        incs = []
        if inc is not None:
            for s, a in (inc if isinstance(inc, list) else [inc]):
                self.sem(s)
                self.cnt[s] += a
                incs.append((self.sems[s], a))

        def run(e, fn=fn, incs=incs):
            ins = fn(e)
            for h, a in incs:
                ins = ins.then_inc(h, a)
            return ins
        self.q[eng].append(run)

    def wait(self, eng, s, val):
        if val <= 0:
            return
        h = self.sem(s)
        self.q[eng].append(lambda e, h=h, v=val: e.wait_ge(h, v))

    def val(self, s):
        self.sem(s)
        return self.cnt[s]

    def barrier(self, tag):
        vals = {sname: self.cnt[sname] for sname in self.cnt}
        for eng in self.ENGS:
            for sname, v in vals.items():
                self.wait(eng, sname, v)

    def finalize(self):
        with self.nc.Block() as block:
            for engname in self.ENGS:
                ops = self.q[engname]

                def body(eng, ops=ops):
                    for f in ops:
                        f(eng)
                getattr(block, engname)(body)


def build_program(sands):
    nc = bass.Bass("TRN2", target_bir_lowering=False, debug=False,
                   detect_race_conditions=DETECT_RACES)
    pr = Prog(nc)
    tn = {}
    A = {}
    P = {}

    tn["x_in"] = nc.dram_tensor("x_in", [D, T], F32, kind="ExternalInput")
    tn["x_out"] = nc.dram_tensor("x_out", [D, T], F32, kind="ExternalOutput")

    if "l0a" in sands:
        for w in ("wqT", "wkT", "wvT"):
            tn[w] = nc.dram_tensor(w, [D, D], F32, kind="ExternalInput")
        tn["bqkv"] = nc.dram_tensor("bqkv", [128, 24], F32, kind="ExternalInput")
        tn["gb"] = nc.dram_tensor("gb", [128, 16], F32, kind="ExternalInput")
        tn["bvrow"] = nc.dram_tensor("bvrow", [1, D], F32, kind="ExternalInput")
    if "l0f" in sands:
        tn["wgt"] = nc.dram_tensor("wgt", [D, E8], F32, kind="ExternalInput")
        tn["bg"] = nc.dram_tensor("bg", [E8, 1], F32, kind="ExternalInput")
        tn["w1t"] = nc.dram_tensor("w1t", [E8, 32, D, 128], F32, kind="ExternalInput")
        tn["b1t"] = nc.dram_tensor("b1t", [E8, 128, 32], F32, kind="ExternalInput")
        tn["w2t"] = nc.dram_tensor("w2t", [E8, DC, F1, 128], F32, kind="ExternalInput")
        tn["b2"] = nc.dram_tensor("b2", [E8, D], F32, kind="ExternalInput")
    if "l1a" in sands:
        tn["wqT1a"] = nc.dram_tensor("wqT1a", [D, D], F32, kind="ExternalInput")
        tn["wqT1b"] = nc.dram_tensor("wqT1b", [D, D], F32, kind="ExternalInput")
        tn["wk1a"] = nc.dram_tensor("wk1a", [D, HD], F32, kind="ExternalInput")
        tn["wk1b"] = nc.dram_tensor("wk1b", [D, HD], F32, kind="ExternalInput")
        tn["wv1"] = nc.dram_tensor("wv1", [D, HD], F32, kind="ExternalInput")
        tn["woT"] = nc.dram_tensor("woT", [D, D], F32, kind="ExternalInput")
        tn["cos4"] = nc.dram_tensor("cos4", [128, T], F32, kind="ExternalInput")
        tn["sin4"] = nc.dram_tensor("sin4", [128, T], F32, kind="ExternalInput")
    if "l1f" in sands:
        tn["wgT1"] = nc.dram_tensor("wgT1", [24, D, 128], I32, kind="ExternalInput")
        tn["wuT1"] = nc.dram_tensor("wuT1", [24, D, 128], I32, kind="ExternalInput")
        tn["wdT1"] = nc.dram_tensor("wdT1", [DC, H3, 128], I32, kind="ExternalInput")

    tn["statb"] = nc.dram_tensor("statb", [1, 4096], F32)
    tn["rowb"] = nc.dram_tensor("rowb", [1, 1024], F32)
    if "l0a" in sands or "l1a" in sands:
        tn["g1_src"] = nc.dram_tensor("g1_src", [BLK, T], BF16)
        tn["g1_out"] = nc.dram_tensor("g1_out", [4 * BLK, T], BF16)
        tn["g2_src"] = nc.dram_tensor("g2_src", [T, D], BF16)
        tn["g2_out"] = nc.dram_tensor("g2_out", [S, D], BF16)
        tn["g3_src"] = nc.dram_tensor("g3_src", [256, S], BF16)
        tn["g3_out"] = nc.dram_tensor("g3_out", [D, S], BF16)
    if "l0f" in sands:
        tn["w8d"] = nc.dram_tensor("w8d", [E8, T], BF16)

    kl = np.arange(128)[:, None]
    ql = np.arange(512)[None, :]
    if "l0a" in sands:
        m0 = np.concatenate([(kl + o >= ql).astype(BF) for o in (0, 128, 256, 384)],
                            axis=1)
        tn["mask0"] = nc.inline_tensor(np.ascontiguousarray(m0), "mask0")
    if "l1a" in sands:
        m1 = np.concatenate([(kl + o <= ql).astype(BF) for o in (0, 128, 256, 384)],
                            axis=1)
        tn["mask1"] = nc.inline_tensor(np.ascontiguousarray(m1), "mask1")
    if "l0f" in sands:
        tn["ident"] = nc.inline_tensor(np.eye(128, dtype=BF), "ident")

    BASE = (nc.sbuf_base + 31) // 32 * 32
    AB = BASE + 131072
    TAIL = BASE + 202240

    def sb(name, shape, dtype, off):
        A[name] = nc.alloc_sbuf_tensor_at(name, shape, dtype, offset=off)
        return A[name]

    sb("X", [128, DC * T], F32, BASE + 0)
    sb("LNX", [128, DC * T], BF16, BASE + 65536)
    sb("Z", [128, DC * T], BF16, BASE + 98304)
    sb("TMP", [128, T], F32, AB + 0)
    sb("SQ", [128, 1024], F32, AB + 0)
    sb("ABC", [128, T], F32, AB + 8192)
    sb("BBC", [128, T], F32, AB + 16384)
    sb("R0", [1, 512], F32, AB + 24576)
    sb("R1", [1, 512], F32, AB + 26624)
    sb("R2", [1, 512], F32, AB + 28672)
    sb("WB0", [128, 8192], BF16, AB + 30720)
    sb("STG", [128, T], BF16, AB + 47104)
    sb("ONESF", [128, 1], F32, TAIL + 0)
    sb("ONESB", [128, 1], BF16, TAIL + 32)
    sb("SCL", [128, 4], F32, TAIL + 64)
    if "l0a" in sands or "l1a" in sands:
        sb("QS0", [128, S], BF16, AB + 0)
        sb("KP", [128, S], BF16, AB + 16384)
        sb("VS0", [128, 64 * 64], BF16, AB + 32768)
        sb("VS1", [128, 64 * 64], BF16, AB + 40960)
        sb("EB0", [128, 1024], BF16, AB + 49152)
        sb("EB1", [128, 1024], BF16, AB + 51200)
        sb("EM", [128, 2048], BF16, AB + 53248)
        sb("PSTG", [128, 1024], BF16, AB + 57344)
        sb("RB", [128, 1024], F32, AB + 59392)
        sb("RRX", [128, 512], F32, AB + 63488)
        sb("MSK", [128, 2048], BF16, TAIL + 800)
    if "l0a" in sands:
        sb("BVB", [128, D], BF16, TAIL + 4896)
        sb("gbs", [128, 16], F32, TAIL + 96)
        sb("bqkvs", [128, 24], F32, TAIL + 160)
    if "l1a" in sands:
        sb("COS", [128, T], F32, AB + 0)
        sb("SIN", [128, T], F32, AB + 8192)
        sb("WRKA", [128, T], BF16, AB + 16384)
        sb("WRKB", [128, T], BF16, AB + 20480)
        sb("KTA", [64, T], BF16, AB + 24576)
        sb("KTB", [64, T], BF16, AB + 28672)
        sb("WVs", [128, 512], BF16, AB + 28672)
    if "l0f" in sands:
        sb("HSL", [128, 32 * 512], BF16, AB + 0)
        sb("W1B", [128, 2048], BF16, AB + 32768)
        sb("W2B", [128, 2 * F1], BF16, AB + 36864)
        sb("WBC", [128, 1024], F32, AB + 53248)
        sb("TMI", [128, 512], F32, AB + 57344)
        sb("GS", [8, T], BF16, AB + 59392)
        sb("GTA", [128, 128], BF16, AB + 63488)
        sb("GEB", [128, 128], F32, AB + 63744)
        sb("GEC", [128, 128], F32, AB + 64256)
        sb("REDA", [128, 48], F32, AB + 64768)
        sb("W8B", [8, T], BF16, AB + 65024)
        sb("B2S", [8, D], BF16, AB + 69120)
        sb("WGS", [128, 64], BF16, TAIL + 416)
        sb("IDS", [128, 128], BF16, TAIL + 544)
        sb("bgs", [8, 1], F32, TAIL + 6432)
        sb("b1s", [128, 32], F32, TAIL + 6464)
    if "l1f" in sands:
        sb("HSL1", [128, 24 * 1024], BF16, AB + 0)
        sb("WI", [128, 3072], I32, AB + 49152)
        sb("WF0", [128, 3072], BF16, AB + 61440)
        sb("WF1", [128, 1024], BF16, AB + 67584)
        sb("SIL", [128, 512], BF16, AB + 69632)

    P["B0"] = nc.alloc_psum_tensor("B0", [128, 512], F32)
    P["B1"] = nc.alloc_psum_tensor("B1", [128, 512], F32)
    P["B2"] = nc.alloc_psum_tensor("B2", [128, 512], F32)
    P["B3"] = nc.alloc_psum_tensor("B3", [128, 512], F32)
    P["ACC"] = nc.alloc_psum_tensor("ACC", [128, 512], F32)
    P["DEN"] = nc.alloc_psum_tensor("DEN", [128, 512], F32)
    P["PST"] = nc.alloc_psum_tensor("PST", [8, 512], F32)
    P["PTR"] = nc.alloc_psum_tensor("PTR", [128, 128], F32)

    pid_cache = {}

    def get_sym(e, name):
        key = (str(e.engine), name)
        if key not in pid_cache:
            pid = nc.partition_id(engines=[e.engine])
            mod = pid % 4
            pid_cache[(str(e.engine), "hg256")] = mod * 256
            pid_cache[(str(e.engine), "tcol")] = mod * T
        return pid_cache[key]

    ctx = dict(pr=pr, nc=nc, tn=tn, A=A, P=P, get_sym=get_sym)

    pr.emit("sync", lambda e: e.dma_start(
        out=A["X"].ap().rearrange("p (c t) -> p c t", t=T),
        in_=tn["x_in"].ap().rearrange("(c p) t -> p c t", p=128)), inc=("ldw", 16))
    pr.emit("vector", lambda e: e.memset(A["ONESF"][:, :], 1.0))
    pr.emit("vector", lambda e: e.memset(A["ONESB"][:, :], 1.0))
    pr.emit("vector", lambda e: e.memset(A["SCL"][:, 0:1], L0SCALE))
    pr.emit("vector", lambda e: e.memset(A["SCL"][:, 1:2], L1SCALE))
    pr.emit("vector", lambda e: e.memset(A["SCL"][:, 2:3], Q127))
    pr.emit("vector", lambda e: e.memset(A["SCL"][:, 3:4], 0.0), inc=("dve", 1))
    m = pr.val("dve")
    for eng in ("tensor", "scalar", "gpsimd", "sync"):
        pr.wait(eng, "dve", m)
    lm = pr.val("ldw")
    for eng in ("tensor", "vector", "scalar", "gpsimd"):
        pr.wait(eng, "ldw", lm)

    for sand in sands:
        if sand == "l0a":
            emit_l0a(ctx)
        elif sand == "l0f":
            emit_l0f(ctx)
        elif sand == "l1a":
            emit_l1a(ctx)
        elif sand == "l1f":
            emit_l1f(ctx)
        pr.barrier(sand)

    for nm in DEBUG_DUMP:
        h = A[nm]
        shp = list(h.ap().shape)
        dt_ = h.ap().dtype
        tdbg = nc.dram_tensor(f"dbg_{nm}", shp, dt_, kind="ExternalOutput")
        pr.emit("gpsimd", lambda e, tdbg=tdbg, h=h: e.dma_start(
            out=tdbg.ap(), in_=h[:, :]), inc=("st", 16))
    pr.emit("gpsimd", lambda e: e.dma_start(
        out=tn["x_out"].ap().rearrange("(c p) t -> p c t", p=128),
        in_=A["X"].ap().rearrange("p (c t) -> p c t", t=T)), inc=("st", 16))
    pr.wait("gpsimd", "st", pr.val("st"))
    pr.finalize()
    return nc


def emit_ln(ctx, src, src_dt, mode):
    pr, nc, tn, A, P = ctx["pr"], ctx["nc"], ctx["tn"], ctx["A"], ctx["P"]
    ones = A["ONESF"] if src_dt == F32 else A["ONESB"]
    ssrc = A[src]
    for sname in ("dve", "zev", "hev", "cst"):
        pr.wait("tensor", sname, pr.val(sname))
        pr.wait("scalar", sname, pr.val(sname))
    sqmm = []
    rowst = []
    for qc in range(4):
        for dc in range(DC):
            col = _sl(dc * T + qc * 512, 512)
            half = _sl((dc % 2) * 512, 512)
            if len(sqmm) >= 2:
                pr.wait("scalar", "pe2", sqmm[-2])
            pr.emit("scalar", lambda e, col=col, half=half: e.activation(
                A["SQ"][:, half], ssrc[:, col], AF.Square), inc=("act", 1))
            pr.wait("tensor", "act", pr.val("act"))
            pr.emit("tensor", lambda e, dc=dc, half=half: e.matmul(
                P["B3"][0:1, :], A["ONESF"][:, :], A["SQ"][:, half],
                start=(dc == 0), stop=(dc == DC - 1)), inc=("pe2", 1))
            sqmm.append(pr.val("pe2"))
            pr.emit("tensor", lambda e, col=col, dc=dc: e.matmul(
                P["PST"][0:1, :], ones[:, :], ssrc[:, col],
                start=(dc == 0), stop=(dc == DC - 1)),
                inc=("pe", 1) if dc == DC - 1 else None)
        pm = pr.val("pe")
        pr.wait("vector", "pe", pm)
        pr.wait("vector", "pe2", pr.val("pe2"))
        if rowst:
            pr.wait("vector", "st", rowst[-1])
        pr.emit("vector", lambda e: e.tensor_scalar(
            A["R0"][:, :], P["PST"][0:1, :], INVD, None, ALU.mult))
        pr.emit("vector", lambda e: e.tensor_scalar(
            A["R1"][:, :], P["B3"][0:1, :], INVD, None, ALU.mult),
            inc=("dve", 1))
        bankfree = pr.val("dve")
        pr.wait("tensor", "dve", bankfree)
        pr.emit("vector", lambda e: e.tensor_tensor(
            A["R2"][:, :], A["R0"][:, :], A["R0"][:, :], ALU.mult))
        pr.emit("vector", lambda e: e.tensor_tensor(
            A["R1"][:, :], A["R1"][:, :], A["R2"][:, :], ALU.subtract))
        pr.emit("vector", lambda e: e.tensor_scalar(
            A["R1"][:, :], A["R1"][:, :], EPS, None, ALU.add), inc=("dve", 1))
        pr.wait("scalar", "dve", pr.val("dve"))
        pr.emit("scalar", lambda e: e.activation(
            A["R2"][:, :], A["R1"][:, :], AF.Sqrt), inc=("act", 1))
        pr.wait("vector", "act", pr.val("act"))
        pr.emit("vector", lambda e: e.reciprocal(A["R1"][:, :], A["R2"][:, :]))
        pr.emit("vector", lambda e: e.tensor_tensor(
            A["R2"][:, :], A["R0"][:, :], A["R1"][:, :], ALU.mult))
        pr.emit("vector", lambda e: e.tensor_scalar(
            A["R2"][:, :], A["R2"][:, :], -1.0, None, ALU.mult), inc=("dve", 1))
        rm = pr.val("dve")
        pr.wait("gpsimd", "dve", rm)
        pr.emit("gpsimd", lambda e, qc=qc: e.dma_start(
            out=tn["statb"].ap()[0:1, _sl(qc * 512, 512)], in_=A["R1"][:, :]),
            inc=("st", 16))
        pr.emit("gpsimd", lambda e, qc=qc: e.dma_start(
            out=tn["statb"].ap()[0:1, _sl(2048 + qc * 512, 512)], in_=A["R2"][:, :]),
            inc=("st", 16))
        rowst.append(pr.val("st"))
    stm = pr.val("st")
    pr.wait("sync", "st", stm)
    pr.emit("sync", lambda e: e.dma_start(
        out=A["ABC"][:, :], in_=tn["statb"].ap()[0:1, 0:2048].broadcast_to([128, T])),
        inc=("ldw", 16))
    pr.emit("sync", lambda e: e.dma_start(
        out=A["BBC"][:, :], in_=tn["statb"].ap()[0:1, 2048:4096].broadcast_to([128, T])),
        inc=("ldw", 16))
    pr.wait("vector", "ldw", pr.val("ldw"))
    for dc in range(DC):
        col = _sl(dc * T, T)
        pr.emit("vector", lambda e, col=col: e.tensor_tensor(
            A["TMP"][:, :], ssrc[:, col], A["ABC"][:, :], ALU.mult))
        if mode == "pre":
            pr.emit("vector", lambda e, col=col: e.tensor_tensor(
                A["LNX"][:, col], A["TMP"][:, :], A["BBC"][:, :], ALU.add),
                inc=("dve", 1))
        elif mode == "aff":
            pr.emit("vector", lambda e: e.tensor_tensor(
                A["TMP"][:, :], A["TMP"][:, :], A["BBC"][:, :], ALU.add))
            pr.emit("vector", lambda e, col=col, dc=dc: e.tensor_scalar(
                A["LNX"][:, col], A["TMP"][:, :],
                A["gbs"][:, _sl(dc, 1)], A["gbs"][:, _sl(8 + dc, 1)],
                ALU.mult, ALU.add), inc=("dve", 1))
        elif mode == "res":
            pr.emit("vector", lambda e: e.tensor_tensor(
                A["TMP"][:, :], A["TMP"][:, :], A["BBC"][:, :], ALU.add))
            pr.emit("vector", lambda e, col=col: e.tensor_tensor(
                A["X"][:, col], A["X"][:, col], A["TMP"][:, :], ALU.add),
                inc=("dve", 1))


def emit_proj(ctx, wnames, n_out, out_cb, wbuf="WB0", rhs_slab="LNX"):
    pr, nc, tn, A, P = ctx["pr"], ctx["nc"], ctx["tn"], ctx["A"], ctx["P"]
    W = A[wbuf]
    marks = []
    for wi, wname in enumerate(wnames):
        pr.wait("gpsimd", "pe", pr.val("pe"))
        pr.wait("tensor", "dve", pr.val("dve"))
        for dc in range(DC):
            src = tn[wname].ap()[_sl(dc * 128, 128), :]
            pr.emit("gpsimd", lambda e, dc=dc, src=src, n_out=n_out: e.dma_start(
                out=W[:, _sl(dc * n_out, n_out)], in_=src), inc=("ldg", 16))
        pr.wait("tensor", "ldg", pr.val("ldg"))
        for oc in range(n_out // 128):
            for qc in range(4):
                half = (oc * 4 + qc) % 4
                ps = [P["B0"][:, :], P["B1"][:, :], P["B2"][:, :],
                      P["B3"][:, :]][half]
                if len(marks) >= 4:
                    pr.wait("tensor", "dve", marks[-4])
                for dc in range(DC):
                    lhsT = W[:, _sl(dc * n_out + oc * 128, 128)]
                    rhs = A[rhs_slab][:, _sl(dc * T + qc * 512, 512)]
                    pr.emit("tensor", lambda e, ps=ps, lhsT=lhsT, rhs=rhs, dc=dc:
                            e.matmul(ps, lhsT, rhs, start=(dc == 0),
                                     stop=(dc == DC - 1)),
                            inc=("pe", 1) if dc == DC - 1 else None)
                pm = pr.val("pe")
                pr.wait("vector", "pe", pm)
                out_cb(wi, oc, qc, ps)
                pr.emit("vector", lambda e: e.engine_nop(), inc=("dve", 1))
                marks.append(pr.val("dve"))


def emit_l0a(ctx):
    pr, nc, tn, A, P = ctx["pr"], ctx["nc"], ctx["tn"], ctx["A"], ctx["P"]
    pr.emit("sync", lambda e: e.dma_start(out=A["gbs"][:, :], in_=tn["gb"].ap()),
            inc=("ldw", 16))
    pr.emit("sync", lambda e: e.dma_start(out=A["bqkvs"][:, :], in_=tn["bqkv"].ap()),
            inc=("ldw", 16))
    pr.emit("sync", lambda e: e.dma_start(out=A["MSK"][:, :], in_=tn["mask0"].ap()),
            inc=("ldw", 16))
    pr.emit("gpsimd", lambda e: e.dma_start(
        out=A["BVB"][:, :], in_=tn["bvrow"].ap()[0:1, :].broadcast_to([128, D])),
        inc=("ldg", 16))
    pr.wait("vector", "ldw", pr.val("ldw"))
    pr.wait("vector", "ldg", pr.val("ldg"))
    pr.wait("scalar", "ldw", pr.val("ldw"))
    if DEBUG_PHASE < 1:
        return

    emit_ln(ctx, "X", F32, "pre")
    pr.wait("tensor", "dve", pr.val("dve"))
    if DEBUG_PHASE < 2:
        return

    stflags = {"last": None}

    def store_fm(base_row, bias_base):
        def cb(wi, oc, qc, ps):
            if qc == 0 and stflags["last"] is not None:
                pr.wait("vector", "st", stflags["last"])
            pr.emit("vector", lambda e, ps=ps, qc=qc, oc=oc: e.tensor_scalar(
                A["STG"][:, _sl(qc * 512, 512)], ps,
                A["bqkvs"][:, _sl(bias_base + oc, 1)], None, ALU.add),
                inc=("dve", 1))
            if qc == 3:
                dm = pr.val("dve")
                pr.wait("gpsimd", "dve", dm)
                pr.emit("gpsimd", lambda e, oc=oc: e.dma_start(
                    out=tn["g1_src"].ap()[_sl(base_row + oc * 128, 128), :],
                    in_=A["STG"][:, :]), inc=("st", 16))
                stflags["last"] = pr.val("st")
        return cb

    emit_proj(ctx, ["wkT"], D, store_fm(1024, 8))
    if DEBUG_PHASE < 3:
        return
    pr.wait("gpsimd", "pe", pr.val("pe"))
    for dc in range(DC):
        pr.emit("gpsimd", lambda e, dc=dc: e.dma_start(
            out=A["WB0"][:, _sl(dc * D, D)], in_=tn["wvT"].ap()[_sl(dc * 128, 128), :]),
            inc=("ldg", 16))
    pr.wait("tensor", "ldg", pr.val("ldg"))
    pr.wait("tensor", "dve", pr.val("dve"))
    marks = []
    for tt in range(16):
        for fh in range(2):
            half = (tt * 2 + fh) % 4
            ps = [P["B0"][:, :], P["B1"][:, :], P["B2"][:, :],
                  P["B3"][:, :]][half]
            if len(marks) >= 4:
                pr.wait("tensor", "dve", marks[-4])
            for dc in range(DC):
                lhsT = A["LNX"][:, _sl(dc * T + tt * 128, 128)]
                rhs = A["WB0"][:, _sl(dc * D + fh * 512, 512)]
                pr.emit("tensor", lambda e, ps=ps, lhsT=lhsT, rhs=rhs, dc=dc: e.matmul(
                    ps, lhsT, rhs, start=(dc == 0), stop=(dc == DC - 1)),
                    inc=("pe", 1) if dc == DC - 1 else None)
            pm = pr.val("pe")
            pr.wait("vector", "pe", pm)
            if fh == 0 and stflags["last"] is not None:
                pr.wait("vector", "st", stflags["last"])
            pr.emit("vector", lambda e, ps=ps, fh=fh: e.tensor_tensor(
                A["STG"][:, _sl(fh * 512, 512)], ps, A["BVB"][:, _sl(fh * 512, 512)],
                ALU.add), inc=("dve", 1))
            marks.append(pr.val("dve"))
        dm = pr.val("dve")
        pr.wait("gpsimd", "dve", dm)
        pr.emit("gpsimd", lambda e, tt=tt: e.dma_start(
            out=tn["g2_src"].ap()[_sl(tt * 128, 128), :], in_=A["STG"][:, 0:1024]),
            inc=("st", 16))
        stflags["last"] = pr.val("st")
    pr.wait("gpsimd", "st", pr.val("st"))
    pr.emit("gpsimd", lambda e: e.collective_compute(
        "AllGather", ALU.bypass, replica_groups=GROUPS,
        ins=[tn["g2_src"].ap()], outs=[tn["g2_out"].ap()]), inc=("cc", 1))
    if DEBUG_PHASE < 4:
        return
    emit_proj(ctx, ["wqT"], D, store_fm(0, 0))
    pr.wait("gpsimd", "st", pr.val("st"))
    pr.emit("gpsimd", lambda e: e.collective_compute(
        "AllGather", ALU.bypass, replica_groups=GROUPS,
        ins=[tn["g1_src"].ap()], outs=[tn["g1_out"].ap()]), inc=("cc", 1))

    if DEBUG_PHASE < 5:
        return
    emit_attn(ctx, layer=0)
    if DEBUG_PHASE < 6:
        return

    pr.wait("gpsimd", "st", pr.val("st"))
    pr.emit("gpsimd", lambda e: e.collective_compute(
        "AllGather", ALU.bypass, replica_groups=GROUPS,
        ins=[tn["g3_src"].ap()], outs=[tn["g3_out"].ap()]), inc=("cc", 1))
    ccz = pr.val("cc")
    pr.wait("sync", "cc", ccz)
    pr.wait("sync", "dve", pr.val("dve"))
    def ldz(e):
        c0 = ctx["get_sym"](e, "tcol")
        src = tn["g3_out"].ap().rearrange("(c p) t -> p c t", p=128)
        src = src[:, :, bass.ds(c0, T)]
        return e.dma_start(out=A["Z"].ap().rearrange("p (c t) -> p c t", t=T),
                           in_=src)
    pr.emit("sync", ldz, inc=("ldw", 16))
    pr.wait("vector", "ldw", pr.val("ldw"))
    for dc in range(DC):
        col = _sl(dc * T, T)
        pr.emit("vector", lambda e, col=col: e.tensor_tensor(
            A["Z"][:, col], A["Z"][:, col], A["LNX"][:, col], ALU.add), inc=("dve", 1))
    emit_ln(ctx, "Z", BF16, "aff")
    emit_ln(ctx, "LNX", BF16, "res")


def emit_attn(ctx, layer):
    pr, nc, tn, A, P = ctx["pr"], ctx["nc"], ctx["tn"], ctx["A"], ctx["P"]
    ccmark = pr.val("cc")
    pr.wait("sync", "cc", ccmark)
    scale_col = _sl(layer, 1)

    state = {"acfree": 0, "ss_last": [0, 0], "eb_last": [0, 0], "em_last": [0, 0]}
    pr.wait("tensor", "dve", pr.val("dve"))
    for p_ in range(2):
        QS = A["QS0"]
        pr.wait("sync", "pe", pr.val("pe"))

        def ldq(e, p_=p_, QS=QS):
            hg = ctx["get_sym"](e, "hg256")
            src = tn["g1_out"].ap().rearrange("(r b) t -> b r t", r=4)
            src = src[_sl(p_ * 128, 896), :, :][bass.ds(hg, 128), :, :]
            return e.dma_start(out=QS.ap().rearrange("p (r t) -> p r t", r=4),
                               in_=src)
        pr.emit("sync", ldq, inc=("ldw", 16))
        if layer == 0:
            def ldk(e, p_=p_):
                hg = ctx["get_sym"](e, "hg256")
                src = tn["g1_out"].ap().rearrange("(r b) t -> b r t", r=4)
                src = src[_sl(1024 + p_ * 128, 896), :, :][bass.ds(hg, 128), :, :]
                return e.dma_start(out=A["KP"].ap().rearrange(
                    "p (r t) -> p r t", r=4), in_=src)
            pr.emit("sync", ldk, inc=("ldw", 16))
        else:
            for hb in range(2):
                def ldk(e, hb=hb):
                    src = tn["g1_out"].ap().rearrange(
                        "(r b) t -> b r t", r=4)[_sl(1024, 64), :, :]
                    return e.dma_start(
                        out=A["KP"].ap().rearrange(
                            "p (r t) -> p r t", r=4)[_sl(hb * 64, 64), :, :],
                        in_=src)
                pr.emit("sync", ldk, inc=("ldw", 16))
        if layer == 0:
            for hh in range(2):
                def ldv(e, p_=p_, hh=hh):
                    hg = ctx["get_sym"](e, "hg256")
                    src = tn["g2_out"].ap().rearrange(
                        "(r k p) c -> p r k c", p=128, r=4)
                    src = src[:, :, :, _sl(p_ * 128 + hh * 64, 832)]
                    src = src[:, :, :, bass.ds(hg, 64)]
                    dst = A[("VS0", "VS1")[hh]].ap().rearrange(
                        "p (r k c) -> p r k c", c=64, r=4)
                    return e.dma_start(out=dst, in_=src)
                pr.emit("sync", ldv, inc=("ldw", 16))
        else:
            def ldv(e):
                src = tn["g2_out"].ap().rearrange(
                    "(r k p) c -> p r k c", p=128, r=4)[:, :, :, 0:64]
                dst = A["VS0"].ap().rearrange(
                    "p (r k c) -> p r k c", c=64, r=4)
                return e.dma_start(out=dst, in_=src)
            pr.emit("sync", ldv, inc=("ldw", 16))
        kvm = pr.val("ldw")
        for eng in ("tensor", "scalar", "vector"):
            pr.wait(eng, "ldw", kvm)

        for qc in range(16):
            if layer == 0:
                kts = list(range(64))
                regions = ["below" if kt < 4 * qc else
                           ("diag" if kt <= 4 * qc + 3 else "above") for kt in kts]
            else:
                kts = list(range(min(64, 4 * qc + 4)))
                regions = ["full" if kt < 4 * qc else "diag" for kt in kts]
            n = len(kts)
            qk_marks = [None] * n
            exp_marks = [None] * n
            pv_marks = [None] * n
            msk_marks = [None] * n
            grp = {"acc": False, "den": False}
            last_acc = max(i for i in range(n) if regions[i] != "below")
            last_den = n - 1

            def emit_pv(j):
                kt, region = kts[j], regions[j]
                EBj = A["EB0"] if j % 2 == 0 else A["EB1"]
                pr.wait("tensor", "act", exp_marks[j])
                if region == "diag":
                    pr.wait("tensor", "dve", msk_marks[j])
                pr.wait("tensor", "dve", state["acfree"])
                num_src = (A["EM"].ap()[:, _sl((j % 2) * 1024, 1024)]
                           if region == "diag" else EBj)
                den_src = EBj if layer == 0 else \
                    (A["EM"].ap()[:, _sl((j % 2) * 1024, 1024)]
                     if region == "diag" else EBj)
                if region != "below":
                    st0 = not grp["acc"]
                    grp["acc"] = True
                    stop = (j == last_acc)
                    pr.emit("tensor", lambda e, kt=kt, src=num_src, st0=st0, stop=stop:
                            e.matmul(P["ACC"][0:64, :],
                                     A["VS0"][:, _sl(kt * 64, 64)],
                                     src[:, 0:512], start=st0, stop=stop,
                                     tile_position=(0, 0)))
                    VS_o = A["VS0"] if layer == 1 else A["VS1"]
                    pr.emit("tensor", lambda e, kt=kt, src=num_src, st0=st0, stop=stop,
                            VS_o=VS_o:
                            e.matmul(P["ACC"][64:128, :],
                                     VS_o[:, _sl(kt * 64, 64)],
                                     src[:, 512:1024], start=st0, stop=stop,
                                     tile_position=(0, 64)))
                st0 = not grp["den"]
                grp["den"] = True
                stop = (j == last_den)
                pr.emit("tensor", lambda e, src=den_src, st0=st0, stop=stop:
                        e.matmul(P["DEN"][0:1, :], A["ONESB"][:, :],
                                 src[:, 0:512], start=st0, stop=stop,
                                 tile_position=(0, 0)))
                pr.emit("tensor", lambda e, src=den_src, st0=st0, stop=stop:
                        e.matmul(P["DEN"][64:65, :], A["ONESB"][:, :],
                                 src[:, 512:1024], start=st0, stop=stop,
                                 tile_position=(0, 64)), inc=("pe", 1))
                pv_marks[j] = pr.val("pe")
                state["eb_last"][j % 2] = pv_marks[j]
                if regions[j] == "diag":
                    state["em_last"][j % 2] = pv_marks[j]

            for i, kt in enumerate(kts):
                SSa = P["B0"] if i % 2 == 0 else P["B2"]
                SSb = P["B1"] if i % 2 == 0 else P["B3"]
                EB = A["EB0"] if i % 2 == 0 else A["EB1"]
                pr.wait("tensor", "act", state["ss_last"][i % 2])
                pr.emit("tensor", lambda e, SSa=SSa, kt=kt, qc=qc, QS=QS: e.matmul(
                    SSa[:, :], A["KP"][0:64, _sl(kt * 128, 128)],
                    QS[0:64, _sl(qc * 512, 512)], start=True, stop=True,
                    tile_position=(0, 0)))
                pr.emit("tensor", lambda e, SSb=SSb, kt=kt, qc=qc, QS=QS: e.matmul(
                    SSb[:, :], A["KP"][64:128, _sl(kt * 128, 128)],
                    QS[64:128, _sl(qc * 512, 512)], start=True, stop=True,
                    tile_position=(64, 0)), inc=("pe", 1))
                qk_marks[i] = pr.val("pe")
                pr.wait("scalar", "pe", qk_marks[i])
                pr.wait("scalar", "pe", state["eb_last"][i % 2])
                pr.emit("scalar", lambda e, SSa=SSa, EB=EB: e.activation(
                    EB[:, 0:512], SSa[:, :], AF.Exp, scale=A["SCL"][:, scale_col]))
                pr.emit("scalar", lambda e, SSb=SSb, EB=EB: e.activation(
                    EB[:, 512:1024], SSb[:, :], AF.Exp, scale=A["SCL"][:, scale_col]),
                    inc=("act", 1))
                exp_marks[i] = pr.val("act")
                state["ss_last"][i % 2] = exp_marks[i]
                if regions[i] == "diag":
                    o = kt - 4 * qc
                    eh = _sl((i % 2) * 1024, 1024)
                    pr.wait("vector", "act", exp_marks[i])
                    pr.wait("vector", "pe", state["em_last"][i % 2])
                    pr.emit("vector", lambda e, EB=EB, o=o, eh=eh: e.tensor_tensor(
                        A["EM"][:, eh][:, 0:512], EB[:, 0:512],
                        A["MSK"][:, _sl(o * 512, 512)], ALU.mult))
                    pr.emit("vector", lambda e, EB=EB, o=o, eh=eh: e.tensor_tensor(
                        A["EM"][:, eh][:, 512:1024], EB[:, 512:1024],
                        A["MSK"][:, _sl(o * 512, 512)], ALU.mult), inc=("dve", 1))
                    msk_marks[i] = pr.val("dve")
                    state["em_last"][i % 2] = 0
                if i >= 1:
                    emit_pv(i - 1)
            emit_pv(n - 1)
            pr.wait("vector", "pe", pv_marks[n - 1])
            pr.emit("vector", lambda e: e.tensor_copy(A["PSTG"][:, 0:512],
                                                      P["ACC"][:, :]))
            pr.emit("vector", lambda e: e.reciprocal(A["RRX"][0:1, :],
                                                     P["DEN"][0:1, :]))
            pr.emit("vector", lambda e: e.reciprocal(A["RRX"][64:65, :],
                                                     P["DEN"][64:65, :]),
                    inc=("dve", 1))
            state["acfree"] = pr.val("dve")
            pr.wait("gpsimd", "dve", state["acfree"])
            pr.wait("gpsimd", "ldw", pr.val("ldw"))
            pr.emit("gpsimd", lambda e: e.dma_start(
                out=tn["rowb"].ap()[0:1, 0:512], in_=A["RRX"][0:1, :]),
                inc=("st", 16))
            pr.emit("gpsimd", lambda e: e.dma_start(
                out=tn["rowb"].ap()[0:1, 512:1024], in_=A["RRX"][64:65, :]),
                inc=("st", 16))
            stm = pr.val("st")
            pr.wait("sync", "st", stm)
            pr.emit("sync", lambda e: e.dma_start(
                out=A["RB"][:, :], in_=tn["rowb"].ap()[0:1, :].broadcast_to([128, 1024])),
                inc=("ldw", 16))
            rbm = pr.val("ldw")
            pr.wait("vector", "ldw", rbm)
            pr.emit("vector", lambda e, p_=p_, qc=qc: e.tensor_tensor(
                A["Z"][0:64, _sl(p_ * S + qc * 512, 512)], A["PSTG"][0:64, 0:512],
                A["RB"][0:64, 0:512], ALU.mult))
            pr.emit("vector", lambda e, p_=p_, qc=qc: e.tensor_tensor(
                A["Z"][64:128, _sl(p_ * S + qc * 512, 512)], A["PSTG"][64:128, 0:512],
                A["RB"][64:128, 512:1024], ALU.mult), inc=("dve", 1))
        zm = pr.val("dve")
        pr.wait("gpsimd", "dve", zm)
        pr.emit("gpsimd", lambda e, p_=p_: e.dma_start(
            out=tn["g3_src"].ap()[_sl(p_ * 128, 128), :],
            in_=A["Z"][:, _sl(p_ * S, S)]), inc=("st", 16))


def emit_l0f(ctx):
    pr, nc, tn, A, P = ctx["pr"], ctx["nc"], ctx["tn"], ctx["A"], ctx["P"]
    emit_ln(ctx, "X", F32, "pre")
    pr.wait("tensor", "dve", pr.val("dve"))

    pr.emit("gpsimd", lambda e: e.dma_start(
        out=A["WGS"].ap().rearrange("p (c e) -> p c e", e=E8),
        in_=tn["wgt"].ap().rearrange("(c p) e -> p c e", p=128)),
        inc=("ldg", 16))
    pr.emit("gpsimd", lambda e: e.dma_start(out=A["B2S"][:, :], in_=tn["b2"].ap()),
            inc=("ldg", 16))
    pr.emit("sync", lambda e: e.dma_start(out=A["bgs"][:, :], in_=tn["bg"].ap()),
            inc=("ldw", 16))
    pr.emit("sync", lambda e: e.dma_start(out=A["IDS"][:, :], in_=tn["ident"].ap()),
            inc=("ldw", 16))
    pr.wait("tensor", "ldg", pr.val("ldg"))
    pr.wait("tensor", "ldw", pr.val("ldw"))
    pr.wait("vector", "ldw", pr.val("ldw"))

    for qc in range(4):
        pr.wait("tensor", "dve", pr.val("dve"))
        for dc in range(DC):
            pr.emit("tensor", lambda e, dc=dc, qc=qc: e.matmul(
                P["PST"][:, :], A["WGS"][:, _sl(dc * 8, 8)],
                A["LNX"][:, _sl(dc * T + qc * 512, 512)],
                start=(dc == 0), stop=(dc == DC - 1)),
                inc=("pe", 1) if dc == DC - 1 else None)
        pr.wait("vector", "pe", pr.val("pe"))
        pr.emit("vector", lambda e, qc=qc: e.tensor_scalar(
            A["GS"][:, _sl(qc * 512, 512)], P["PST"][0:8, :], A["bgs"][:, 0:1],
            None, ALU.add), inc=("dve", 1))
    pr.wait("tensor", "dve", pr.val("dve"))
    for tt in range(16):
        pr.emit("tensor", lambda e, tt=tt: e.matmul(
            P["PTR"][:, _sl(tt * 8, 8)], A["GS"][:, _sl(tt * 128, 128)],
            A["IDS"][0:8, 0:8], start=True, stop=True),
            inc=("pe", 1) if tt == 15 else None)
    pr.wait("vector", "pe", pr.val("pe"))

    def vstep(fn):
        pr.emit("vector", fn, inc=("dve", 1))
        pr.wait("vector", "dve", pr.val("dve"))

    def seg(t, w=8):
        return t.ap().rearrange("p (s e) -> p s e", e=w)

    def segb(t, c0):
        v = t.ap()[:, _sl(c0, 16)].rearrange("p (s o) -> p s o", o=1)
        return v.broadcast_to([128, 16, 8])

    vstep(lambda e: e.tensor_copy(A["GTA"][:, :], P["PTR"][:, :]))
    vstep(lambda e: e.reduce_max(A["REDA"][:, 0:16], seg(A["GTA"]), AX.X))
    vstep(lambda e: e.tensor_tensor(seg(A["GEB"]), seg(A["GTA"]),
                                    segb(A["REDA"], 0), ALU.subtract))
    am0 = pr.val("dve")
    pr.wait("scalar", "dve", am0)
    pr.emit("scalar", lambda e: e.activation(A["GEC"][:, :], A["GEB"][:, :], AF.Exp),
            inc=("act", 1))
    pr.wait("vector", "act", pr.val("act"))
    vstep(lambda e: e.reduce_sum(A["REDA"][:, 16:32], seg(A["GEC"]), AX.X))
    vstep(lambda e: e.reciprocal(A["REDA"][:, 32:48], A["REDA"][:, 16:32]))
    vstep(lambda e: e.tensor_tensor(seg(A["GEC"]), seg(A["GEC"]),
                                    segb(A["REDA"], 32), ALU.mult))
    vstep(lambda e: e.reduce_max(A["REDA"][:, 0:16], seg(A["GEC"]), AX.X))
    vstep(lambda e: e.tensor_tensor(seg(A["GEB"]), seg(A["GEC"]),
                                    segb(A["REDA"], 0), ALU.is_equal))
    vstep(lambda e: e.scalar_tensor_tensor(A["GEB"][:, :], A["GEB"][:, :], -4.0,
                                           A["GEC"][:, :], ALU.mult, ALU.add))
    vstep(lambda e: e.reduce_max(A["REDA"][:, 16:32], seg(A["GEB"]), AX.X))
    vstep(lambda e: e.tensor_tensor(seg(A["GEB"]), seg(A["GEC"]),
                                    segb(A["REDA"], 16), ALU.is_ge))
    vstep(lambda e: e.tensor_tensor(A["GTA"][:, :], A["GEC"][:, :], A["GEB"][:, :],
                                    ALU.mult))
    for tt in range(16):
        pr.wait("tensor", "dve", pr.val("dve"))
        pr.emit("tensor", lambda e, tt=tt: e.matmul(
            P["PTR"][0:8, 0:128], A["GTA"][:, _sl(tt * 8, 8)], A["IDS"][:, :],
            start=True, stop=True), inc=("pe", 1))
        pr.wait("vector", "pe", pr.val("pe"))
        pr.emit("vector", lambda e, tt=tt: e.tensor_copy(
            A["W8B"][:, _sl(tt * 128, 128)], P["PTR"][0:8, 0:128]), inc=("dve", 1))
    wm = pr.val("dve")
    pr.wait("gpsimd", "dve", wm)
    pr.emit("gpsimd", lambda e: e.dma_start(out=tn["w8d"].ap(), in_=A["W8B"][:, :]),
            inc=("st", 16))
    w8st = pr.val("st")
    pr.wait("tensor", "dve", wm)

    marks = []
    for dc in range(DC):
        for qc in range(4):
            half = (dc * 4 + qc) % 4
            ps = [P["B0"][:, :], P["B1"][:, :], P["B2"][:, :],
                  P["B3"][:, :]][half]
            if len(marks) >= 4:
                pr.wait("tensor", "dve", marks[-4])
            pr.emit("tensor", lambda e, ps=ps, dc=dc, qc=qc: e.matmul(
                ps, A["B2S"][:, _sl(dc * 128, 128)], A["W8B"][:, _sl(qc * 512, 512)],
                start=True, stop=True), inc=("pe", 1))
            pr.wait("vector", "pe", pr.val("pe"))
            pr.emit("vector", lambda e, ps=ps, dc=dc, qc=qc: e.tensor_copy(
                A["Z"][:, _sl(dc * T + qc * 512, 512)], ps), inc=("dve", 1))
            marks.append(pr.val("dve"))

    w1_marks = [None, None]
    w2_marks = [None, None]
    wbc_marks = [None, None]
    b1_mark = None
    for ex in range(E8):
        pr.wait("sync", "dve", pr.val("dve"))
        pr.wait("sync", "act", pr.val("act"))
        pr.emit("sync", lambda e, ex=ex: e.dma_start(
            out=A["b1s"][:, :], in_=tn["b1t"].ap()[ex]), inc=("ldw", 16))
        b1m = pr.val("ldw")
        pr.wait("vector", "ldw", b1m)
        pr.wait("scalar", "ldw", b1m)
        for tc in range(4):
            tb = tc % 2
            if wbc_marks[tb] is not None:
                pr.wait("gpsimd", "dve", wbc_marks[tb])
            pr.wait("gpsimd", "st", w8st)
            pr.emit("gpsimd", lambda e, ex=ex, tc=tc, tb=tb: e.dma_start(
                out=A["WBC"][:, _sl(tb * 512, 512)],
                in_=tn["w8d"].ap()[_sl(ex, 1), _sl(tc * 512, 512)].broadcast_to(
                    [128, 512])), inc=("ldg", 16))
            wbcm = pr.val("ldg")
            pr.wait("tensor", "dve", pr.val("dve"))
            pr.wait("tensor", "zev", pr.val("zev"))
            hmarks = []
            for fc in range(32):
                wb = fc % 2
                if w1_marks[wb] is not None:
                    pr.wait("gpsimd", "pe", w1_marks[wb])
                pr.emit("gpsimd", lambda e, ex=ex, fc=fc, wb=wb: e.dma_start(
                    out=A["W1B"][:, _sl(wb * 1024, 1024)].rearrange(
                        "p (c f) -> p c f", f=128),
                    in_=tn["w1t"].ap()[ex, fc].rearrange("(c p) f -> p c f", p=128)),
                    inc=("ldg", 16))
                ldm = pr.val("ldg")
                half = fc % 4
                ps = [P["B0"][:, :], P["B1"][:, :], P["B2"][:, :],
                      P["B3"][:, :]][half]
                pr.wait("tensor", "ldg", ldm)
                if len(hmarks) >= 4:
                    pr.wait("tensor", "hev", hmarks[-4])
                for dc in range(DC):
                    pr.emit("tensor", lambda e, ps=ps, wb=wb, dc=dc, tc=tc: e.matmul(
                        ps, A["W1B"][:, _sl(wb * 1024 + dc * 128, 128)],
                        A["LNX"][:, _sl(dc * T + tc * 512, 512)],
                        start=(dc == 0), stop=(dc == DC - 1)),
                        inc=("pe", 1) if dc == DC - 1 else None)
                w1_marks[wb] = pr.val("pe")
                pm = pr.val("pe")
                if fc % 2 == 0:
                    pr.wait("scalar", "pe", pm)
                    pr.emit("scalar", lambda e, ps=ps, fc=fc: e.activation(
                        A["HSL"][:, _sl(fc * 512, 512)], ps, AF.Relu,
                        bias=A["b1s"][:, _sl(fc, 1)]), inc=("hev", 1))
                else:
                    pr.wait("vector", "pe", pm)
                    pr.emit("vector", lambda e, ps=ps, fc=fc: e.tensor_scalar(
                        A["HSL"][:, _sl(fc * 512, 512)], ps, A["b1s"][:, _sl(fc, 1)],
                        0.0, ALU.add, ALU.max), inc=("hev", 1))
                hmarks.append(pr.val("hev"))
            hall = pr.val("hev")
            pr.wait("tensor", "hev", hall)
            for dc in range(DC):
                wb2 = dc % 2
                if w2_marks[wb2] is not None:
                    pr.wait("gpsimd", "pe", w2_marks[wb2])
                pr.emit("gpsimd", lambda e, ex=ex, dc=dc, wb2=wb2: e.dma_start(
                    out=A["W2B"][:, _sl(wb2 * F1, F1)].rearrange(
                        "p (c f) -> p c f", f=128),
                    in_=tn["w2t"].ap()[ex, dc].rearrange("(c p) f -> p c f", p=128)),
                    inc=("ldg", 16))
                ldm = pr.val("ldg")
                half = dc % 4
                ps = [P["B0"][:, :], P["B1"][:, :], P["B2"][:, :],
                      P["B3"][:, :]][half]
                pr.wait("tensor", "ldg", ldm)
                if len(hmarks) >= 36:
                    pass
                pr.wait("tensor", "zev", max(0, pr.val("zev") - 3))
                for fc in range(32):
                    pr.emit("tensor", lambda e, ps=ps, wb2=wb2, fc=fc: e.matmul(
                        ps, A["W2B"][:, _sl(wb2 * F1 + fc * 128, 128)],
                        A["HSL"][:, _sl(fc * 512, 512)],
                        start=(fc == 0), stop=(fc == 31)),
                        inc=("pe", 1) if fc == 31 else None)
                w2_marks[wb2] = pr.val("pe")
                pm = pr.val("pe")
                pr.wait("vector", "pe", pm)
                pr.wait("vector", "ldg", wbcm)
                pr.emit("vector", lambda e, ps=ps, tb=tb: e.tensor_tensor(
                    A["TMI"][:, :], ps, A["WBC"][:, _sl(tb * 512, 512)], ALU.mult))
                pr.emit("vector", lambda e, dc=dc, tc=tc: e.tensor_tensor(
                    A["Z"][:, _sl(dc * T + tc * 512, 512)],
                    A["Z"][:, _sl(dc * T + tc * 512, 512)], A["TMI"][:, :], ALU.add),
                    inc=("zev", 1))
            wbc_marks[tb] = pr.val("dve")

    emit_ln(ctx, "Z", BF16, "res")


def emit_l1a(ctx):
    pr, nc, tn, A, P = ctx["pr"], ctx["nc"], ctx["tn"], ctx["A"], ctx["P"]
    pr.emit("sync", lambda e: e.dma_start(out=A["MSK"][:, :], in_=tn["mask1"].ap()),
            inc=("ldw", 16))

    emit_ln(ctx, "X", F32, "pre")
    pr.wait("tensor", "dve", pr.val("dve"))
    if DEBUG_PHASE < 2:
        return
    pr.wait("sync", "dve", pr.val("dve"))
    pr.emit("sync", lambda e: e.dma_start(out=A["COS"][:, :], in_=tn["cos4"].ap()),
            inc=("ldw", 16))
    pr.emit("sync", lambda e: e.dma_start(out=A["SIN"][:, :], in_=tn["sin4"].ap()),
            inc=("ldw", 16))
    pr.wait("vector", "ldw", pr.val("ldw"))

    stl = {"last": None}

    pr.wait("gpsimd", "pe", pr.val("pe"))
    for wi, wname in enumerate(("wk1a", "wk1b")):
        pr.emit("gpsimd", lambda e, wi=wi, wname=wname: e.dma_start(
            out=A["WB0"][:, _sl(wi * 512, 512)].rearrange("p (c f) -> p c f", f=64),
            in_=tn[wname].ap().rearrange("(c p) f -> p c f", p=128)),
            inc=("ldg", 16))
    pr.wait("tensor", "ldg", pr.val("ldg"))
    pr.wait("tensor", "dve", pr.val("dve"))
    for wi in range(2):
        DST = A["KTA"] if wi == 0 else A["KTB"]
        for qc in range(4):
            ps = P["B0"][:, :] if qc % 2 == 0 else P["B2"][:, :]
            pr.wait("tensor", "dve", max(0, pr.val("dve") - 1))
            for dc in range(DC):
                pr.emit("tensor", lambda e, ps=ps, wi=wi, dc=dc, qc=qc: e.matmul(
                    ps[0:64, :], A["WB0"][:, _sl(wi * 512 + dc * 64, 64)],
                    A["LNX"][:, _sl(dc * T + qc * 512, 512)],
                    start=(dc == 0), stop=(dc == DC - 1)),
                    inc=("pe", 1) if dc == DC - 1 else None)
            pr.wait("vector", "pe", pr.val("pe"))
            pr.emit("vector", lambda e, ps=ps, DST=DST, qc=qc: e.tensor_copy(
                DST[:, _sl(qc * 512, 512)], ps[0:64, :]), inc=("dve", 1))
    pr.emit("vector", lambda e: e.tensor_tensor(
        A["WRKA"][0:64, :], A["KTA"][:, :], A["COS"][0:64, :], ALU.mult))
    pr.emit("vector", lambda e: e.tensor_tensor(
        A["WRKB"][0:64, :], A["KTB"][:, :], A["SIN"][0:64, :], ALU.mult))
    pr.emit("vector", lambda e: e.tensor_tensor(
        A["STG"][0:64, :], A["WRKA"][0:64, :], A["WRKB"][0:64, :], ALU.add),
        inc=("dve", 1))
    km = pr.val("dve")
    pr.wait("gpsimd", "dve", km)
    pr.emit("gpsimd", lambda e: e.dma_start(
        out=tn["g1_src"].ap()[_sl(1024, 64), :], in_=A["STG"][0:64, :]),
        inc=("st", 16))
    stl["last"] = pr.val("st")
    if DEBUG_PHASE < 3:
        return

    pr.wait("gpsimd", "pe", pr.val("pe"))
    pr.wait("gpsimd", "dve", pr.val("dve"))
    pr.emit("gpsimd", lambda e: e.dma_start(
        out=A["WVs"].ap().rearrange("p (c f) -> p c f", f=64),
        in_=tn["wv1"].ap().rearrange("(c p) f -> p c f", p=128)),
        inc=("ldg", 16))
    pr.wait("tensor", "ldg", pr.val("ldg"))
    pr.wait("tensor", "dve", pr.val("dve"))
    for tt in range(16):
        ps = P["B0"][:, :] if tt % 2 == 0 else P["B2"][:, :]
        pr.wait("tensor", "dve", max(0, pr.val("dve") - 1))
        for dc in range(DC):
            pr.emit("tensor", lambda e, ps=ps, dc=dc, tt=tt: e.matmul(
                ps[:, 0:64], A["LNX"][:, _sl(dc * T + tt * 128, 128)],
                A["WVs"][:, _sl(dc * 64, 64)], start=(dc == 0), stop=(dc == DC - 1)),
                inc=("pe", 1) if dc == DC - 1 else None)
        pr.wait("vector", "pe", pr.val("pe"))
        if stl["last"] is not None:
            pr.wait("vector", "st", stl["last"])
        pr.emit("vector", lambda e, ps=ps: e.tensor_copy(A["STG"][:, 0:64], ps[:, 0:64]),
                inc=("dve", 1))
        vm = pr.val("dve")
        pr.wait("gpsimd", "dve", vm)
        pr.emit("gpsimd", lambda e, tt=tt: e.dma_start(
            out=tn["g2_src"].ap()[_sl(tt * 128, 128), 0:64], in_=A["STG"][:, 0:64]),
            inc=("st", 16))
        stl["last"] = pr.val("st")
    pr.wait("gpsimd", "st", pr.val("st"))
    pr.emit("gpsimd", lambda e: e.collective_compute(
        "AllGather", ALU.bypass, replica_groups=GROUPS,
        ins=[tn["g2_src"].ap()], outs=[tn["g2_out"].ap()]), inc=("cc", 1))

    if DEBUG_PHASE < 4:
        return
    pr.wait("gpsimd", "pe", pr.val("pe"))
    for dc in range(DC):
        pr.emit("gpsimd", lambda e, dc=dc: e.dma_start(
            out=A["WB0"][:, _sl(dc * D, D)],
            in_=tn["wqT1a"].ap()[_sl(dc * 128, 128), :]), inc=("ldg", 16))
    qa_m = pr.val("ldg")
    pr.wait("tensor", "ldg", qa_m)
    pr.wait("tensor", "dve", pr.val("dve"))
    marks = []
    for oc in range(8):
        for qc in range(4):
            half = (oc * 4 + qc) % 4
            ps = [P["B0"][:, :], P["B1"][:, :], P["B2"][:, :],
                  P["B3"][:, :]][half]
            if len(marks) >= 4:
                pr.wait("tensor", "dve", marks[-4])
            for dc in range(DC):
                pr.emit("tensor", lambda e, ps=ps, dc=dc, oc=oc, qc=qc: e.matmul(
                    ps, A["WB0"][:, _sl(dc * D + oc * 128, 128)],
                    A["LNX"][:, _sl(dc * T + qc * 512, 512)],
                    start=(dc == 0), stop=(dc == DC - 1)),
                    inc=("pe", 1) if dc == DC - 1 else None)
            pr.wait("vector", "pe", pr.val("pe"))
            pr.emit("vector", lambda e, ps=ps, oc=oc, qc=qc: e.tensor_copy(
                A["Z"][:, _sl(oc * T + qc * 512, 512)], ps), inc=("dve", 1))
            marks.append(pr.val("dve"))
    qa_all = pr.val("dve")
    pr.wait("gpsimd", "pe", pr.val("pe"))
    for dc in range(DC):
        pr.emit("gpsimd", lambda e, dc=dc: e.dma_start(
            out=A["WB0"][:, _sl(dc * D, D)],
            in_=tn["wqT1b"].ap()[_sl(dc * 128, 128), :]), inc=("ldg", 16))
    pr.wait("tensor", "ldg", pr.val("ldg"))
    pr.wait("tensor", "dve", pr.val("dve"))
    for oc in range(8):
        for qc in range(4):
            half = (oc * 4 + qc) % 4
            ps = [P["B0"][:, :], P["B1"][:, :], P["B2"][:, :],
                  P["B3"][:, :]][half]
            if len(marks) >= 4:
                pr.wait("tensor", "dve", marks[-4])
            for dc in range(DC):
                pr.emit("tensor", lambda e, ps=ps, dc=dc, oc=oc, qc=qc: e.matmul(
                    ps, A["WB0"][:, _sl(dc * D + oc * 128, 128)],
                    A["LNX"][:, _sl(dc * T + qc * 512, 512)],
                    start=(dc == 0), stop=(dc == DC - 1)),
                    inc=("pe", 1) if dc == DC - 1 else None)
            pr.wait("vector", "pe", pr.val("pe"))
            if qc == 0 and stl["last"] is not None:
                pr.wait("vector", "st", stl["last"])
            pr.emit("vector", lambda e, ps=ps, qc=qc: e.tensor_copy(
                A["STG"][:, _sl(qc * 512, 512)], ps), inc=("dve", 1))
            marks.append(pr.val("dve"))
        pr.emit("vector", lambda e, oc=oc: e.tensor_tensor(
            A["WRKA"][:, :], A["Z"][:, _sl(oc * T, T)], A["COS"][:, :], ALU.mult))
        pr.emit("vector", lambda e: e.tensor_tensor(
            A["WRKB"][:, :], A["STG"][:, :], A["SIN"][:, :], ALU.mult))
        if stl["last"] is not None:
            pr.wait("vector", "st", stl["last"])
        pr.emit("vector", lambda e: e.tensor_tensor(
            A["STG"][:, :], A["WRKA"][:, :], A["WRKB"][:, :], ALU.add), inc=("dve", 1))
        qm = pr.val("dve")
        pr.wait("gpsimd", "dve", qm)
        pr.emit("gpsimd", lambda e, oc=oc: e.dma_start(
            out=tn["g1_src"].ap()[_sl(oc * 128, 128), :], in_=A["STG"][:, :]),
            inc=("st", 16))
        stl["last"] = pr.val("st")
    pr.wait("gpsimd", "st", pr.val("st"))
    pr.emit("gpsimd", lambda e: e.collective_compute(
        "AllGather", ALU.bypass, replica_groups=GROUPS,
        ins=[tn["g1_src"].ap()], outs=[tn["g1_out"].ap()]), inc=("cc", 1))

    if DEBUG_PHASE < 5:
        return
    emit_attn(ctx, layer=1)
    if DEBUG_PHASE < 6:
        return

    pr.wait("gpsimd", "st", pr.val("st"))
    pr.emit("gpsimd", lambda e: e.collective_compute(
        "AllGather", ALU.bypass, replica_groups=GROUPS,
        ins=[tn["g3_src"].ap()], outs=[tn["g3_out"].ap()]), inc=("cc", 1))
    ccz = pr.val("cc")
    pr.wait("sync", "cc", ccz)
    pr.wait("sync", "dve", pr.val("dve"))
    def ldz1(e):
        c0 = ctx["get_sym"](e, "tcol")
        src = tn["g3_out"].ap().rearrange("(c p) t -> p c t", p=128)
        src = src[:, :, bass.ds(c0, T)]
        return e.dma_start(out=A["Z"].ap().rearrange("p (c t) -> p c t", t=T),
                           in_=src)
    pr.emit("sync", ldz1, inc=("ldw", 16))
    zm = pr.val("ldw")
    pr.wait("tensor", "ldw", zm)
    pr.wait("vector", "ldw", zm)
    def ocb(wi, oc, qc, ps):
        pr.emit("vector", lambda e, ps=ps, oc=oc, qc=qc: e.tensor_copy(
            A["LNX"][:, _sl(oc * T + qc * 512, 512)], ps), inc=("dve", 1))
    emit_proj(ctx, ["woT"], D, ocb, rhs_slab="Z")
    emit_ln(ctx, "LNX", BF16, "res")


def emit_l1f(ctx):
    pr, nc, tn, A, P = ctx["pr"], ctx["nc"], ctx["tn"], ctx["A"], ctx["P"]
    emit_ln(ctx, "X", F32, "pre")
    pr.wait("tensor", "dve", pr.val("dve"))

    cast_marks = {}
    def cast_load(idx, src_ap, ncols, dst, dmark):
        if dmark is not None:
            pr.wait("sync", "pe", dmark)
        pr.wait("sync", "cst", cast_marks.get("wi", 0))
        pr.emit("sync", lambda e, src_ap=src_ap, ncols=ncols: e.dma_start(
            out=A["WI"][:, 0:ncols].rearrange("p (c f) -> p c f", f=128),
            in_=src_ap), inc=("ldw", 16))
        lm = pr.val("ldw")
        pr.wait("vector", "ldw", lm)
        pr.emit("vector", lambda e, ncols=ncols, dst=dst: e.tensor_scalar(
            dst[:, 0:ncols], A["WI"][:, 0:ncols], Q127, None, ALU.mult),
            inc=("cst", 1))
        cast_marks["wi"] = pr.val("cst")
        return pr.val("cst")

    wf_mark = [None, None]
    for tc in range(2):
        for hc in range(24):
            m0 = cast_load(0, tn["wgT1"].ap()[hc].rearrange("(c p) f -> p c f", p=128),
                           1024, A["WF0"], wf_mark[0])
            m1 = cast_load(1, tn["wuT1"].ap()[hc].rearrange("(c p) f -> p c f", p=128),
                           1024, A["WF1"], wf_mark[1])
            pr.wait("tensor", "cst", m1)
            for tq in range(2):
                psg = [P["B0"], P["B1"]][tq][:, :]
                psu = [P["B2"], P["B3"]][tq][:, :]
                pr.wait("tensor", "hev", max(0, pr.val("hev") - 1))
                for dc in range(DC):
                    rhs = A["LNX"][:, _sl(dc * T + tc * 1024 + tq * 512, 512)]
                    pr.emit("tensor", lambda e, psg=psg, dc=dc, rhs=rhs: e.matmul(
                        psg, A["WF0"][:, _sl(dc * 128, 128)], rhs,
                        start=(dc == 0), stop=(dc == DC - 1)))
                for dc in range(DC):
                    rhs = A["LNX"][:, _sl(dc * T + tc * 1024 + tq * 512, 512)]
                    pr.emit("tensor", lambda e, psu=psu, dc=dc, rhs=rhs: e.matmul(
                        psu, A["WF1"][:, _sl(dc * 128, 128)], rhs,
                        start=(dc == 0), stop=(dc == DC - 1)),
                        inc=("pe", 1) if dc == DC - 1 else None)
                pm = pr.val("pe")
                pr.wait("scalar", "pe", pm)
                pr.emit("scalar", lambda e, psg=psg: e.activation(
                    A["SIL"][:, :], psg, AF.Silu), inc=("act", 1))
                pr.wait("vector", "act", pr.val("act"))
                pr.emit("vector", lambda e, psu=psu, hc=hc, tq=tq: e.tensor_tensor(
                    A["HSL1"][:, _sl(hc * 1024 + tq * 512, 512)], A["SIL"][:, :],
                    psu, ALU.mult), inc=("hev", 1))
            wf_mark[0] = pr.val("pe")
            wf_mark[1] = pr.val("pe")
        hall = pr.val("hev")
        pr.wait("tensor", "hev", hall)
        wd_mark = None
        for dc in range(DC):
            md = cast_load(2, tn["wdT1"].ap()[dc].rearrange("(c p) f -> p c f", p=128),
                           3072, A["WF0"], wd_mark)
            pr.wait("tensor", "cst", md)
            for tq in range(2):
                ps = [P["B0"], P["B1"]][tq][:, :]
                pr.wait("tensor", "zev", max(0, pr.val("zev") - 1))
                for hc in range(24):
                    pr.emit("tensor", lambda e, ps=ps, hc=hc, tq=tq: e.matmul(
                        ps, A["WF0"][:, _sl(hc * 128, 128)],
                        A["HSL1"][:, _sl(hc * 1024 + tq * 512, 512)],
                        start=(hc == 0), stop=(hc == 23)),
                        inc=("pe", 1) if hc == 23 else None)
                pm = pr.val("pe")
                pr.wait("vector", "pe", pm)
                pr.emit("vector", lambda e, ps=ps, dc=dc, tc=tc, tq=tq: e.tensor_copy(
                    A["Z"][:, _sl(dc * T + tc * 1024 + tq * 512, 512)], ps),
                    inc=("zev", 1))
            wd_mark = pr.val("pe")
        wf_mark[0] = pr.val("pe")

    emit_ln(ctx, "Z", BF16, "res")


def _prep(inputs):
    pc = {}
    x = np.asarray(inputs["L0_attn_embedded"], dtype=np.float32)
    pc["x_in"] = [np.ascontiguousarray(x[c // 4, (c % 4) * T:(c % 4 + 1) * T, :].T)
                  for c in range(NCORE)]

    def rep(a):
        a = np.ascontiguousarray(a)
        return [a] * NCORE

    pc["wqT"] = rep(np.asarray(inputs["L0_attn_Wq"], np.float32).T)
    pc["wkT"] = rep(np.asarray(inputs["L0_attn_Wk"], np.float32).T)
    pc["wvT"] = rep(np.asarray(inputs["L0_attn_Wv"], np.float32).T)
    bq = np.asarray(inputs["L0_attn_bq"], np.float32).reshape(8, 128).T
    bk = np.asarray(inputs["L0_attn_bk"], np.float32).reshape(8, 128).T
    bv = np.asarray(inputs["L0_attn_bv"], np.float32).reshape(8, 128).T
    pc["bqkv"] = rep(np.concatenate([bq, bk, bv], axis=1))
    ga = np.asarray(inputs["L0_attn_gamma"], np.float32).reshape(8, 128).T
    be = np.asarray(inputs["L0_attn_beta"], np.float32).reshape(8, 128).T
    pc["gb"] = rep(np.concatenate([ga, be], axis=1))
    pc["bvrow"] = rep(np.asarray(inputs["L0_attn_bv"], np.float32).reshape(1, D))

    pc["wgt"] = rep(np.asarray(inputs["L0_ffn_Wg"], np.float32))
    pc["bg"] = rep(np.asarray(inputs["L0_ffn_bg"], np.float32).reshape(8, 1))
    w1 = np.asarray(inputs["L0_ffn_W1"], np.float32)
    pc["w1t"] = rep(np.transpose(w1.reshape(E8, D, 32, 128), (0, 2, 1, 3)))
    b1 = np.asarray(inputs["L0_ffn_b1"], np.float32)
    pc["b1t"] = rep(np.transpose(b1.reshape(E8, 32, 128), (0, 2, 1)))
    w2 = np.asarray(inputs["L0_ffn_W2"], np.float32)
    pc["w2t"] = rep(np.transpose(w2.reshape(E8, F1, DC, 128), (0, 2, 1, 3)))
    pc["b2"] = rep(np.asarray(inputs["L0_ffn_b2"], np.float32))

    perm_a = np.concatenate([np.concatenate([64 * h + np.arange(0, 64, 2),
                                             64 * h + np.arange(1, 64, 2)])
                             for h in range(H)])
    perm_b = np.concatenate([np.concatenate([64 * h + np.arange(1, 64, 2),
                                             64 * h + np.arange(0, 64, 2)])
                             for h in range(H)])
    wq1 = np.asarray(inputs["L1_attn_wq"], np.float32).T
    pc["wqT1a"] = rep(wq1[:, perm_a])
    pc["wqT1b"] = rep(wq1[:, perm_b])
    wk1 = np.asarray(inputs["L1_attn_wk"], np.float32)
    pa64 = np.concatenate([np.arange(0, 64, 2), np.arange(1, 64, 2)])
    pb64 = np.concatenate([np.arange(1, 64, 2), np.arange(0, 64, 2)])
    pc["wk1a"] = rep(wk1[:, pa64])
    pc["wk1b"] = rep(wk1[:, pb64])
    pc["wv1"] = rep(np.asarray(inputs["L1_attn_wv"], np.float32))
    pc["woT"] = rep(np.asarray(inputs["L1_attn_wo"], np.float32).T)
    freqs = 1.0 / (1000.0 ** (np.arange(0, HD, 2, dtype=np.float64) / HD))
    cos_l, sin_l = [], []
    for c in range(NCORE):
        posv = np.arange((c % 4) * T, (c % 4 + 1) * T, dtype=np.float64)
        ang = freqs[:, None] * posv[None, :]
        co = np.cos(ang).astype(np.float32)
        si = np.sin(ang).astype(np.float32)
        cos4 = np.tile(co, (4, 1))
        sin4 = np.concatenate([-si, si, -si, si], axis=0)
        cos_l.append(np.ascontiguousarray(cos4))
        sin_l.append(np.ascontiguousarray(sin4))
    pc["cos4"] = cos_l
    pc["sin4"] = sin_l

    wgq = np.asarray(inputs["L1_ffn_w_gate"], np.int32)
    wuq = np.asarray(inputs["L1_ffn_w_up"], np.int32)
    wdq = np.asarray(inputs["L1_ffn_w_down"], np.int32)
    pc["wgT1"] = rep(np.ascontiguousarray(
        np.transpose(wgq.T.reshape(D, 24, 128), (1, 0, 2))))
    pc["wuT1"] = rep(np.ascontiguousarray(
        np.transpose(wuq.T.reshape(D, 24, 128), (1, 0, 2))))
    pc["wdT1"] = rep(np.ascontiguousarray(
        np.transpose(wdq.T.reshape(H3, DC, 128), (1, 0, 2))))
    return pc


_IN_NAMES = {
    "l0a": ["wqT", "wkT", "wvT", "bqkv", "gb", "bvrow"],
    "l0f": ["wgt", "bg", "w1t", "b1t", "w2t", "b2"],
    "l1a": ["wqT1a", "wqT1b", "wk1a", "wk1b", "wv1", "woT", "cos4", "sin4"],
    "l1f": ["wgT1", "wuT1", "wdT1"],
}


def kernel(**inputs):
    pc = _prep(inputs)
    xcur = pc["x_in"]
    for group in LAUNCH_PLAN:
        nc = build_program(group)
        in_maps = []
        for c in range(NCORE):
            m = {"x_in": np.ascontiguousarray(xcur[c])}
            for s in group:
                for n in _IN_NAMES[s]:
                    m[n] = pc[n][c]
            in_maps.append(m)
        res = run_bass_kernel_spmd(nc, in_maps, core_ids=list(range(NCORE)))
        xcur = [np.asarray(res.results[c]["x_out"]) for c in range(NCORE)]
    out = np.empty((B, S, D), dtype=np.float32)
    for c in range(NCORE):
        out[c // 4, (c % 4) * T:(c % 4 + 1) * T, :] = xcur[c].T
    return out


# version 03
# speedup vs baseline: 1.1184x; 1.1184x over previous
import numpy as np
import ml_dtypes

import concourse.bass as bass
import concourse.mybir as mybir
from concourse.bass_utils import run_bass_kernel_spmd

F32 = mybir.dt.float32
BF16 = mybir.dt.bfloat16
I32 = mybir.dt.int32
AF = mybir.ActivationFunctionType
ALU = mybir.AluOpType
AX = mybir.AxisListType
BF = ml_dtypes.bfloat16

B, S, D, T = 2, 8192, 1024, 2048
H, HD = 16, 64
E8, F1, H3 = 8, 4096, 3072
NCORE = 8
DC = D // 128
BLK = 2048 + 64
GROUPS = [[0, 1, 2, 3], [4, 5, 6, 7]]
L0SCALE = float(1.0 / np.sqrt(S))
L1SCALE = float(1.0 / np.sqrt(HD))
EPS = 1e-5
INVD = 1.0 / D
Q127 = 1.0 / 127.0

LAUNCH_PLAN = [["l0a", "l0f", "l1a", "l1f"]]
DEBUG_PHASE = 99
DEBUG_DUMP = []
DETECT_RACES = True


def _sl(s, n):
    return slice(s, s + n)


class Prog:
    ENGS = ("tensor", "vector", "scalar", "gpsimd", "sync")

    def __init__(self, nc):
        self.nc = nc
        self.q = {e: [] for e in self.ENGS}
        self.sems = {}
        self.cnt = {}

    def sem(self, name):
        if name not in self.sems:
            self.sems[name] = self.nc.alloc_semaphore(name)
            self.cnt[name] = 0
        return self.sems[name]

    def emit(self, eng, fn, inc=None):
        incs = []
        if inc is not None:
            for s, a in (inc if isinstance(inc, list) else [inc]):
                self.sem(s)
                self.cnt[s] += a
                incs.append((self.sems[s], a))

        def run(e, fn=fn, incs=incs):
            ins = fn(e)
            for h, a in incs:
                ins = ins.then_inc(h, a)
            return ins
        self.q[eng].append(run)

    def wait(self, eng, s, val):
        if val <= 0:
            return
        h = self.sem(s)
        self.q[eng].append(lambda e, h=h, v=val: e.wait_ge(h, v))

    def val(self, s):
        self.sem(s)
        return self.cnt[s]

    def barrier(self, tag):
        vals = {sname: self.cnt[sname] for sname in self.cnt}
        for eng in self.ENGS:
            for sname, v in vals.items():
                self.wait(eng, sname, v)

    def finalize(self):
        with self.nc.Block() as block:
            for engname in self.ENGS:
                ops = self.q[engname]

                def body(eng, ops=ops):
                    for f in ops:
                        f(eng)
                getattr(block, engname)(body)


def build_program(sands):
    nc = bass.Bass("TRN2", target_bir_lowering=False, debug=False,
                   detect_race_conditions=DETECT_RACES)
    pr = Prog(nc)
    tn = {}
    A = {}
    P = {}

    tn["x_in"] = nc.dram_tensor("x_in", [D, T], F32, kind="ExternalInput")
    tn["x_out"] = nc.dram_tensor("x_out", [D, T], F32, kind="ExternalOutput")

    if "l0a" in sands:
        for w in ("wqT", "wkT", "wvT"):
            tn[w] = nc.dram_tensor(w, [D, D], F32, kind="ExternalInput")
        tn["bqkv"] = nc.dram_tensor("bqkv", [128, 24], F32, kind="ExternalInput")
        tn["gb"] = nc.dram_tensor("gb", [128, 16], F32, kind="ExternalInput")
        tn["bvrow"] = nc.dram_tensor("bvrow", [1, D], F32, kind="ExternalInput")
    if "l0f" in sands:
        tn["wgt"] = nc.dram_tensor("wgt", [D, E8], F32, kind="ExternalInput")
        tn["bg"] = nc.dram_tensor("bg", [E8, 1], F32, kind="ExternalInput")
        tn["w1t"] = nc.dram_tensor("w1t", [E8, 32, D, 128], F32, kind="ExternalInput")
        tn["b1t"] = nc.dram_tensor("b1t", [E8, 128, 32], F32, kind="ExternalInput")
        tn["w2t"] = nc.dram_tensor("w2t", [E8, DC, F1, 128], F32, kind="ExternalInput")
        tn["b2"] = nc.dram_tensor("b2", [E8, D], F32, kind="ExternalInput")
    if "l1a" in sands:
        tn["wqT1a"] = nc.dram_tensor("wqT1a", [D, D], F32, kind="ExternalInput")
        tn["wqT1b"] = nc.dram_tensor("wqT1b", [D, D], F32, kind="ExternalInput")
        tn["wk1a"] = nc.dram_tensor("wk1a", [D, HD], F32, kind="ExternalInput")
        tn["wk1b"] = nc.dram_tensor("wk1b", [D, HD], F32, kind="ExternalInput")
        tn["wv1"] = nc.dram_tensor("wv1", [D, HD], F32, kind="ExternalInput")
        tn["woT"] = nc.dram_tensor("woT", [D, D], F32, kind="ExternalInput")
        tn["cos4"] = nc.dram_tensor("cos4", [128, T], F32, kind="ExternalInput")
        tn["sin4"] = nc.dram_tensor("sin4", [128, T], F32, kind="ExternalInput")
    if "l1f" in sands:
        tn["wgT1"] = nc.dram_tensor("wgT1", [24, D, 128], I32, kind="ExternalInput")
        tn["wuT1"] = nc.dram_tensor("wuT1", [24, D, 128], I32, kind="ExternalInput")
        tn["wdT1"] = nc.dram_tensor("wdT1", [DC, H3, 128], I32, kind="ExternalInput")

    tn["statb"] = nc.dram_tensor("statb", [1, 4096], F32)
    tn["rowb"] = nc.dram_tensor("rowb", [1, 1024], F32)
    if "l0a" in sands or "l1a" in sands:
        tn["g1_src"] = nc.dram_tensor("g1_src", [BLK, T], BF16)
        tn["g1_out"] = nc.dram_tensor("g1_out", [4 * BLK, T], BF16)
        tn["g2_src"] = nc.dram_tensor("g2_src", [T, D], BF16)
        tn["g2_out"] = nc.dram_tensor("g2_out", [S, D], BF16)
        tn["g3_src"] = nc.dram_tensor("g3_src", [256, S], BF16)
        tn["g3_out"] = nc.dram_tensor("g3_out", [D, S], BF16)
    if "l0f" in sands:
        tn["w8d"] = nc.dram_tensor("w8d", [E8, T], BF16)

    kl = np.arange(128)[:, None]
    ql = np.arange(512)[None, :]
    if "l0a" in sands:
        m0 = np.concatenate([(kl + o >= ql).astype(BF) for o in (0, 128, 256, 384)],
                            axis=1)
        tn["mask0"] = nc.inline_tensor(np.ascontiguousarray(m0), "mask0")
    if "l1a" in sands:
        m1 = np.concatenate([(kl + o <= ql).astype(BF) for o in (0, 128, 256, 384)],
                            axis=1)
        tn["mask1"] = nc.inline_tensor(np.ascontiguousarray(m1), "mask1")
    if "l0f" in sands:
        tn["ident"] = nc.inline_tensor(np.eye(128, dtype=BF), "ident")

    BASE = (nc.sbuf_base + 31) // 32 * 32
    AB = BASE + 131072
    TAIL = BASE + 202240
    _ranges = []

    def sb(name, shape, dtype, off, group="a"):
        import numpy as _np
        nbytes = int(_np.prod(shape[1:])) * {F32: 4, BF16: 2, I32: 4}[dtype]
        assert off + nbytes <= nc.sbuf_top, f"{name} past sbuf top"
        for (n2, o2, e2, g2) in _ranges:
            if g2 == group and not (off + nbytes <= o2 or e2 <= off):
                raise AssertionError(f"sbuf overlap {name} [{off},{off + nbytes}) "
                                     f"vs {n2} [{o2},{e2}) group {group}")
        _ranges.append((name, off, off + nbytes, group))
        A[name] = nc.alloc_sbuf_tensor_at(name, shape, dtype, offset=off)
        return A[name]

    sb("X", [128, DC * T], F32, BASE + 0, "tail")
    sb("LNX", [128, DC * T], BF16, BASE + 65536, "tail")
    sb("Z", [128, DC * T], BF16, BASE + 98304, "tail")
    sb("TMP", [128, T], F32, AB + 0, "ln1")
    sb("SQ", [128, 1024], F32, AB + 0, "ln2")
    sb("ABC", [128, T], F32, AB + 8192, "ln1")
    sb("BBC", [128, T], F32, AB + 16384, "ln1")
    sb("R0", [1, 512], F32, AB + 24576, "ln1")
    sb("R1", [1, 512], F32, AB + 26624, "ln1")
    sb("R2", [1, 512], F32, AB + 28672, "ln1")
    sb("WB0", [128, 8192], BF16, AB + 30720, "ln1")
    sb("STG", [128, T], BF16, AB + 47104, "ln1")
    sb("ONESF", [128, 1], F32, TAIL + 0, "tail")
    sb("ONESB", [128, 1], BF16, TAIL + 32, "tail")
    sb("SCL", [128, 4], F32, TAIL + 64, "tail")
    if "l0a" in sands:
        sb("gbs", [128, 16], F32, TAIL + 96, "tail")
        sb("bqkvs", [128, 24], F32, TAIL + 160, "tail")
        sb("BVB", [128, D], BF16, TAIL + 6976, "tail")
    if "l0f" in sands:
        sb("bgs", [8, 1], F32, TAIL + 288, "tail")
        sb("b1s", [128, 32], F32, TAIL + 320, "tail")
        sb("WGS", [128, 64], BF16, TAIL + 448, "tail")
        sb("IDS", [128, 128], BF16, TAIL + 576, "tail")
        sb("B2S", [8, D], BF16, TAIL + 832, "tail")
    if "l0a" in sands or "l1a" in sands:
        sb("MSK", [128, 2048], BF16, TAIL + 2880, "tail")
        sb("QS0", [128, S], BF16, AB + 0, "attn")
        sb("KP", [128, S], BF16, AB + 16384, "attn")
        sb("VS0", [128, 64 * 64], BF16, AB + 32768, "attn")
        sb("VS1", [128, 64 * 64], BF16, AB + 40960, "attn")
        sb("EB0", [128, 1024], BF16, AB + 49152, "attn")
        sb("EB1", [128, 1024], BF16, AB + 51200, "attn")
        sb("EM", [128, 2048], BF16, AB + 53248, "attn")
        sb("PSTG", [128, 1024], BF16, AB + 57344, "attn")
        sb("RB", [128, 1024], F32, AB + 59392, "attn")
        sb("RRX", [128, 512], F32, AB + 63488, "attn")
    if "l1a" in sands:
        sb("COS", [128, T], F32, AB + 0, "l1t")
        sb("SIN", [128, T], F32, AB + 8192, "l1t")
        sb("WRKA", [128, T], BF16, AB + 16384, "l1t")
        sb("WRKB", [128, T], BF16, AB + 20480, "l1t")
        sb("KTA", [64, T], BF16, AB + 24576, "l1t")
        sb("KTB", [64, T], BF16, AB + 28672, "l1t")
        sb("WVs", [128, 512], BF16, AB + 28672, "l1t2")
    if "l0f" in sands:
        sb("HSL", [128, 32 * 512], BF16, AB + 0, "l0f_h")
        sb("W8B", [8, T], BF16, AB + 28672, "l0f")
        sb("W1B", [128, 4096], BF16, AB + 32768, "l0f")
        sb("W2B", [128, 2 * F1], BF16, AB + 40960, "l0f")
        sb("WBC", [128, 1024], F32, AB + 57344, "l0f")
        sb("TMI", [128, 512], F32, AB + 61440, "l0f")
        sb("GS", [8, T], BF16, AB + 63488, "l0f")
        sb("GTA", [128, 128], BF16, AB + 67584, "l0f")
        sb("GEB", [128, 128], F32, AB + 67840, "l0f")
        sb("GEC", [128, 128], F32, AB + 68352, "l0f")
        sb("REDA", [128, 48], F32, AB + 68864, "l0f")
    if "l1f" in sands:
        sb("HSL1", [128, 24 * 1024], BF16, AB + 0, "l1f")
        sb("WI", [128, 3072], I32, AB + 49152, "l1f")
        sb("WF0", [128, 3072], BF16, AB + 61440, "l1f")
        sb("WF1", [128, 1024], BF16, AB + 67584, "l1f")
        sb("SIL", [128, 512], BF16, AB + 69632, "l1f")

    P["B0"] = nc.alloc_psum_tensor("B0", [128, 512], F32)
    P["B1"] = nc.alloc_psum_tensor("B1", [128, 512], F32)
    P["B2"] = nc.alloc_psum_tensor("B2", [128, 512], F32)
    P["B3"] = nc.alloc_psum_tensor("B3", [128, 512], F32)
    P["ACC"] = nc.alloc_psum_tensor("ACC", [128, 512], F32)
    P["DEN"] = nc.alloc_psum_tensor("DEN", [128, 512], F32)
    P["PST"] = nc.alloc_psum_tensor("PST", [8, 512], F32)
    P["PTR"] = nc.alloc_psum_tensor("PTR", [128, 128], F32)

    pid_cache = {}

    def get_sym(e, name):
        key = (str(e.engine), name)
        if key not in pid_cache:
            pid = nc.partition_id(engines=[e.engine])
            mod = pid % 4
            pid_cache[(str(e.engine), "hg256")] = mod * 256
            pid_cache[(str(e.engine), "tcol")] = mod * T
        return pid_cache[key]

    ctx = dict(pr=pr, nc=nc, tn=tn, A=A, P=P, get_sym=get_sym)

    pr.emit("sync", lambda e: e.dma_start(
        out=A["X"].ap().rearrange("p (c t) -> p c t", t=T),
        in_=tn["x_in"].ap().rearrange("(c p) t -> p c t", p=128)), inc=("ldw", 16))
    pr.emit("vector", lambda e: e.memset(A["ONESF"][:, :], 1.0))
    pr.emit("vector", lambda e: e.memset(A["ONESB"][:, :], 1.0))
    pr.emit("vector", lambda e: e.memset(A["SCL"][:, 0:1], L0SCALE))
    pr.emit("vector", lambda e: e.memset(A["SCL"][:, 1:2], L1SCALE))
    pr.emit("vector", lambda e: e.memset(A["SCL"][:, 2:3], Q127))
    pr.emit("vector", lambda e: e.memset(A["SCL"][:, 3:4], 0.0), inc=("dve", 1))
    m = pr.val("dve")
    for eng in ("tensor", "scalar", "gpsimd", "sync"):
        pr.wait(eng, "dve", m)
    lm = pr.val("ldw")
    for eng in ("tensor", "vector", "scalar", "gpsimd"):
        pr.wait(eng, "ldw", lm)

    for sand in sands:
        if sand == "l0a":
            emit_l0a(ctx)
        elif sand == "l0f":
            emit_l0f(ctx)
        elif sand == "l1a":
            emit_l1a(ctx)
        elif sand == "l1f":
            emit_l1f(ctx)
        pr.barrier(sand)

    for nm in DEBUG_DUMP:
        h = A[nm]
        shp = list(h.ap().shape)
        dt_ = h.ap().dtype
        tdbg = nc.dram_tensor(f"dbg_{nm}", shp, dt_, kind="ExternalOutput")
        pr.emit("gpsimd", lambda e, tdbg=tdbg, h=h: e.dma_start(
            out=tdbg.ap(), in_=h[:, :]), inc=("st", 16))
    pr.emit("gpsimd", lambda e: e.dma_start(
        out=tn["x_out"].ap().rearrange("(c p) t -> p c t", p=128),
        in_=A["X"].ap().rearrange("p (c t) -> p c t", t=T)), inc=("st", 16))
    pr.wait("gpsimd", "st", pr.val("st"))
    pr.finalize()
    return nc


def emit_ln(ctx, src, src_dt, mode):
    pr, nc, tn, A, P = ctx["pr"], ctx["nc"], ctx["tn"], ctx["A"], ctx["P"]
    ones = A["ONESF"] if src_dt == F32 else A["ONESB"]
    ssrc = A[src]
    for sname in ("dve", "zev", "hev", "cst"):
        pr.wait("tensor", sname, pr.val(sname))
        pr.wait("scalar", sname, pr.val(sname))
    sqmm = []
    rowst = []
    for qc in range(4):
        for dc in range(DC):
            col = _sl(dc * T + qc * 512, 512)
            half = _sl((dc % 2) * 512, 512)
            if len(sqmm) >= 2:
                pr.wait("scalar", "pe2", sqmm[-2])
            pr.emit("scalar", lambda e, col=col, half=half: e.activation(
                A["SQ"][:, half], ssrc[:, col], AF.Square), inc=("act", 1))
            pr.wait("tensor", "act", pr.val("act"))
            pr.emit("tensor", lambda e, dc=dc, half=half: e.matmul(
                P["B3"][0:1, :], A["ONESF"][:, :], A["SQ"][:, half],
                start=(dc == 0), stop=(dc == DC - 1)), inc=("pe2", 1))
            sqmm.append(pr.val("pe2"))
            pr.emit("tensor", lambda e, col=col, dc=dc: e.matmul(
                P["PST"][0:1, :], ones[:, :], ssrc[:, col],
                start=(dc == 0), stop=(dc == DC - 1)),
                inc=("pe", 1) if dc == DC - 1 else None)
        pm = pr.val("pe")
        pr.wait("vector", "pe", pm)
        pr.wait("vector", "pe2", pr.val("pe2"))
        if rowst:
            pr.wait("vector", "st", rowst[-1])
        pr.emit("vector", lambda e: e.tensor_scalar(
            A["R0"][:, :], P["PST"][0:1, :], INVD, None, ALU.mult))
        pr.emit("vector", lambda e: e.tensor_scalar(
            A["R1"][:, :], P["B3"][0:1, :], INVD, None, ALU.mult),
            inc=("dve", 1))
        bankfree = pr.val("dve")
        pr.wait("tensor", "dve", bankfree)
        pr.emit("vector", lambda e: e.tensor_tensor(
            A["R2"][:, :], A["R0"][:, :], A["R0"][:, :], ALU.mult))
        pr.emit("vector", lambda e: e.tensor_tensor(
            A["R1"][:, :], A["R1"][:, :], A["R2"][:, :], ALU.subtract))
        pr.emit("vector", lambda e: e.tensor_scalar(
            A["R1"][:, :], A["R1"][:, :], EPS, None, ALU.add), inc=("dve", 1))
        pr.wait("scalar", "dve", pr.val("dve"))
        pr.emit("scalar", lambda e: e.activation(
            A["R2"][:, :], A["R1"][:, :], AF.Sqrt), inc=("act", 1))
        pr.wait("vector", "act", pr.val("act"))
        pr.emit("vector", lambda e: e.reciprocal(A["R1"][:, :], A["R2"][:, :]))
        pr.emit("vector", lambda e: e.tensor_tensor(
            A["R2"][:, :], A["R0"][:, :], A["R1"][:, :], ALU.mult))
        pr.emit("vector", lambda e: e.tensor_scalar(
            A["R2"][:, :], A["R2"][:, :], -1.0, None, ALU.mult), inc=("dve", 1))
        rm = pr.val("dve")
        pr.wait("gpsimd", "dve", rm)
        pr.emit("gpsimd", lambda e, qc=qc: e.dma_start(
            out=tn["statb"].ap()[0:1, _sl(qc * 512, 512)], in_=A["R1"][:, :]),
            inc=("st", 16))
        pr.emit("gpsimd", lambda e, qc=qc: e.dma_start(
            out=tn["statb"].ap()[0:1, _sl(2048 + qc * 512, 512)], in_=A["R2"][:, :]),
            inc=("st", 16))
        rowst.append(pr.val("st"))
    stm = pr.val("st")
    pr.wait("sync", "st", stm)
    pr.emit("sync", lambda e: e.dma_start(
        out=A["ABC"][:, :], in_=tn["statb"].ap()[0:1, 0:2048].broadcast_to([128, T])),
        inc=("ldw", 16))
    pr.emit("sync", lambda e: e.dma_start(
        out=A["BBC"][:, :], in_=tn["statb"].ap()[0:1, 2048:4096].broadcast_to([128, T])),
        inc=("ldw", 16))
    pr.wait("vector", "ldw", pr.val("ldw"))
    for dc in range(DC):
        col = _sl(dc * T, T)
        pr.emit("vector", lambda e, col=col: e.tensor_tensor(
            A["TMP"][:, :], ssrc[:, col], A["ABC"][:, :], ALU.mult))
        if mode == "pre":
            pr.emit("vector", lambda e, col=col: e.tensor_tensor(
                A["LNX"][:, col], A["TMP"][:, :], A["BBC"][:, :], ALU.add),
                inc=("dve", 1))
        elif mode == "aff":
            pr.emit("vector", lambda e: e.tensor_tensor(
                A["TMP"][:, :], A["TMP"][:, :], A["BBC"][:, :], ALU.add))
            pr.emit("vector", lambda e, col=col, dc=dc: e.tensor_scalar(
                A["LNX"][:, col], A["TMP"][:, :],
                A["gbs"][:, _sl(dc, 1)], A["gbs"][:, _sl(8 + dc, 1)],
                ALU.mult, ALU.add), inc=("dve", 1))
        elif mode == "res":
            pr.emit("vector", lambda e: e.tensor_tensor(
                A["TMP"][:, :], A["TMP"][:, :], A["BBC"][:, :], ALU.add))
            pr.emit("vector", lambda e, col=col: e.tensor_tensor(
                A["X"][:, col], A["X"][:, col], A["TMP"][:, :], ALU.add),
                inc=("dve", 1))


def emit_proj(ctx, wnames, n_out, out_cb, wbuf="WB0", rhs_slab="LNX"):
    pr, nc, tn, A, P = ctx["pr"], ctx["nc"], ctx["tn"], ctx["A"], ctx["P"]
    W = A[wbuf]
    marks = []
    for wi, wname in enumerate(wnames):
        pr.wait("gpsimd", "pe", pr.val("pe"))
        pr.wait("tensor", "dve", pr.val("dve"))
        for dc in range(DC):
            src = tn[wname].ap()[_sl(dc * 128, 128), :]
            pr.emit("gpsimd", lambda e, dc=dc, src=src, n_out=n_out: e.dma_start(
                out=W[:, _sl(dc * n_out, n_out)], in_=src), inc=("ldg", 16))
        pr.wait("tensor", "ldg", pr.val("ldg"))
        for oc in range(n_out // 128):
            for qc in range(4):
                half = (oc * 4 + qc) % 4
                ps = [P["B0"][:, :], P["B1"][:, :], P["B2"][:, :],
                      P["B3"][:, :]][half]
                if len(marks) >= 4:
                    pr.wait("tensor", "dve", marks[-4])
                for dc in range(DC):
                    lhsT = W[:, _sl(dc * n_out + oc * 128, 128)]
                    rhs = A[rhs_slab][:, _sl(dc * T + qc * 512, 512)]
                    pr.emit("tensor", lambda e, ps=ps, lhsT=lhsT, rhs=rhs, dc=dc:
                            e.matmul(ps, lhsT, rhs, start=(dc == 0),
                                     stop=(dc == DC - 1)),
                            inc=("pe", 1) if dc == DC - 1 else None)
                pm = pr.val("pe")
                pr.wait("vector", "pe", pm)
                out_cb(wi, oc, qc, ps)
                pr.emit("vector", lambda e: e.engine_nop(), inc=("dve", 1))
                marks.append(pr.val("dve"))


def emit_l0a(ctx):
    pr, nc, tn, A, P = ctx["pr"], ctx["nc"], ctx["tn"], ctx["A"], ctx["P"]
    pr.emit("sync", lambda e: e.dma_start(out=A["gbs"][:, :], in_=tn["gb"].ap()),
            inc=("ldw", 16))
    pr.emit("sync", lambda e: e.dma_start(out=A["bqkvs"][:, :], in_=tn["bqkv"].ap()),
            inc=("ldw", 16))
    pr.emit("sync", lambda e: e.dma_start(out=A["MSK"][:, :], in_=tn["mask0"].ap()),
            inc=("ldw", 16))
    pr.emit("gpsimd", lambda e: e.dma_start(
        out=A["BVB"][:, :], in_=tn["bvrow"].ap()[0:1, :].broadcast_to([128, D])),
        inc=("ldg", 16))
    pr.wait("vector", "ldw", pr.val("ldw"))
    pr.wait("vector", "ldg", pr.val("ldg"))
    pr.wait("scalar", "ldw", pr.val("ldw"))
    if DEBUG_PHASE < 1:
        return

    emit_ln(ctx, "X", F32, "pre")
    pr.wait("tensor", "dve", pr.val("dve"))
    if DEBUG_PHASE < 2:
        return

    stflags = {"last": None}

    def store_fm(base_row, bias_base):
        def cb(wi, oc, qc, ps):
            if qc == 0 and stflags["last"] is not None:
                pr.wait("vector", "st", stflags["last"])
            pr.emit("vector", lambda e, ps=ps, qc=qc, oc=oc: e.tensor_scalar(
                A["STG"][:, _sl(qc * 512, 512)], ps,
                A["bqkvs"][:, _sl(bias_base + oc, 1)], None, ALU.add),
                inc=("dve", 1))
            if qc == 3:
                dm = pr.val("dve")
                pr.wait("gpsimd", "dve", dm)
                pr.emit("gpsimd", lambda e, oc=oc: e.dma_start(
                    out=tn["g1_src"].ap()[_sl(base_row + oc * 128, 128), :],
                    in_=A["STG"][:, :]), inc=("st", 16))
                stflags["last"] = pr.val("st")
        return cb

    emit_proj(ctx, ["wkT"], D, store_fm(1024, 8))
    if DEBUG_PHASE < 3:
        return
    pr.wait("gpsimd", "pe", pr.val("pe"))
    for dc in range(DC):
        pr.emit("gpsimd", lambda e, dc=dc: e.dma_start(
            out=A["WB0"][:, _sl(dc * D, D)], in_=tn["wvT"].ap()[_sl(dc * 128, 128), :]),
            inc=("ldg", 16))
    pr.wait("tensor", "ldg", pr.val("ldg"))
    pr.wait("tensor", "dve", pr.val("dve"))
    marks = []
    for tt in range(16):
        for fh in range(2):
            half = (tt * 2 + fh) % 4
            ps = [P["B0"][:, :], P["B1"][:, :], P["B2"][:, :],
                  P["B3"][:, :]][half]
            if len(marks) >= 4:
                pr.wait("tensor", "dve", marks[-4])
            for dc in range(DC):
                lhsT = A["LNX"][:, _sl(dc * T + tt * 128, 128)]
                rhs = A["WB0"][:, _sl(dc * D + fh * 512, 512)]
                pr.emit("tensor", lambda e, ps=ps, lhsT=lhsT, rhs=rhs, dc=dc: e.matmul(
                    ps, lhsT, rhs, start=(dc == 0), stop=(dc == DC - 1)),
                    inc=("pe", 1) if dc == DC - 1 else None)
            pm = pr.val("pe")
            pr.wait("vector", "pe", pm)
            if fh == 0 and stflags["last"] is not None:
                pr.wait("vector", "st", stflags["last"])
            pr.emit("vector", lambda e, ps=ps, fh=fh: e.tensor_tensor(
                A["STG"][:, _sl(fh * 512, 512)], ps, A["BVB"][:, _sl(fh * 512, 512)],
                ALU.add), inc=("dve", 1))
            marks.append(pr.val("dve"))
        dm = pr.val("dve")
        pr.wait("gpsimd", "dve", dm)
        pr.emit("gpsimd", lambda e, tt=tt: e.dma_start(
            out=tn["g2_src"].ap()[_sl(tt * 128, 128), :], in_=A["STG"][:, 0:1024]),
            inc=("st", 16))
        stflags["last"] = pr.val("st")
    pr.wait("gpsimd", "st", pr.val("st"))
    pr.emit("gpsimd", lambda e: e.collective_compute(
        "AllGather", ALU.bypass, replica_groups=GROUPS,
        ins=[tn["g2_src"].ap()], outs=[tn["g2_out"].ap()]), inc=("cc", 1))
    if DEBUG_PHASE < 4:
        return
    emit_proj(ctx, ["wqT"], D, store_fm(0, 0))
    pr.wait("gpsimd", "st", pr.val("st"))
    pr.emit("gpsimd", lambda e: e.collective_compute(
        "AllGather", ALU.bypass, replica_groups=GROUPS,
        ins=[tn["g1_src"].ap()], outs=[tn["g1_out"].ap()]), inc=("cc", 1))

    if DEBUG_PHASE < 5:
        return
    emit_attn(ctx, layer=0)
    if DEBUG_PHASE < 6:
        return

    pr.wait("gpsimd", "st", pr.val("st"))
    pr.emit("gpsimd", lambda e: e.collective_compute(
        "AllGather", ALU.bypass, replica_groups=GROUPS,
        ins=[tn["g3_src"].ap()], outs=[tn["g3_out"].ap()]), inc=("cc", 1))
    ccz = pr.val("cc")
    pr.wait("sync", "cc", ccz)
    pr.wait("sync", "dve", pr.val("dve"))
    def ldz(e):
        c0 = ctx["get_sym"](e, "tcol")
        src = tn["g3_out"].ap().rearrange("(c p) t -> p c t", p=128)
        src = src[:, :, bass.ds(c0, T)]
        return e.dma_start(out=A["Z"].ap().rearrange("p (c t) -> p c t", t=T),
                           in_=src)
    pr.emit("sync", ldz, inc=("ldw", 16))
    pr.wait("vector", "ldw", pr.val("ldw"))
    for dc in range(DC):
        col = _sl(dc * T, T)
        pr.emit("vector", lambda e, col=col: e.tensor_tensor(
            A["Z"][:, col], A["Z"][:, col], A["LNX"][:, col], ALU.add), inc=("dve", 1))
    emit_ln(ctx, "Z", BF16, "aff")
    emit_ln(ctx, "LNX", BF16, "res")


def emit_attn(ctx, layer):
    pr, nc, tn, A, P = ctx["pr"], ctx["nc"], ctx["tn"], ctx["A"], ctx["P"]
    ccmark = pr.val("cc")
    pr.wait("sync", "cc", ccmark)
    scale_col = _sl(layer, 1)

    state = {"acfree": 0, "ss_last": [0, 0], "eb_last": [0, 0], "em_last": [0, 0]}
    pr.wait("tensor", "dve", pr.val("dve"))
    for p_ in range(2):
        QS = A["QS0"]
        pr.wait("sync", "pe", pr.val("pe"))

        def ldq(e, p_=p_, QS=QS):
            hg = ctx["get_sym"](e, "hg256")
            src = tn["g1_out"].ap().rearrange("(r b) t -> b r t", r=4)
            src = src[_sl(p_ * 128, 896), :, :][bass.ds(hg, 128), :, :]
            return e.dma_start(out=QS.ap().rearrange("p (r t) -> p r t", r=4),
                               in_=src)
        pr.emit("sync", ldq, inc=("ldw", 16))
        if layer == 0:
            def ldk(e, p_=p_):
                hg = ctx["get_sym"](e, "hg256")
                src = tn["g1_out"].ap().rearrange("(r b) t -> b r t", r=4)
                src = src[_sl(1024 + p_ * 128, 896), :, :][bass.ds(hg, 128), :, :]
                return e.dma_start(out=A["KP"].ap().rearrange(
                    "p (r t) -> p r t", r=4), in_=src)
            pr.emit("sync", ldk, inc=("ldw", 16))
        else:
            for hb in range(2):
                def ldk(e, hb=hb):
                    src = tn["g1_out"].ap().rearrange(
                        "(r b) t -> b r t", r=4)[_sl(1024, 64), :, :]
                    return e.dma_start(
                        out=A["KP"].ap().rearrange(
                            "p (r t) -> p r t", r=4)[_sl(hb * 64, 64), :, :],
                        in_=src)
                pr.emit("sync", ldk, inc=("ldw", 16))
        if layer == 0:
            for hh in range(2):
                def ldv(e, p_=p_, hh=hh):
                    hg = ctx["get_sym"](e, "hg256")
                    src = tn["g2_out"].ap().rearrange(
                        "(r k p) c -> p r k c", p=128, r=4)
                    src = src[:, :, :, _sl(p_ * 128 + hh * 64, 832)]
                    src = src[:, :, :, bass.ds(hg, 64)]
                    dst = A[("VS0", "VS1")[hh]].ap().rearrange(
                        "p (r k c) -> p r k c", c=64, r=4)
                    return e.dma_start(out=dst, in_=src)
                pr.emit("sync", ldv, inc=("ldw", 16))
        else:
            def ldv(e):
                src = tn["g2_out"].ap().rearrange(
                    "(r k p) c -> p r k c", p=128, r=4)[:, :, :, 0:64]
                dst = A["VS0"].ap().rearrange(
                    "p (r k c) -> p r k c", c=64, r=4)
                return e.dma_start(out=dst, in_=src)
            pr.emit("sync", ldv, inc=("ldw", 16))
        kvm = pr.val("ldw")
        for eng in ("tensor", "scalar", "vector"):
            pr.wait(eng, "ldw", kvm)

        for qc in range(16):
            if layer == 0:
                kts = list(range(64))
                regions = ["below" if kt < 4 * qc else
                           ("diag" if kt <= 4 * qc + 3 else "above") for kt in kts]
            else:
                kts = list(range(min(64, 4 * qc + 4)))
                regions = ["full" if kt < 4 * qc else "diag" for kt in kts]
            n = len(kts)
            qk_marks = [None] * n
            exp_marks = [None] * n
            pv_marks = [None] * n
            msk_marks = [None] * n
            grp = {"acc": False, "den": False}
            last_acc = max(i for i in range(n) if regions[i] != "below")
            last_den = n - 1

            def emit_pv(j):
                kt, region = kts[j], regions[j]
                EBj = A["EB0"] if j % 2 == 0 else A["EB1"]
                pr.wait("tensor", "act", exp_marks[j])
                if region == "diag":
                    pr.wait("tensor", "dve", msk_marks[j])
                pr.wait("tensor", "dve", state["acfree"])
                num_src = (A["EM"].ap()[:, _sl((j % 2) * 1024, 1024)]
                           if region == "diag" else EBj)
                den_src = EBj if layer == 0 else \
                    (A["EM"].ap()[:, _sl((j % 2) * 1024, 1024)]
                     if region == "diag" else EBj)
                if region != "below":
                    st0 = not grp["acc"]
                    grp["acc"] = True
                    stop = (j == last_acc)
                    pr.emit("tensor", lambda e, kt=kt, src=num_src, st0=st0, stop=stop:
                            e.matmul(P["ACC"][0:64, :],
                                     A["VS0"][:, _sl(kt * 64, 64)],
                                     src[:, 0:512], start=st0, stop=stop,
                                     tile_position=(0, 0)))
                    VS_o = A["VS0"] if layer == 1 else A["VS1"]
                    pr.emit("tensor", lambda e, kt=kt, src=num_src, st0=st0, stop=stop,
                            VS_o=VS_o:
                            e.matmul(P["ACC"][64:128, :],
                                     VS_o[:, _sl(kt * 64, 64)],
                                     src[:, 512:1024], start=st0, stop=stop,
                                     tile_position=(0, 64)))
                st0 = not grp["den"]
                grp["den"] = True
                stop = (j == last_den)
                pr.emit("tensor", lambda e, src=den_src, st0=st0, stop=stop:
                        e.matmul(P["DEN"][0:1, :], A["ONESB"][:, :],
                                 src[:, 0:512], start=st0, stop=stop,
                                 tile_position=(0, 0)))
                pr.emit("tensor", lambda e, src=den_src, st0=st0, stop=stop:
                        e.matmul(P["DEN"][64:65, :], A["ONESB"][:, :],
                                 src[:, 512:1024], start=st0, stop=stop,
                                 tile_position=(0, 64)), inc=("pe", 1))
                pv_marks[j] = pr.val("pe")
                state["eb_last"][j % 2] = pv_marks[j]
                if regions[j] == "diag":
                    state["em_last"][j % 2] = pv_marks[j]

            for i, kt in enumerate(kts):
                SSa = P["B0"] if i % 2 == 0 else P["B2"]
                SSb = P["B1"] if i % 2 == 0 else P["B3"]
                EB = A["EB0"] if i % 2 == 0 else A["EB1"]
                pr.wait("tensor", "act", state["ss_last"][i % 2])
                pr.emit("tensor", lambda e, SSa=SSa, kt=kt, qc=qc, QS=QS: e.matmul(
                    SSa[:, :], A["KP"][0:64, _sl(kt * 128, 128)],
                    QS[0:64, _sl(qc * 512, 512)], start=True, stop=True,
                    tile_position=(0, 0)))
                pr.emit("tensor", lambda e, SSb=SSb, kt=kt, qc=qc, QS=QS: e.matmul(
                    SSb[:, :], A["KP"][64:128, _sl(kt * 128, 128)],
                    QS[64:128, _sl(qc * 512, 512)], start=True, stop=True,
                    tile_position=(64, 0)), inc=("pe", 1))
                qk_marks[i] = pr.val("pe")
                pr.wait("scalar", "pe", qk_marks[i])
                pr.wait("scalar", "pe", state["eb_last"][i % 2])
                pr.emit("scalar", lambda e, SSa=SSa, EB=EB: e.activation(
                    EB[:, 0:512], SSa[:, :], AF.Exp, scale=A["SCL"][:, scale_col]))
                pr.emit("scalar", lambda e, SSb=SSb, EB=EB: e.activation(
                    EB[:, 512:1024], SSb[:, :], AF.Exp, scale=A["SCL"][:, scale_col]),
                    inc=("act", 1))
                exp_marks[i] = pr.val("act")
                state["ss_last"][i % 2] = exp_marks[i]
                if regions[i] == "diag":
                    o = kt - 4 * qc
                    eh = _sl((i % 2) * 1024, 1024)
                    pr.wait("vector", "act", exp_marks[i])
                    pr.wait("vector", "pe", state["em_last"][i % 2])
                    pr.emit("vector", lambda e, EB=EB, o=o, eh=eh: e.tensor_tensor(
                        A["EM"][:, eh][:, 0:512], EB[:, 0:512],
                        A["MSK"][:, _sl(o * 512, 512)], ALU.mult))
                    pr.emit("vector", lambda e, EB=EB, o=o, eh=eh: e.tensor_tensor(
                        A["EM"][:, eh][:, 512:1024], EB[:, 512:1024],
                        A["MSK"][:, _sl(o * 512, 512)], ALU.mult), inc=("dve", 1))
                    msk_marks[i] = pr.val("dve")
                    state["em_last"][i % 2] = 0
                if i >= 1:
                    emit_pv(i - 1)
            emit_pv(n - 1)
            pr.wait("vector", "pe", pv_marks[n - 1])
            pr.emit("vector", lambda e: e.tensor_copy(A["PSTG"][:, 0:512],
                                                      P["ACC"][:, :]))
            pr.emit("vector", lambda e: e.reciprocal(A["RRX"][0:1, :],
                                                     P["DEN"][0:1, :]))
            pr.emit("vector", lambda e: e.reciprocal(A["RRX"][64:65, :],
                                                     P["DEN"][64:65, :]),
                    inc=("dve", 1))
            state["acfree"] = pr.val("dve")
            pr.wait("gpsimd", "dve", state["acfree"])
            pr.wait("gpsimd", "ldw", pr.val("ldw"))
            pr.emit("gpsimd", lambda e: e.dma_start(
                out=tn["rowb"].ap()[0:1, 0:512], in_=A["RRX"][0:1, :]),
                inc=("st", 16))
            pr.emit("gpsimd", lambda e: e.dma_start(
                out=tn["rowb"].ap()[0:1, 512:1024], in_=A["RRX"][64:65, :]),
                inc=("st", 16))
            stm = pr.val("st")
            pr.wait("sync", "st", stm)
            pr.emit("sync", lambda e: e.dma_start(
                out=A["RB"][:, :], in_=tn["rowb"].ap()[0:1, :].broadcast_to([128, 1024])),
                inc=("ldw", 16))
            rbm = pr.val("ldw")
            pr.wait("vector", "ldw", rbm)
            pr.emit("vector", lambda e, p_=p_, qc=qc: e.tensor_tensor(
                A["Z"][0:64, _sl(p_ * S + qc * 512, 512)], A["PSTG"][0:64, 0:512],
                A["RB"][0:64, 0:512], ALU.mult))
            pr.emit("vector", lambda e, p_=p_, qc=qc: e.tensor_tensor(
                A["Z"][64:128, _sl(p_ * S + qc * 512, 512)], A["PSTG"][64:128, 0:512],
                A["RB"][64:128, 512:1024], ALU.mult), inc=("dve", 1))
        zm = pr.val("dve")
        pr.wait("gpsimd", "dve", zm)
        pr.emit("gpsimd", lambda e, p_=p_: e.dma_start(
            out=tn["g3_src"].ap()[_sl(p_ * 128, 128), :],
            in_=A["Z"][:, _sl(p_ * S, S)]), inc=("st", 16))


def emit_l0f(ctx):
    pr, nc, tn, A, P = ctx["pr"], ctx["nc"], ctx["tn"], ctx["A"], ctx["P"]
    emit_ln(ctx, "X", F32, "pre")
    pr.wait("tensor", "dve", pr.val("dve"))

    pr.emit("gpsimd", lambda e: e.dma_start(
        out=A["WGS"].ap().rearrange("p (c e) -> p c e", e=E8),
        in_=tn["wgt"].ap().rearrange("(c p) e -> p c e", p=128)),
        inc=("ldg", 16))
    pr.emit("gpsimd", lambda e: e.dma_start(out=A["B2S"][:, :], in_=tn["b2"].ap()),
            inc=("ldg", 16))
    pr.emit("sync", lambda e: e.dma_start(out=A["bgs"][:, :], in_=tn["bg"].ap()),
            inc=("ldw", 16))
    pr.emit("sync", lambda e: e.dma_start(out=A["IDS"][:, :], in_=tn["ident"].ap()),
            inc=("ldw", 16))
    pr.wait("tensor", "ldg", pr.val("ldg"))
    pr.wait("tensor", "ldw", pr.val("ldw"))
    pr.wait("vector", "ldw", pr.val("ldw"))

    for qc in range(4):
        pr.wait("tensor", "dve", pr.val("dve"))
        for dc in range(DC):
            pr.emit("tensor", lambda e, dc=dc, qc=qc: e.matmul(
                P["PST"][:, :], A["WGS"][:, _sl(dc * 8, 8)],
                A["LNX"][:, _sl(dc * T + qc * 512, 512)],
                start=(dc == 0), stop=(dc == DC - 1)),
                inc=("pe", 1) if dc == DC - 1 else None)
        pr.wait("vector", "pe", pr.val("pe"))
        pr.emit("vector", lambda e, qc=qc: e.tensor_scalar(
            A["GS"][:, _sl(qc * 512, 512)], P["PST"][0:8, :], A["bgs"][:, 0:1],
            None, ALU.add), inc=("dve", 1))
    pr.wait("tensor", "dve", pr.val("dve"))
    for tt in range(16):
        pr.emit("tensor", lambda e, tt=tt: e.matmul(
            P["PTR"][:, _sl(tt * 8, 8)], A["GS"][:, _sl(tt * 128, 128)],
            A["IDS"][0:8, 0:8], start=True, stop=True),
            inc=("pe", 1) if tt == 15 else None)
    pr.wait("vector", "pe", pr.val("pe"))

    def vstep(fn):
        pr.emit("vector", fn, inc=("dve", 1))
        pr.wait("vector", "dve", pr.val("dve"))

    def seg(t, w=8):
        return t.ap().rearrange("p (s e) -> p s e", e=w)

    def segb(t, c0):
        v = t.ap()[:, _sl(c0, 16)].rearrange("p (s o) -> p s o", o=1)
        return v.broadcast_to([128, 16, 8])

    vstep(lambda e: e.tensor_copy(A["GTA"][:, :], P["PTR"][:, :]))
    vstep(lambda e: e.reduce_max(A["REDA"][:, 0:16], seg(A["GTA"]), AX.X))
    vstep(lambda e: e.tensor_tensor(seg(A["GEB"]), seg(A["GTA"]),
                                    segb(A["REDA"], 0), ALU.subtract))
    am0 = pr.val("dve")
    pr.wait("scalar", "dve", am0)
    pr.emit("scalar", lambda e: e.activation(A["GEC"][:, :], A["GEB"][:, :], AF.Exp),
            inc=("act", 1))
    pr.wait("vector", "act", pr.val("act"))
    vstep(lambda e: e.reduce_sum(A["REDA"][:, 16:32], seg(A["GEC"]), AX.X))
    vstep(lambda e: e.reciprocal(A["REDA"][:, 32:48], A["REDA"][:, 16:32]))
    vstep(lambda e: e.tensor_tensor(seg(A["GEC"]), seg(A["GEC"]),
                                    segb(A["REDA"], 32), ALU.mult))
    vstep(lambda e: e.reduce_max(A["REDA"][:, 0:16], seg(A["GEC"]), AX.X))
    vstep(lambda e: e.tensor_tensor(seg(A["GEB"]), seg(A["GEC"]),
                                    segb(A["REDA"], 0), ALU.is_equal))
    vstep(lambda e: e.scalar_tensor_tensor(A["GEB"][:, :], A["GEB"][:, :], -4.0,
                                           A["GEC"][:, :], ALU.mult, ALU.add))
    vstep(lambda e: e.reduce_max(A["REDA"][:, 16:32], seg(A["GEB"]), AX.X))
    vstep(lambda e: e.tensor_tensor(seg(A["GEB"]), seg(A["GEC"]),
                                    segb(A["REDA"], 16), ALU.is_ge))
    vstep(lambda e: e.tensor_tensor(A["GTA"][:, :], A["GEC"][:, :], A["GEB"][:, :],
                                    ALU.mult))
    for tt in range(16):
        pr.wait("tensor", "dve", pr.val("dve"))
        pr.emit("tensor", lambda e, tt=tt: e.matmul(
            P["PTR"][0:8, 0:128], A["GTA"][:, _sl(tt * 8, 8)], A["IDS"][:, :],
            start=True, stop=True), inc=("pe", 1))
        pr.wait("vector", "pe", pr.val("pe"))
        pr.emit("vector", lambda e, tt=tt: e.tensor_copy(
            A["W8B"][:, _sl(tt * 128, 128)], P["PTR"][0:8, 0:128]), inc=("dve", 1))
    wm = pr.val("dve")
    pr.wait("gpsimd", "dve", wm)
    pr.emit("gpsimd", lambda e: e.dma_start(out=tn["w8d"].ap(), in_=A["W8B"][:, :]),
            inc=("st", 16))
    w8st = pr.val("st")
    pr.wait("tensor", "dve", wm)

    marks = []
    for dc in range(DC):
        for qc in range(4):
            half = (dc * 4 + qc) % 4
            ps = [P["B0"][:, :], P["B1"][:, :], P["B2"][:, :],
                  P["B3"][:, :]][half]
            if len(marks) >= 4:
                pr.wait("tensor", "dve", marks[-4])
            pr.emit("tensor", lambda e, ps=ps, dc=dc, qc=qc: e.matmul(
                ps, A["B2S"][:, _sl(dc * 128, 128)], A["W8B"][:, _sl(qc * 512, 512)],
                start=True, stop=True), inc=("pe", 1))
            pr.wait("vector", "pe", pr.val("pe"))
            pr.emit("vector", lambda e, ps=ps, dc=dc, qc=qc: e.tensor_copy(
                A["Z"][:, _sl(dc * T + qc * 512, 512)], ps), inc=("dve", 1))
            marks.append(pr.val("dve"))

    w1_marks = [None, None, None, None]
    w2_marks = [None, None]
    wbc_marks = [None, None]
    b1_mark = None
    for ex in range(E8):
        pr.wait("sync", "hev", pr.val("hev"))
        pr.emit("sync", lambda e, ex=ex: e.dma_start(
            out=A["b1s"][:, :], in_=tn["b1t"].ap()[ex]), inc=("ldw", 16))
        b1m = pr.val("ldw")
        pr.wait("vector", "ldw", b1m)
        pr.wait("scalar", "ldw", b1m)
        for tc in range(4):
            tb = tc % 2
            if wbc_marks[tb] is not None:
                pr.wait("gpsimd", "zev", wbc_marks[tb])
            pr.wait("gpsimd", "st", w8st)
            pr.emit("gpsimd", lambda e, ex=ex, tc=tc, tb=tb: e.dma_start(
                out=A["WBC"][:, _sl(tb * 512, 512)],
                in_=tn["w8d"].ap()[_sl(ex, 1), _sl(tc * 512, 512)].broadcast_to(
                    [128, 512])), inc=("ldg", 16))
            wbcm = pr.val("ldg")
            pr.wait("tensor", "dve", pr.val("dve"))
            pr.wait("tensor", "zev", pr.val("zev"))
            if ex == 0 and tc == 0:
                pr.wait("scalar", "st", w8st)
                pr.wait("vector", "st", w8st)
            hmarks = []
            for fc in range(32):
                wb = fc % 4
                if w1_marks[wb] is not None:
                    pr.wait("gpsimd", "pe", w1_marks[wb])
                pr.emit("gpsimd", lambda e, ex=ex, fc=fc, wb=wb: e.dma_start(
                    out=A["W1B"][:, _sl(wb * 1024, 1024)].rearrange(
                        "p (c f) -> p c f", f=128),
                    in_=tn["w1t"].ap()[ex, fc].rearrange("(c p) f -> p c f", p=128)),
                    inc=("ldg", 16))
                ldm = pr.val("ldg")
                half = fc % 4
                ps = [P["B0"][:, :], P["B1"][:, :], P["B2"][:, :],
                      P["B3"][:, :]][half]
                pr.wait("tensor", "ldg", ldm)
                if len(hmarks) >= 4:
                    pr.wait("tensor", "hev", hmarks[-4])
                for dc in range(DC):
                    pr.emit("tensor", lambda e, ps=ps, wb=wb, dc=dc, tc=tc: e.matmul(
                        ps, A["W1B"][:, _sl(wb * 1024 + dc * 128, 128)],
                        A["LNX"][:, _sl(dc * T + tc * 512, 512)],
                        start=(dc == 0), stop=(dc == DC - 1)),
                        inc=("pe", 1) if dc == DC - 1 else None)
                w1_marks[wb] = pr.val("pe")
                pm = pr.val("pe")
                if fc % 2 == 0:
                    pr.wait("scalar", "pe", pm)
                    pr.emit("scalar", lambda e, ps=ps, fc=fc: e.activation(
                        A["HSL"][:, _sl(fc * 512, 512)], ps, AF.Relu,
                        bias=A["b1s"][:, _sl(fc, 1)]), inc=("hev", 1))
                else:
                    pr.wait("vector", "pe", pm)
                    pr.emit("vector", lambda e, ps=ps, fc=fc: e.tensor_scalar(
                        A["HSL"][:, _sl(fc * 512, 512)], ps, A["b1s"][:, _sl(fc, 1)],
                        0.0, ALU.add, ALU.max), inc=("hev", 1))
                hmarks.append(pr.val("hev"))
            hall = pr.val("hev")
            pr.wait("tensor", "hev", hall)
            for dc in range(DC):
                wb2 = dc % 2
                if w2_marks[wb2] is not None:
                    pr.wait("gpsimd", "pe", w2_marks[wb2])
                pr.emit("gpsimd", lambda e, ex=ex, dc=dc, wb2=wb2: e.dma_start(
                    out=A["W2B"][:, _sl(wb2 * F1, F1)].rearrange(
                        "p (c f) -> p c f", f=128),
                    in_=tn["w2t"].ap()[ex, dc].rearrange("(c p) f -> p c f", p=128)),
                    inc=("ldg", 16))
                ldm = pr.val("ldg")
                half = dc % 4
                ps = [P["B0"][:, :], P["B1"][:, :], P["B2"][:, :],
                      P["B3"][:, :]][half]
                pr.wait("tensor", "ldg", ldm)
                pr.wait("tensor", "zev", max(0, pr.val("zev") - 3))
                for fc in range(32):
                    pr.emit("tensor", lambda e, ps=ps, wb2=wb2, fc=fc: e.matmul(
                        ps, A["W2B"][:, _sl(wb2 * F1 + fc * 128, 128)],
                        A["HSL"][:, _sl(fc * 512, 512)],
                        start=(fc == 0), stop=(fc == 31)),
                        inc=("pe", 1) if fc == 31 else None)
                w2_marks[wb2] = pr.val("pe")
                pm = pr.val("pe")
                pr.wait("vector", "pe", pm)
                pr.wait("vector", "ldg", wbcm)
                pr.emit("vector", lambda e, ps=ps, tb=tb: e.tensor_tensor(
                    A["TMI"][:, :], ps, A["WBC"][:, _sl(tb * 512, 512)], ALU.mult))
                pr.emit("vector", lambda e, dc=dc, tc=tc: e.tensor_tensor(
                    A["Z"][:, _sl(dc * T + tc * 512, 512)],
                    A["Z"][:, _sl(dc * T + tc * 512, 512)], A["TMI"][:, :], ALU.add),
                    inc=("zev", 1))
            wbc_marks[tb] = pr.val("zev")

    emit_ln(ctx, "Z", BF16, "res")


def emit_l1a(ctx):
    pr, nc, tn, A, P = ctx["pr"], ctx["nc"], ctx["tn"], ctx["A"], ctx["P"]
    pr.emit("sync", lambda e: e.dma_start(out=A["MSK"][:, :], in_=tn["mask1"].ap()),
            inc=("ldw", 16))

    emit_ln(ctx, "X", F32, "pre")
    pr.wait("tensor", "dve", pr.val("dve"))
    if DEBUG_PHASE < 2:
        return
    pr.wait("sync", "dve", pr.val("dve"))
    pr.emit("sync", lambda e: e.dma_start(out=A["COS"][:, :], in_=tn["cos4"].ap()),
            inc=("ldw", 16))
    pr.emit("sync", lambda e: e.dma_start(out=A["SIN"][:, :], in_=tn["sin4"].ap()),
            inc=("ldw", 16))
    pr.wait("vector", "ldw", pr.val("ldw"))

    stl = {"last": None}

    pr.wait("gpsimd", "pe", pr.val("pe"))
    for wi, wname in enumerate(("wk1a", "wk1b")):
        pr.emit("gpsimd", lambda e, wi=wi, wname=wname: e.dma_start(
            out=A["WB0"][:, _sl(wi * 512, 512)].rearrange("p (c f) -> p c f", f=64),
            in_=tn[wname].ap().rearrange("(c p) f -> p c f", p=128)),
            inc=("ldg", 16))
    pr.wait("tensor", "ldg", pr.val("ldg"))
    pr.wait("tensor", "dve", pr.val("dve"))
    for wi in range(2):
        DST = A["KTA"] if wi == 0 else A["KTB"]
        for qc in range(4):
            ps = P["B0"][:, :] if qc % 2 == 0 else P["B2"][:, :]
            pr.wait("tensor", "dve", max(0, pr.val("dve") - 1))
            for dc in range(DC):
                pr.emit("tensor", lambda e, ps=ps, wi=wi, dc=dc, qc=qc: e.matmul(
                    ps[0:64, :], A["WB0"][:, _sl(wi * 512 + dc * 64, 64)],
                    A["LNX"][:, _sl(dc * T + qc * 512, 512)],
                    start=(dc == 0), stop=(dc == DC - 1)),
                    inc=("pe", 1) if dc == DC - 1 else None)
            pr.wait("vector", "pe", pr.val("pe"))
            pr.emit("vector", lambda e, ps=ps, DST=DST, qc=qc: e.tensor_copy(
                DST[:, _sl(qc * 512, 512)], ps[0:64, :]), inc=("dve", 1))
    pr.emit("vector", lambda e: e.tensor_tensor(
        A["WRKA"][0:64, :], A["KTA"][:, :], A["COS"][0:64, :], ALU.mult))
    pr.emit("vector", lambda e: e.tensor_tensor(
        A["WRKB"][0:64, :], A["KTB"][:, :], A["SIN"][0:64, :], ALU.mult))
    pr.emit("vector", lambda e: e.tensor_tensor(
        A["STG"][0:64, :], A["WRKA"][0:64, :], A["WRKB"][0:64, :], ALU.add),
        inc=("dve", 1))
    km = pr.val("dve")
    pr.wait("gpsimd", "dve", km)
    pr.emit("gpsimd", lambda e: e.dma_start(
        out=tn["g1_src"].ap()[_sl(1024, 64), :], in_=A["STG"][0:64, :]),
        inc=("st", 16))
    stl["last"] = pr.val("st")
    if DEBUG_PHASE < 3:
        return

    pr.wait("gpsimd", "pe", pr.val("pe"))
    pr.wait("gpsimd", "dve", pr.val("dve"))
    pr.emit("gpsimd", lambda e: e.dma_start(
        out=A["WVs"].ap().rearrange("p (c f) -> p c f", f=64),
        in_=tn["wv1"].ap().rearrange("(c p) f -> p c f", p=128)),
        inc=("ldg", 16))
    pr.wait("tensor", "ldg", pr.val("ldg"))
    pr.wait("tensor", "dve", pr.val("dve"))
    for tt in range(16):
        ps = P["B0"][:, :] if tt % 2 == 0 else P["B2"][:, :]
        pr.wait("tensor", "dve", max(0, pr.val("dve") - 1))
        for dc in range(DC):
            pr.emit("tensor", lambda e, ps=ps, dc=dc, tt=tt: e.matmul(
                ps[:, 0:64], A["LNX"][:, _sl(dc * T + tt * 128, 128)],
                A["WVs"][:, _sl(dc * 64, 64)], start=(dc == 0), stop=(dc == DC - 1)),
                inc=("pe", 1) if dc == DC - 1 else None)
        pr.wait("vector", "pe", pr.val("pe"))
        if stl["last"] is not None:
            pr.wait("vector", "st", stl["last"])
        pr.emit("vector", lambda e, ps=ps: e.tensor_copy(A["STG"][:, 0:64], ps[:, 0:64]),
                inc=("dve", 1))
        vm = pr.val("dve")
        pr.wait("gpsimd", "dve", vm)
        pr.emit("gpsimd", lambda e, tt=tt: e.dma_start(
            out=tn["g2_src"].ap()[_sl(tt * 128, 128), 0:64], in_=A["STG"][:, 0:64]),
            inc=("st", 16))
        stl["last"] = pr.val("st")
    pr.wait("gpsimd", "st", pr.val("st"))
    pr.emit("gpsimd", lambda e: e.collective_compute(
        "AllGather", ALU.bypass, replica_groups=GROUPS,
        ins=[tn["g2_src"].ap()], outs=[tn["g2_out"].ap()]), inc=("cc", 1))

    if DEBUG_PHASE < 4:
        return
    pr.wait("gpsimd", "pe", pr.val("pe"))
    for dc in range(DC):
        pr.emit("gpsimd", lambda e, dc=dc: e.dma_start(
            out=A["WB0"][:, _sl(dc * D, D)],
            in_=tn["wqT1a"].ap()[_sl(dc * 128, 128), :]), inc=("ldg", 16))
    qa_m = pr.val("ldg")
    pr.wait("tensor", "ldg", qa_m)
    pr.wait("tensor", "dve", pr.val("dve"))
    marks = []
    for oc in range(8):
        for qc in range(4):
            half = (oc * 4 + qc) % 4
            ps = [P["B0"][:, :], P["B1"][:, :], P["B2"][:, :],
                  P["B3"][:, :]][half]
            if len(marks) >= 4:
                pr.wait("tensor", "dve", marks[-4])
            for dc in range(DC):
                pr.emit("tensor", lambda e, ps=ps, dc=dc, oc=oc, qc=qc: e.matmul(
                    ps, A["WB0"][:, _sl(dc * D + oc * 128, 128)],
                    A["LNX"][:, _sl(dc * T + qc * 512, 512)],
                    start=(dc == 0), stop=(dc == DC - 1)),
                    inc=("pe", 1) if dc == DC - 1 else None)
            pr.wait("vector", "pe", pr.val("pe"))
            pr.emit("vector", lambda e, ps=ps, oc=oc, qc=qc: e.tensor_copy(
                A["Z"][:, _sl(oc * T + qc * 512, 512)], ps), inc=("dve", 1))
            marks.append(pr.val("dve"))
    qa_all = pr.val("dve")
    pr.wait("gpsimd", "pe", pr.val("pe"))
    for dc in range(DC):
        pr.emit("gpsimd", lambda e, dc=dc: e.dma_start(
            out=A["WB0"][:, _sl(dc * D, D)],
            in_=tn["wqT1b"].ap()[_sl(dc * 128, 128), :]), inc=("ldg", 16))
    pr.wait("tensor", "ldg", pr.val("ldg"))
    pr.wait("tensor", "dve", pr.val("dve"))
    for oc in range(8):
        for qc in range(4):
            half = (oc * 4 + qc) % 4
            ps = [P["B0"][:, :], P["B1"][:, :], P["B2"][:, :],
                  P["B3"][:, :]][half]
            if len(marks) >= 4:
                pr.wait("tensor", "dve", marks[-4])
            for dc in range(DC):
                pr.emit("tensor", lambda e, ps=ps, dc=dc, oc=oc, qc=qc: e.matmul(
                    ps, A["WB0"][:, _sl(dc * D + oc * 128, 128)],
                    A["LNX"][:, _sl(dc * T + qc * 512, 512)],
                    start=(dc == 0), stop=(dc == DC - 1)),
                    inc=("pe", 1) if dc == DC - 1 else None)
            pr.wait("vector", "pe", pr.val("pe"))
            if qc == 0 and stl["last"] is not None:
                pr.wait("vector", "st", stl["last"])
            pr.emit("vector", lambda e, ps=ps, qc=qc: e.tensor_copy(
                A["STG"][:, _sl(qc * 512, 512)], ps), inc=("dve", 1))
            marks.append(pr.val("dve"))
        pr.emit("vector", lambda e, oc=oc: e.tensor_tensor(
            A["WRKA"][:, :], A["Z"][:, _sl(oc * T, T)], A["COS"][:, :], ALU.mult))
        pr.emit("vector", lambda e: e.tensor_tensor(
            A["WRKB"][:, :], A["STG"][:, :], A["SIN"][:, :], ALU.mult))
        if stl["last"] is not None:
            pr.wait("vector", "st", stl["last"])
        pr.emit("vector", lambda e: e.tensor_tensor(
            A["STG"][:, :], A["WRKA"][:, :], A["WRKB"][:, :], ALU.add), inc=("dve", 1))
        qm = pr.val("dve")
        pr.wait("gpsimd", "dve", qm)
        pr.emit("gpsimd", lambda e, oc=oc: e.dma_start(
            out=tn["g1_src"].ap()[_sl(oc * 128, 128), :], in_=A["STG"][:, :]),
            inc=("st", 16))
        stl["last"] = pr.val("st")
    pr.wait("gpsimd", "st", pr.val("st"))
    pr.emit("gpsimd", lambda e: e.collective_compute(
        "AllGather", ALU.bypass, replica_groups=GROUPS,
        ins=[tn["g1_src"].ap()], outs=[tn["g1_out"].ap()]), inc=("cc", 1))

    if DEBUG_PHASE < 5:
        return
    emit_attn(ctx, layer=1)
    if DEBUG_PHASE < 6:
        return

    pr.wait("gpsimd", "st", pr.val("st"))
    pr.emit("gpsimd", lambda e: e.collective_compute(
        "AllGather", ALU.bypass, replica_groups=GROUPS,
        ins=[tn["g3_src"].ap()], outs=[tn["g3_out"].ap()]), inc=("cc", 1))
    ccz = pr.val("cc")
    pr.wait("sync", "cc", ccz)
    pr.wait("sync", "dve", pr.val("dve"))
    def ldz1(e):
        c0 = ctx["get_sym"](e, "tcol")
        src = tn["g3_out"].ap().rearrange("(c p) t -> p c t", p=128)
        src = src[:, :, bass.ds(c0, T)]
        return e.dma_start(out=A["Z"].ap().rearrange("p (c t) -> p c t", t=T),
                           in_=src)
    pr.emit("sync", ldz1, inc=("ldw", 16))
    zm = pr.val("ldw")
    pr.wait("tensor", "ldw", zm)
    pr.wait("vector", "ldw", zm)
    def ocb(wi, oc, qc, ps):
        pr.emit("vector", lambda e, ps=ps, oc=oc, qc=qc: e.tensor_copy(
            A["LNX"][:, _sl(oc * T + qc * 512, 512)], ps), inc=("dve", 1))
    emit_proj(ctx, ["woT"], D, ocb, rhs_slab="Z")
    emit_ln(ctx, "LNX", BF16, "res")


def emit_l1f(ctx):
    pr, nc, tn, A, P = ctx["pr"], ctx["nc"], ctx["tn"], ctx["A"], ctx["P"]
    emit_ln(ctx, "X", F32, "pre")
    pr.wait("tensor", "dve", pr.val("dve"))

    cast_marks = {}
    def cast_load(idx, src_ap, ncols, dst, dmark):
        if dmark is not None:
            pr.wait("sync", "pe", dmark)
        pr.wait("sync", "cst", cast_marks.get("wi", 0))
        pr.emit("sync", lambda e, src_ap=src_ap, ncols=ncols: e.dma_start(
            out=A["WI"][:, 0:ncols].rearrange("p (c f) -> p c f", f=128),
            in_=src_ap), inc=("ldw", 16))
        lm = pr.val("ldw")
        pr.wait("vector", "ldw", lm)
        pr.emit("vector", lambda e, ncols=ncols, dst=dst: e.tensor_scalar(
            dst[:, 0:ncols], A["WI"][:, 0:ncols], Q127, None, ALU.mult),
            inc=("cst", 1))
        cast_marks["wi"] = pr.val("cst")
        return pr.val("cst")

    wf_mark = [None, None]
    for tc in range(2):
        for hc in range(24):
            m0 = cast_load(0, tn["wgT1"].ap()[hc].rearrange("(c p) f -> p c f", p=128),
                           1024, A["WF0"], wf_mark[0])
            m1 = cast_load(1, tn["wuT1"].ap()[hc].rearrange("(c p) f -> p c f", p=128),
                           1024, A["WF1"], wf_mark[1])
            pr.wait("tensor", "cst", m1)
            for tq in range(2):
                psg = [P["B0"], P["B1"]][tq][:, :]
                psu = [P["B2"], P["B3"]][tq][:, :]
                pr.wait("tensor", "hev", max(0, pr.val("hev") - 1))
                for dc in range(DC):
                    rhs = A["LNX"][:, _sl(dc * T + tc * 1024 + tq * 512, 512)]
                    pr.emit("tensor", lambda e, psg=psg, dc=dc, rhs=rhs: e.matmul(
                        psg, A["WF0"][:, _sl(dc * 128, 128)], rhs,
                        start=(dc == 0), stop=(dc == DC - 1)))
                for dc in range(DC):
                    rhs = A["LNX"][:, _sl(dc * T + tc * 1024 + tq * 512, 512)]
                    pr.emit("tensor", lambda e, psu=psu, dc=dc, rhs=rhs: e.matmul(
                        psu, A["WF1"][:, _sl(dc * 128, 128)], rhs,
                        start=(dc == 0), stop=(dc == DC - 1)),
                        inc=("pe", 1) if dc == DC - 1 else None)
                pm = pr.val("pe")
                pr.wait("scalar", "pe", pm)
                pr.emit("scalar", lambda e, psg=psg: e.activation(
                    A["SIL"][:, :], psg, AF.Silu), inc=("act", 1))
                pr.wait("vector", "act", pr.val("act"))
                pr.emit("vector", lambda e, psu=psu, hc=hc, tq=tq: e.tensor_tensor(
                    A["HSL1"][:, _sl(hc * 1024 + tq * 512, 512)], A["SIL"][:, :],
                    psu, ALU.mult), inc=("hev", 1))
            wf_mark[0] = pr.val("pe")
            wf_mark[1] = pr.val("pe")
        hall = pr.val("hev")
        pr.wait("tensor", "hev", hall)
        wd_mark = None
        for dc in range(DC):
            md = cast_load(2, tn["wdT1"].ap()[dc].rearrange("(c p) f -> p c f", p=128),
                           3072, A["WF0"], wd_mark)
            pr.wait("tensor", "cst", md)
            for tq in range(2):
                ps = [P["B0"], P["B1"]][tq][:, :]
                pr.wait("tensor", "zev", max(0, pr.val("zev") - 1))
                for hc in range(24):
                    pr.emit("tensor", lambda e, ps=ps, hc=hc, tq=tq: e.matmul(
                        ps, A["WF0"][:, _sl(hc * 128, 128)],
                        A["HSL1"][:, _sl(hc * 1024 + tq * 512, 512)],
                        start=(hc == 0), stop=(hc == 23)),
                        inc=("pe", 1) if hc == 23 else None)
                pm = pr.val("pe")
                pr.wait("vector", "pe", pm)
                pr.emit("vector", lambda e, ps=ps, dc=dc, tc=tc, tq=tq: e.tensor_copy(
                    A["Z"][:, _sl(dc * T + tc * 1024 + tq * 512, 512)], ps),
                    inc=("zev", 1))
            wd_mark = pr.val("pe")
        wf_mark[0] = pr.val("pe")

    emit_ln(ctx, "Z", BF16, "res")


def _prep(inputs):
    pc = {}
    x = np.asarray(inputs["L0_attn_embedded"], dtype=np.float32)
    pc["x_in"] = [np.ascontiguousarray(x[c // 4, (c % 4) * T:(c % 4 + 1) * T, :].T)
                  for c in range(NCORE)]

    def rep(a):
        a = np.ascontiguousarray(a)
        return [a] * NCORE

    pc["wqT"] = rep(np.asarray(inputs["L0_attn_Wq"], np.float32).T)
    pc["wkT"] = rep(np.asarray(inputs["L0_attn_Wk"], np.float32).T)
    pc["wvT"] = rep(np.asarray(inputs["L0_attn_Wv"], np.float32).T)
    bq = np.asarray(inputs["L0_attn_bq"], np.float32).reshape(8, 128).T
    bk = np.asarray(inputs["L0_attn_bk"], np.float32).reshape(8, 128).T
    bv = np.asarray(inputs["L0_attn_bv"], np.float32).reshape(8, 128).T
    pc["bqkv"] = rep(np.concatenate([bq, bk, bv], axis=1))
    ga = np.asarray(inputs["L0_attn_gamma"], np.float32).reshape(8, 128).T
    be = np.asarray(inputs["L0_attn_beta"], np.float32).reshape(8, 128).T
    pc["gb"] = rep(np.concatenate([ga, be], axis=1))
    pc["bvrow"] = rep(np.asarray(inputs["L0_attn_bv"], np.float32).reshape(1, D))

    pc["wgt"] = rep(np.asarray(inputs["L0_ffn_Wg"], np.float32))
    pc["bg"] = rep(np.asarray(inputs["L0_ffn_bg"], np.float32).reshape(8, 1))
    w1 = np.asarray(inputs["L0_ffn_W1"], np.float32)
    pc["w1t"] = rep(np.transpose(w1.reshape(E8, D, 32, 128), (0, 2, 1, 3)))
    b1 = np.asarray(inputs["L0_ffn_b1"], np.float32)
    pc["b1t"] = rep(np.transpose(b1.reshape(E8, 32, 128), (0, 2, 1)))
    w2 = np.asarray(inputs["L0_ffn_W2"], np.float32)
    pc["w2t"] = rep(np.transpose(w2.reshape(E8, F1, DC, 128), (0, 2, 1, 3)))
    pc["b2"] = rep(np.asarray(inputs["L0_ffn_b2"], np.float32))

    perm_a = np.concatenate([np.concatenate([64 * h + np.arange(0, 64, 2),
                                             64 * h + np.arange(1, 64, 2)])
                             for h in range(H)])
    perm_b = np.concatenate([np.concatenate([64 * h + np.arange(1, 64, 2),
                                             64 * h + np.arange(0, 64, 2)])
                             for h in range(H)])
    wq1 = np.asarray(inputs["L1_attn_wq"], np.float32).T
    pc["wqT1a"] = rep(wq1[:, perm_a])
    pc["wqT1b"] = rep(wq1[:, perm_b])
    wk1 = np.asarray(inputs["L1_attn_wk"], np.float32)
    pa64 = np.concatenate([np.arange(0, 64, 2), np.arange(1, 64, 2)])
    pb64 = np.concatenate([np.arange(1, 64, 2), np.arange(0, 64, 2)])
    pc["wk1a"] = rep(wk1[:, pa64])
    pc["wk1b"] = rep(wk1[:, pb64])
    pc["wv1"] = rep(np.asarray(inputs["L1_attn_wv"], np.float32))
    pc["woT"] = rep(np.asarray(inputs["L1_attn_wo"], np.float32).T)
    freqs = 1.0 / (1000.0 ** (np.arange(0, HD, 2, dtype=np.float64) / HD))
    cos_l, sin_l = [], []
    for c in range(NCORE):
        posv = np.arange((c % 4) * T, (c % 4 + 1) * T, dtype=np.float64)
        ang = freqs[:, None] * posv[None, :]
        co = np.cos(ang).astype(np.float32)
        si = np.sin(ang).astype(np.float32)
        cos4 = np.tile(co, (4, 1))
        sin4 = np.concatenate([-si, si, -si, si], axis=0)
        cos_l.append(np.ascontiguousarray(cos4))
        sin_l.append(np.ascontiguousarray(sin4))
    pc["cos4"] = cos_l
    pc["sin4"] = sin_l

    wgq = np.asarray(inputs["L1_ffn_w_gate"], np.int32)
    wuq = np.asarray(inputs["L1_ffn_w_up"], np.int32)
    wdq = np.asarray(inputs["L1_ffn_w_down"], np.int32)
    pc["wgT1"] = rep(np.ascontiguousarray(
        np.transpose(wgq.T.reshape(D, 24, 128), (1, 0, 2))))
    pc["wuT1"] = rep(np.ascontiguousarray(
        np.transpose(wuq.T.reshape(D, 24, 128), (1, 0, 2))))
    pc["wdT1"] = rep(np.ascontiguousarray(
        np.transpose(wdq.T.reshape(H3, DC, 128), (1, 0, 2))))
    return pc


_IN_NAMES = {
    "l0a": ["wqT", "wkT", "wvT", "bqkv", "gb", "bvrow"],
    "l0f": ["wgt", "bg", "w1t", "b1t", "w2t", "b2"],
    "l1a": ["wqT1a", "wqT1b", "wk1a", "wk1b", "wv1", "woT", "cos4", "sin4"],
    "l1f": ["wgT1", "wuT1", "wdT1"],
}


def kernel(**inputs):
    pc = _prep(inputs)
    xcur = pc["x_in"]
    for group in LAUNCH_PLAN:
        nc = build_program(group)
        in_maps = []
        for c in range(NCORE):
            m = {"x_in": np.ascontiguousarray(xcur[c])}
            for s in group:
                for n in _IN_NAMES[s]:
                    m[n] = pc[n][c]
            in_maps.append(m)
        res = run_bass_kernel_spmd(nc, in_maps, core_ids=list(range(NCORE)))
        xcur = [np.asarray(res.results[c]["x_out"]) for c in range(NCORE)]
    out = np.empty((B, S, D), dtype=np.float32)
    for c in range(NCORE):
        out[c // 4, (c % 4) * T:(c % 4 + 1) * T, :] = xcur[c].T
    return out


# version 04
# speedup vs baseline: 1.1831x; 1.0578x over previous
import numpy as np
import ml_dtypes

import concourse.bass as bass
import concourse.mybir as mybir
from concourse.bass_utils import run_bass_kernel_spmd

F32 = mybir.dt.float32
BF16 = mybir.dt.bfloat16
I32 = mybir.dt.int32
AF = mybir.ActivationFunctionType
ALU = mybir.AluOpType
AX = mybir.AxisListType
BF = ml_dtypes.bfloat16

B, S, D, T = 2, 8192, 1024, 2048
H, HD = 16, 64
E8, F1, H3 = 8, 4096, 3072
NCORE = 8
DC = D // 128
BLK = 2048 + 64
GROUPS = [[0, 1, 2, 3], [4, 5, 6, 7]]
L0SCALE = float(1.0 / np.sqrt(S))
L1SCALE = float(1.0 / np.sqrt(HD))
EPS = 1e-5
INVD = 1.0 / D
Q127 = 1.0 / 127.0

LAUNCH_PLAN = [["l0a", "l0f", "l1a", "l1f"]]
DEBUG_PHASE = 99
DEBUG_DUMP = []
DETECT_RACES = True


def _sl(s, n):
    return slice(s, s + n)


class Prog:
    ENGS = ("tensor", "vector", "scalar", "gpsimd", "sync")

    def __init__(self, nc):
        self.nc = nc
        self.q = {e: [] for e in self.ENGS}
        self.sems = {}
        self.cnt = {}

    def sem(self, name):
        if name not in self.sems:
            self.sems[name] = self.nc.alloc_semaphore(name)
            self.cnt[name] = 0
        return self.sems[name]

    def emit(self, eng, fn, inc=None):
        incs = []
        if inc is not None:
            for s, a in (inc if isinstance(inc, list) else [inc]):
                self.sem(s)
                self.cnt[s] += a
                incs.append((self.sems[s], a))

        def run(e, fn=fn, incs=incs):
            ins = fn(e)
            for h, a in incs:
                ins = ins.then_inc(h, a)
            return ins
        self.q[eng].append(run)

    def wait(self, eng, s, val):
        if val <= 0:
            return
        h = self.sem(s)
        self.q[eng].append(lambda e, h=h, v=val: e.wait_ge(h, v))

    def val(self, s):
        self.sem(s)
        return self.cnt[s]

    def barrier(self, tag):
        vals = {sname: self.cnt[sname] for sname in self.cnt}
        for eng in self.ENGS:
            for sname, v in vals.items():
                self.wait(eng, sname, v)

    def finalize(self):
        with self.nc.Block() as block:
            for engname in self.ENGS:
                ops = self.q[engname]

                def body(eng, ops=ops):
                    for f in ops:
                        f(eng)
                getattr(block, engname)(body)


def build_program(sands):
    nc = bass.Bass("TRN2", target_bir_lowering=False, debug=False,
                   detect_race_conditions=DETECT_RACES)
    pr = Prog(nc)
    tn = {}
    A = {}
    P = {}

    tn["x_in"] = nc.dram_tensor("x_in", [D, T], F32, kind="ExternalInput")
    tn["x_out"] = nc.dram_tensor("x_out", [D, T], F32, kind="ExternalOutput")

    if "l0a" in sands:
        for w in ("wqT", "wkT", "wvT"):
            tn[w] = nc.dram_tensor(w, [D, D], F32, kind="ExternalInput")
        tn["bqkv"] = nc.dram_tensor("bqkv", [128, 24], F32, kind="ExternalInput")
        tn["gb"] = nc.dram_tensor("gb", [128, 16], F32, kind="ExternalInput")
        tn["bvrow"] = nc.dram_tensor("bvrow", [1, D], F32, kind="ExternalInput")
    if "l0f" in sands:
        tn["wgt"] = nc.dram_tensor("wgt", [D, E8], F32, kind="ExternalInput")
        tn["bg"] = nc.dram_tensor("bg", [E8, 1], F32, kind="ExternalInput")
        tn["w1t"] = nc.dram_tensor("w1t", [E8, 32, 128, 1024], F32, kind="ExternalInput")
        tn["b1t"] = nc.dram_tensor("b1t", [E8, 128, 32], F32, kind="ExternalInput")
        tn["w2t"] = nc.dram_tensor("w2t", [E8, DC, 128, F1], F32, kind="ExternalInput")
        tn["b2"] = nc.dram_tensor("b2", [E8, D], F32, kind="ExternalInput")
    if "l1a" in sands:
        tn["wqT1a"] = nc.dram_tensor("wqT1a", [D, D], F32, kind="ExternalInput")
        tn["wqT1b"] = nc.dram_tensor("wqT1b", [D, D], F32, kind="ExternalInput")
        tn["wk1a"] = nc.dram_tensor("wk1a", [D, HD], F32, kind="ExternalInput")
        tn["wk1b"] = nc.dram_tensor("wk1b", [D, HD], F32, kind="ExternalInput")
        tn["wv1"] = nc.dram_tensor("wv1", [D, HD], F32, kind="ExternalInput")
        tn["woT"] = nc.dram_tensor("woT", [D, D], F32, kind="ExternalInput")
        tn["cos4"] = nc.dram_tensor("cos4", [128, T], F32, kind="ExternalInput")
        tn["sin4"] = nc.dram_tensor("sin4", [128, T], F32, kind="ExternalInput")
    if "l1f" in sands:
        tn["wgT1"] = nc.dram_tensor("wgT1", [24, 128, 1024], I32, kind="ExternalInput")
        tn["wuT1"] = nc.dram_tensor("wuT1", [24, 128, 1024], I32, kind="ExternalInput")
        tn["wdT1"] = nc.dram_tensor("wdT1", [DC, 128, H3], I32, kind="ExternalInput")

    tn["statb"] = nc.dram_tensor("statb", [1, 4096], F32)
    tn["rowb"] = nc.dram_tensor("rowb", [1, 1024], F32)
    if "l0a" in sands or "l1a" in sands:
        tn["g1_src"] = nc.dram_tensor("g1_src", [BLK, T], BF16)
        tn["g1_out"] = nc.dram_tensor("g1_out", [4 * BLK, T], BF16)
        tn["g2_src"] = nc.dram_tensor("g2_src", [T, D], BF16)
        tn["g2_out"] = nc.dram_tensor("g2_out", [S, D], BF16)
        tn["g3_src"] = nc.dram_tensor("g3_src", [256, S], BF16)
        tn["g3_out"] = nc.dram_tensor("g3_out", [D, S], BF16)
    if "l0f" in sands:
        tn["w8d"] = nc.dram_tensor("w8d", [E8, T], BF16)

    kl = np.arange(128)[:, None]
    ql = np.arange(512)[None, :]
    if "l0a" in sands:
        m0 = np.concatenate([(kl + o >= ql).astype(BF) for o in (0, 128, 256, 384)],
                            axis=1)
        tn["mask0"] = nc.inline_tensor(np.ascontiguousarray(m0), "mask0")
    if "l1a" in sands:
        m1 = np.concatenate([(kl + o <= ql).astype(BF) for o in (0, 128, 256, 384)],
                            axis=1)
        tn["mask1"] = nc.inline_tensor(np.ascontiguousarray(m1), "mask1")
    if "l0f" in sands:
        tn["ident"] = nc.inline_tensor(np.eye(128, dtype=BF), "ident")

    BASE = (nc.sbuf_base + 31) // 32 * 32
    AB = BASE + 131072
    TAIL = BASE + 202240
    _ranges = []

    def sb(name, shape, dtype, off, group="a"):
        import numpy as _np
        nbytes = int(_np.prod(shape[1:])) * {F32: 4, BF16: 2, I32: 4}[dtype]
        assert off + nbytes <= nc.sbuf_top, f"{name} past sbuf top"
        for (n2, o2, e2, g2) in _ranges:
            if g2 == group and not (off + nbytes <= o2 or e2 <= off):
                raise AssertionError(f"sbuf overlap {name} [{off},{off + nbytes}) "
                                     f"vs {n2} [{o2},{e2}) group {group}")
        _ranges.append((name, off, off + nbytes, group))
        A[name] = nc.alloc_sbuf_tensor_at(name, shape, dtype, offset=off)
        return A[name]

    sb("X", [128, DC * T], F32, BASE + 0, "tail")
    sb("LNX", [128, DC * T], BF16, BASE + 65536, "tail")
    sb("Z", [128, DC * T], BF16, BASE + 98304, "tail")
    sb("TMP", [128, T], F32, AB + 0, "ln1")
    sb("SQ", [128, 1024], F32, AB + 0, "ln2")
    sb("ABC", [128, T], F32, AB + 8192, "ln1")
    sb("BBC", [128, T], F32, AB + 16384, "ln1")
    sb("R0", [1, 512], F32, AB + 24576, "ln1")
    sb("R1", [1, 512], F32, AB + 26624, "ln1")
    sb("R2", [1, 512], F32, AB + 28672, "ln1")
    sb("WB0", [128, 8192], BF16, AB + 30720, "ln1")
    sb("STG", [128, T], BF16, AB + 47104, "ln1")
    sb("ONESF", [128, 1], F32, TAIL + 0, "tail")
    sb("ONESB", [128, 1], BF16, TAIL + 32, "tail")
    sb("SCL", [128, 4], F32, TAIL + 64, "tail")
    if "l0a" in sands:
        sb("gbs", [128, 16], F32, TAIL + 96, "tail")
        sb("bqkvs", [128, 24], F32, TAIL + 160, "tail")
        sb("BVB", [128, D], BF16, TAIL + 6976, "tail")
    if "l0f" in sands:
        sb("bgs", [8, 1], F32, TAIL + 288, "tail")
        sb("b1s", [128, 32], F32, TAIL + 320, "tail")
        sb("WGS", [128, 64], BF16, TAIL + 448, "tail")
        sb("IDS", [128, 128], BF16, TAIL + 576, "tail")
        sb("B2S", [8, D], BF16, TAIL + 832, "tail")
    if "l0a" in sands or "l1a" in sands:
        sb("MSK", [128, 2048], BF16, TAIL + 2880, "tail")
        sb("QS0", [128, S], BF16, AB + 0, "attn")
        sb("KP", [128, S], BF16, AB + 16384, "attn")
        sb("VS0", [128, 64 * 64], BF16, AB + 32768, "attn")
        sb("VS1", [128, 64 * 64], BF16, AB + 40960, "attn")
        sb("EB0", [128, 1024], BF16, AB + 49152, "attn")
        sb("EB1", [128, 1024], BF16, AB + 51200, "attn")
        sb("EM", [128, 2048], BF16, AB + 53248, "attn")
        sb("PSTG", [128, 1024], BF16, AB + 57344, "attn")
        sb("RB", [128, 1024], F32, AB + 59392, "attn")
        sb("RRX", [128, 512], F32, AB + 63488, "attn")
    if "l1a" in sands:
        sb("COS", [128, T], F32, AB + 0, "l1t")
        sb("SIN", [128, T], F32, AB + 8192, "l1t")
        sb("WRKA", [128, T], BF16, AB + 16384, "l1t")
        sb("WRKB", [128, T], BF16, AB + 20480, "l1t")
        sb("KTA", [64, T], BF16, AB + 24576, "l1t")
        sb("KTB", [64, T], BF16, AB + 28672, "l1t")
        sb("WVs", [128, 512], BF16, AB + 28672, "l1t2")
    if "l0f" in sands:
        sb("HSL", [128, 32 * 512], BF16, AB + 0, "l0f_h")
        sb("W8B", [8, T], BF16, AB + 28672, "l0f_w8")
        sb("W1B", [128, 8192], BF16, AB + 32768, "l0f_w")
        sb("W2B", [128, 2 * F1], BF16, AB + 49152, "l0f_w")
        sb("WBC", [128, 1024], F32, AB + 65536, "l0f_w")
        sb("TMI", [128, 512], BF16, TAIL + 9024, "tail")
        sb("GS", [8, T], BF16, AB + 32768, "l0f_gate")
        sb("GTA", [128, 128], BF16, AB + 36864, "l0f_gate")
        sb("GEB", [128, 128], F32, AB + 37120, "l0f_gate")
        sb("GEC", [128, 128], F32, AB + 37632, "l0f_gate")
        sb("REDA", [128, 48], F32, AB + 38144, "l0f_gate")
    if "l1f" in sands:
        sb("HSL1", [128, 24 * 1024], BF16, AB + 0, "l1f")
        sb("WI", [128, 3072], I32, AB + 49152, "l1f")
        sb("WF0", [128, 3072], BF16, AB + 61440, "l1f")
        sb("WF1", [128, 1024], BF16, AB + 67584, "l1f")
        sb("SIL", [128, 512], BF16, AB + 69632, "l1f")

    P["B0"] = nc.alloc_psum_tensor("B0", [128, 512], F32)
    P["B1"] = nc.alloc_psum_tensor("B1", [128, 512], F32)
    P["B2"] = nc.alloc_psum_tensor("B2", [128, 512], F32)
    P["B3"] = nc.alloc_psum_tensor("B3", [128, 512], F32)
    P["ACC"] = nc.alloc_psum_tensor("ACC", [128, 512], F32)
    P["DEN"] = nc.alloc_psum_tensor("DEN", [128, 512], F32)
    P["PST"] = nc.alloc_psum_tensor("PST", [8, 512], F32)
    P["PTR"] = nc.alloc_psum_tensor("PTR", [128, 128], F32)

    pid_cache = {}

    def get_sym(e, name):
        key = (str(e.engine), name)
        if key not in pid_cache:
            pid = nc.partition_id(engines=[e.engine])
            mod = pid % 4
            pid_cache[(str(e.engine), "hg256")] = mod * 256
            pid_cache[(str(e.engine), "tcol")] = mod * T
        return pid_cache[key]

    ctx = dict(pr=pr, nc=nc, tn=tn, A=A, P=P, get_sym=get_sym)

    pr.emit("sync", lambda e: e.dma_start(
        out=A["X"].ap().rearrange("p (c t) -> p c t", t=T),
        in_=tn["x_in"].ap().rearrange("(c p) t -> p c t", p=128)), inc=("ldw", 16))
    pr.emit("vector", lambda e: e.memset(A["ONESF"][:, :], 1.0))
    pr.emit("vector", lambda e: e.memset(A["ONESB"][:, :], 1.0))
    pr.emit("vector", lambda e: e.memset(A["SCL"][:, 0:1], L0SCALE))
    pr.emit("vector", lambda e: e.memset(A["SCL"][:, 1:2], L1SCALE))
    pr.emit("vector", lambda e: e.memset(A["SCL"][:, 2:3], Q127))
    pr.emit("vector", lambda e: e.memset(A["SCL"][:, 3:4], 0.0), inc=("dve", 1))
    m = pr.val("dve")
    for eng in ("tensor", "scalar", "gpsimd", "sync"):
        pr.wait(eng, "dve", m)
    lm = pr.val("ldw")
    for eng in ("tensor", "vector", "scalar", "gpsimd"):
        pr.wait(eng, "ldw", lm)

    for sand in sands:
        if sand == "l0a":
            emit_l0a(ctx)
        elif sand == "l0f":
            emit_l0f(ctx)
        elif sand == "l1a":
            emit_l1a(ctx)
        elif sand == "l1f":
            emit_l1f(ctx)
        pr.barrier(sand)

    for nm in DEBUG_DUMP:
        h = A[nm]
        shp = list(h.ap().shape)
        dt_ = h.ap().dtype
        tdbg = nc.dram_tensor(f"dbg_{nm}", shp, dt_, kind="ExternalOutput")
        pr.emit("gpsimd", lambda e, tdbg=tdbg, h=h: e.dma_start(
            out=tdbg.ap(), in_=h[:, :]), inc=("st", 16))
    pr.emit("gpsimd", lambda e: e.dma_start(
        out=tn["x_out"].ap().rearrange("(c p) t -> p c t", p=128),
        in_=A["X"].ap().rearrange("p (c t) -> p c t", t=T)), inc=("st", 16))
    pr.wait("gpsimd", "st", pr.val("st"))
    pr.finalize()
    return nc


def emit_ln(ctx, src, src_dt, mode):
    pr, nc, tn, A, P = ctx["pr"], ctx["nc"], ctx["tn"], ctx["A"], ctx["P"]
    ones = A["ONESF"] if src_dt == F32 else A["ONESB"]
    ssrc = A[src]
    for sname in ("dve", "zev", "hev", "cst"):
        pr.wait("tensor", sname, pr.val(sname))
        pr.wait("scalar", sname, pr.val(sname))
    sqmm = []
    rowst = []
    for qc in range(4):
        for dc in range(DC):
            col = _sl(dc * T + qc * 512, 512)
            half = _sl((dc % 2) * 512, 512)
            if len(sqmm) >= 2:
                pr.wait("scalar", "pe2", sqmm[-2])
            pr.emit("scalar", lambda e, col=col, half=half: e.activation(
                A["SQ"][:, half], ssrc[:, col], AF.Square), inc=("act", 1))
            pr.wait("tensor", "act", pr.val("act"))
            pr.emit("tensor", lambda e, dc=dc, half=half: e.matmul(
                P["B3"][0:1, :], A["ONESF"][:, :], A["SQ"][:, half],
                start=(dc == 0), stop=(dc == DC - 1)), inc=("pe2", 1))
            sqmm.append(pr.val("pe2"))
            pr.emit("tensor", lambda e, col=col, dc=dc: e.matmul(
                P["PST"][0:1, :], ones[:, :], ssrc[:, col],
                start=(dc == 0), stop=(dc == DC - 1)),
                inc=("pe", 1) if dc == DC - 1 else None)
        pm = pr.val("pe")
        pr.wait("vector", "pe", pm)
        pr.wait("vector", "pe2", pr.val("pe2"))
        if rowst:
            pr.wait("vector", "st", rowst[-1])
        pr.emit("vector", lambda e: e.tensor_scalar(
            A["R0"][:, :], P["PST"][0:1, :], INVD, None, ALU.mult))
        pr.emit("vector", lambda e: e.tensor_scalar(
            A["R1"][:, :], P["B3"][0:1, :], INVD, None, ALU.mult),
            inc=("dve", 1))
        bankfree = pr.val("dve")
        pr.wait("tensor", "dve", bankfree)
        pr.emit("vector", lambda e: e.tensor_tensor(
            A["R2"][:, :], A["R0"][:, :], A["R0"][:, :], ALU.mult))
        pr.emit("vector", lambda e: e.tensor_tensor(
            A["R1"][:, :], A["R1"][:, :], A["R2"][:, :], ALU.subtract))
        pr.emit("vector", lambda e: e.tensor_scalar(
            A["R1"][:, :], A["R1"][:, :], EPS, None, ALU.add), inc=("dve", 1))
        pr.wait("scalar", "dve", pr.val("dve"))
        pr.emit("scalar", lambda e: e.activation(
            A["R2"][:, :], A["R1"][:, :], AF.Sqrt), inc=("act", 1))
        pr.wait("vector", "act", pr.val("act"))
        pr.emit("vector", lambda e: e.reciprocal(A["R1"][:, :], A["R2"][:, :]))
        pr.emit("vector", lambda e: e.tensor_tensor(
            A["R2"][:, :], A["R0"][:, :], A["R1"][:, :], ALU.mult))
        pr.emit("vector", lambda e: e.tensor_scalar(
            A["R2"][:, :], A["R2"][:, :], -1.0, None, ALU.mult), inc=("dve", 1))
        rm = pr.val("dve")
        pr.wait("gpsimd", "dve", rm)
        pr.emit("gpsimd", lambda e, qc=qc: e.dma_start(
            out=tn["statb"].ap()[0:1, _sl(qc * 512, 512)], in_=A["R1"][:, :]),
            inc=("st", 16))
        pr.emit("gpsimd", lambda e, qc=qc: e.dma_start(
            out=tn["statb"].ap()[0:1, _sl(2048 + qc * 512, 512)], in_=A["R2"][:, :]),
            inc=("st", 16))
        rowst.append(pr.val("st"))
    stm = pr.val("st")
    pr.wait("sync", "st", stm)
    pr.emit("sync", lambda e: e.dma_start(
        out=A["ABC"][:, :], in_=tn["statb"].ap()[0:1, 0:2048].broadcast_to([128, T])),
        inc=("ldw", 16))
    pr.emit("sync", lambda e: e.dma_start(
        out=A["BBC"][:, :], in_=tn["statb"].ap()[0:1, 2048:4096].broadcast_to([128, T])),
        inc=("ldw", 16))
    pr.wait("vector", "ldw", pr.val("ldw"))
    for dc in range(DC):
        col = _sl(dc * T, T)
        pr.emit("vector", lambda e, col=col: e.tensor_tensor(
            A["TMP"][:, :], ssrc[:, col], A["ABC"][:, :], ALU.mult))
        if mode == "pre":
            pr.emit("vector", lambda e, col=col: e.tensor_tensor(
                A["LNX"][:, col], A["TMP"][:, :], A["BBC"][:, :], ALU.add),
                inc=("dve", 1))
        elif mode == "aff":
            pr.emit("vector", lambda e: e.tensor_tensor(
                A["TMP"][:, :], A["TMP"][:, :], A["BBC"][:, :], ALU.add))
            pr.emit("vector", lambda e, col=col, dc=dc: e.tensor_scalar(
                A["LNX"][:, col], A["TMP"][:, :],
                A["gbs"][:, _sl(dc, 1)], A["gbs"][:, _sl(8 + dc, 1)],
                ALU.mult, ALU.add), inc=("dve", 1))
        elif mode == "res":
            pr.emit("vector", lambda e: e.tensor_tensor(
                A["TMP"][:, :], A["TMP"][:, :], A["BBC"][:, :], ALU.add))
            pr.emit("vector", lambda e, col=col: e.tensor_tensor(
                A["X"][:, col], A["X"][:, col], A["TMP"][:, :], ALU.add),
                inc=("dve", 1))


def emit_proj(ctx, wnames, n_out, out_cb, wbuf="WB0", rhs_slab="LNX"):
    pr, nc, tn, A, P = ctx["pr"], ctx["nc"], ctx["tn"], ctx["A"], ctx["P"]
    W = A[wbuf]
    marks = []
    for wi, wname in enumerate(wnames):
        pr.wait("gpsimd", "pe", pr.val("pe"))
        pr.wait("tensor", "dve", pr.val("dve"))
        for dc in range(DC):
            src = tn[wname].ap()[_sl(dc * 128, 128), :]
            pr.emit("gpsimd", lambda e, dc=dc, src=src, n_out=n_out: e.dma_start(
                out=W[:, _sl(dc * n_out, n_out)], in_=src), inc=("ldg", 16))
        pr.wait("tensor", "ldg", pr.val("ldg"))
        for oc in range(n_out // 128):
            for qc in range(4):
                half = (oc * 4 + qc) % 4
                ps = [P["B0"][:, :], P["B1"][:, :], P["B2"][:, :],
                      P["B3"][:, :]][half]
                if len(marks) >= 4:
                    pr.wait("tensor", "dve", marks[-4])
                for dc in range(DC):
                    lhsT = W[:, _sl(dc * n_out + oc * 128, 128)]
                    rhs = A[rhs_slab][:, _sl(dc * T + qc * 512, 512)]
                    pr.emit("tensor", lambda e, ps=ps, lhsT=lhsT, rhs=rhs, dc=dc:
                            e.matmul(ps, lhsT, rhs, start=(dc == 0),
                                     stop=(dc == DC - 1)),
                            inc=("pe", 1) if dc == DC - 1 else None)
                pm = pr.val("pe")
                pr.wait("vector", "pe", pm)
                out_cb(wi, oc, qc, ps)
                pr.emit("vector", lambda e: e.engine_nop(), inc=("dve", 1))
                marks.append(pr.val("dve"))


def emit_l0a(ctx):
    pr, nc, tn, A, P = ctx["pr"], ctx["nc"], ctx["tn"], ctx["A"], ctx["P"]
    pr.emit("sync", lambda e: e.dma_start(out=A["gbs"][:, :], in_=tn["gb"].ap()),
            inc=("ldw", 16))
    pr.emit("sync", lambda e: e.dma_start(out=A["bqkvs"][:, :], in_=tn["bqkv"].ap()),
            inc=("ldw", 16))
    pr.emit("sync", lambda e: e.dma_start(out=A["MSK"][:, :], in_=tn["mask0"].ap()),
            inc=("ldw", 16))
    pr.emit("gpsimd", lambda e: e.dma_start(
        out=A["BVB"][:, :], in_=tn["bvrow"].ap()[0:1, :].broadcast_to([128, D])),
        inc=("ldg", 16))
    pr.wait("vector", "ldw", pr.val("ldw"))
    pr.wait("vector", "ldg", pr.val("ldg"))
    pr.wait("scalar", "ldw", pr.val("ldw"))
    if DEBUG_PHASE < 1:
        return

    emit_ln(ctx, "X", F32, "pre")
    pr.wait("tensor", "dve", pr.val("dve"))
    if DEBUG_PHASE < 2:
        return

    stflags = {"last": None}

    def store_fm(base_row, bias_base):
        def cb(wi, oc, qc, ps):
            if qc == 0 and stflags["last"] is not None:
                pr.wait("vector", "st", stflags["last"])
            pr.emit("vector", lambda e, ps=ps, qc=qc, oc=oc: e.tensor_scalar(
                A["STG"][:, _sl(qc * 512, 512)], ps,
                A["bqkvs"][:, _sl(bias_base + oc, 1)], None, ALU.add),
                inc=("dve", 1))
            if qc == 3:
                dm = pr.val("dve")
                pr.wait("gpsimd", "dve", dm)
                pr.emit("gpsimd", lambda e, oc=oc: e.dma_start(
                    out=tn["g1_src"].ap()[_sl(base_row + oc * 128, 128), :],
                    in_=A["STG"][:, :]), inc=("st", 16))
                stflags["last"] = pr.val("st")
        return cb

    emit_proj(ctx, ["wkT"], D, store_fm(1024, 8))
    if DEBUG_PHASE < 3:
        return
    pr.wait("gpsimd", "pe", pr.val("pe"))
    for dc in range(DC):
        pr.emit("gpsimd", lambda e, dc=dc: e.dma_start(
            out=A["WB0"][:, _sl(dc * D, D)], in_=tn["wvT"].ap()[_sl(dc * 128, 128), :]),
            inc=("ldg", 16))
    pr.wait("tensor", "ldg", pr.val("ldg"))
    pr.wait("tensor", "dve", pr.val("dve"))
    marks = []
    for tt in range(16):
        for fh in range(2):
            half = (tt * 2 + fh) % 4
            ps = [P["B0"][:, :], P["B1"][:, :], P["B2"][:, :],
                  P["B3"][:, :]][half]
            if len(marks) >= 4:
                pr.wait("tensor", "dve", marks[-4])
            for dc in range(DC):
                lhsT = A["LNX"][:, _sl(dc * T + tt * 128, 128)]
                rhs = A["WB0"][:, _sl(dc * D + fh * 512, 512)]
                pr.emit("tensor", lambda e, ps=ps, lhsT=lhsT, rhs=rhs, dc=dc: e.matmul(
                    ps, lhsT, rhs, start=(dc == 0), stop=(dc == DC - 1)),
                    inc=("pe", 1) if dc == DC - 1 else None)
            pm = pr.val("pe")
            pr.wait("vector", "pe", pm)
            if fh == 0 and stflags["last"] is not None:
                pr.wait("vector", "st", stflags["last"])
            pr.emit("vector", lambda e, ps=ps, fh=fh: e.tensor_tensor(
                A["STG"][:, _sl(fh * 512, 512)], ps, A["BVB"][:, _sl(fh * 512, 512)],
                ALU.add), inc=("dve", 1))
            marks.append(pr.val("dve"))
        dm = pr.val("dve")
        pr.wait("gpsimd", "dve", dm)
        pr.emit("gpsimd", lambda e, tt=tt: e.dma_start(
            out=tn["g2_src"].ap()[_sl(tt * 128, 128), :], in_=A["STG"][:, 0:1024]),
            inc=("st", 16))
        stflags["last"] = pr.val("st")
    pr.wait("gpsimd", "st", pr.val("st"))
    pr.emit("gpsimd", lambda e: e.collective_compute(
        "AllGather", ALU.bypass, replica_groups=GROUPS,
        ins=[tn["g2_src"].ap()], outs=[tn["g2_out"].ap()]), inc=("cc", 1))
    if DEBUG_PHASE < 4:
        return
    emit_proj(ctx, ["wqT"], D, store_fm(0, 0))
    pr.wait("gpsimd", "st", pr.val("st"))
    pr.emit("gpsimd", lambda e: e.collective_compute(
        "AllGather", ALU.bypass, replica_groups=GROUPS,
        ins=[tn["g1_src"].ap()], outs=[tn["g1_out"].ap()]), inc=("cc", 1))

    if DEBUG_PHASE < 5:
        return
    emit_attn(ctx, layer=0)
    if DEBUG_PHASE < 6:
        return

    pr.wait("gpsimd", "st", pr.val("st"))
    pr.emit("gpsimd", lambda e: e.collective_compute(
        "AllGather", ALU.bypass, replica_groups=GROUPS,
        ins=[tn["g3_src"].ap()], outs=[tn["g3_out"].ap()]), inc=("cc", 1))
    ccz = pr.val("cc")
    pr.wait("sync", "cc", ccz)
    pr.wait("sync", "dve", pr.val("dve"))
    def ldz(e):
        c0 = ctx["get_sym"](e, "tcol")
        src = tn["g3_out"].ap().rearrange("(c p) t -> p c t", p=128)
        src = src[:, :, bass.ds(c0, T)]
        return e.dma_start(out=A["Z"].ap().rearrange("p (c t) -> p c t", t=T),
                           in_=src)
    pr.emit("sync", ldz, inc=("ldw", 16))
    pr.wait("vector", "ldw", pr.val("ldw"))
    for dc in range(DC):
        col = _sl(dc * T, T)
        pr.emit("vector", lambda e, col=col: e.tensor_tensor(
            A["Z"][:, col], A["Z"][:, col], A["LNX"][:, col], ALU.add), inc=("dve", 1))
    emit_ln(ctx, "Z", BF16, "aff")
    emit_ln(ctx, "LNX", BF16, "res")


def emit_attn(ctx, layer):
    pr, nc, tn, A, P = ctx["pr"], ctx["nc"], ctx["tn"], ctx["A"], ctx["P"]
    ccmark = pr.val("cc")
    pr.wait("sync", "cc", ccmark)
    scale_col = _sl(layer, 1)

    state = {"acfree": 0, "ss_last": [0, 0], "eb_last": [0, 0], "em_last": [0, 0]}
    pr.wait("tensor", "dve", pr.val("dve"))
    for p_ in range(2):
        QS = A["QS0"]
        pr.wait("sync", "pe", pr.val("pe"))

        def ldq(e, p_=p_, QS=QS):
            hg = ctx["get_sym"](e, "hg256")
            src = tn["g1_out"].ap().rearrange("(r b) t -> b r t", r=4)
            src = src[_sl(p_ * 128, 896), :, :][bass.ds(hg, 128), :, :]
            return e.dma_start(out=QS.ap().rearrange("p (r t) -> p r t", r=4),
                               in_=src)
        pr.emit("sync", ldq, inc=("ldw", 16))
        if layer == 0:
            def ldk(e, p_=p_):
                hg = ctx["get_sym"](e, "hg256")
                src = tn["g1_out"].ap().rearrange("(r b) t -> b r t", r=4)
                src = src[_sl(1024 + p_ * 128, 896), :, :][bass.ds(hg, 128), :, :]
                return e.dma_start(out=A["KP"].ap().rearrange(
                    "p (r t) -> p r t", r=4), in_=src)
            pr.emit("sync", ldk, inc=("ldw", 16))
        else:
            for hb in range(2):
                def ldk(e, hb=hb):
                    src = tn["g1_out"].ap().rearrange(
                        "(r b) t -> b r t", r=4)[_sl(1024, 64), :, :]
                    return e.dma_start(
                        out=A["KP"].ap().rearrange(
                            "p (r t) -> p r t", r=4)[_sl(hb * 64, 64), :, :],
                        in_=src)
                pr.emit("sync", ldk, inc=("ldw", 16))
        if layer == 0:
            for hh in range(2):
                def ldv(e, p_=p_, hh=hh):
                    hg = ctx["get_sym"](e, "hg256")
                    src = tn["g2_out"].ap().rearrange(
                        "(r k p) c -> p r k c", p=128, r=4)
                    src = src[:, :, :, _sl(p_ * 128 + hh * 64, 832)]
                    src = src[:, :, :, bass.ds(hg, 64)]
                    dst = A[("VS0", "VS1")[hh]].ap().rearrange(
                        "p (r k c) -> p r k c", c=64, r=4)
                    return e.dma_start(out=dst, in_=src)
                pr.emit("sync", ldv, inc=("ldw", 16))
        else:
            def ldv(e):
                src = tn["g2_out"].ap().rearrange(
                    "(r k p) c -> p r k c", p=128, r=4)[:, :, :, 0:64]
                dst = A["VS0"].ap().rearrange(
                    "p (r k c) -> p r k c", c=64, r=4)
                return e.dma_start(out=dst, in_=src)
            pr.emit("sync", ldv, inc=("ldw", 16))
        kvm = pr.val("ldw")
        for eng in ("tensor", "scalar", "vector"):
            pr.wait(eng, "ldw", kvm)

        for qc in range(16):
            if layer == 0:
                kts = list(range(64))
                regions = ["below" if kt < 4 * qc else
                           ("diag" if kt <= 4 * qc + 3 else "above") for kt in kts]
            else:
                kts = list(range(min(64, 4 * qc + 4)))
                regions = ["full" if kt < 4 * qc else "diag" for kt in kts]
            n = len(kts)
            qk_marks = [None] * n
            exp_marks = [None] * n
            pv_marks = [None] * n
            msk_marks = [None] * n
            grp = {"acc": False, "den": False}
            last_acc = max(i for i in range(n) if regions[i] != "below")
            last_den = n - 1

            def emit_pv(j):
                kt, region = kts[j], regions[j]
                EBj = A["EB0"] if j % 2 == 0 else A["EB1"]
                pr.wait("tensor", "act", exp_marks[j])
                if region == "diag":
                    pr.wait("tensor", "dve", msk_marks[j])
                pr.wait("tensor", "dve", state["acfree"])
                num_src = (A["EM"].ap()[:, _sl((j % 2) * 1024, 1024)]
                           if region == "diag" else EBj)
                den_src = EBj if layer == 0 else \
                    (A["EM"].ap()[:, _sl((j % 2) * 1024, 1024)]
                     if region == "diag" else EBj)
                if region != "below":
                    st0 = not grp["acc"]
                    grp["acc"] = True
                    stop = (j == last_acc)
                    pr.emit("tensor", lambda e, kt=kt, src=num_src, st0=st0, stop=stop:
                            e.matmul(P["ACC"][0:64, :],
                                     A["VS0"][:, _sl(kt * 64, 64)],
                                     src[:, 0:512], start=st0, stop=stop,
                                     tile_position=(0, 0)))
                    VS_o = A["VS0"] if layer == 1 else A["VS1"]
                    pr.emit("tensor", lambda e, kt=kt, src=num_src, st0=st0, stop=stop,
                            VS_o=VS_o:
                            e.matmul(P["ACC"][64:128, :],
                                     VS_o[:, _sl(kt * 64, 64)],
                                     src[:, 512:1024], start=st0, stop=stop,
                                     tile_position=(0, 64)))
                st0 = not grp["den"]
                grp["den"] = True
                stop = (j == last_den)
                pr.emit("tensor", lambda e, src=den_src, st0=st0, stop=stop:
                        e.matmul(P["DEN"][0:1, :], A["ONESB"][:, :],
                                 src[:, 0:512], start=st0, stop=stop,
                                 tile_position=(0, 0)))
                pr.emit("tensor", lambda e, src=den_src, st0=st0, stop=stop:
                        e.matmul(P["DEN"][64:65, :], A["ONESB"][:, :],
                                 src[:, 512:1024], start=st0, stop=stop,
                                 tile_position=(0, 64)), inc=("pe", 1))
                pv_marks[j] = pr.val("pe")
                state["eb_last"][j % 2] = pv_marks[j]
                if regions[j] == "diag":
                    state["em_last"][j % 2] = pv_marks[j]

            for i, kt in enumerate(kts):
                SSa = P["B0"] if i % 2 == 0 else P["B2"]
                SSb = P["B1"] if i % 2 == 0 else P["B3"]
                EB = A["EB0"] if i % 2 == 0 else A["EB1"]
                pr.wait("tensor", "act", state["ss_last"][i % 2])
                pr.emit("tensor", lambda e, SSa=SSa, kt=kt, qc=qc, QS=QS: e.matmul(
                    SSa[:, :], A["KP"][0:64, _sl(kt * 128, 128)],
                    QS[0:64, _sl(qc * 512, 512)], start=True, stop=True,
                    tile_position=(0, 0)))
                pr.emit("tensor", lambda e, SSb=SSb, kt=kt, qc=qc, QS=QS: e.matmul(
                    SSb[:, :], A["KP"][64:128, _sl(kt * 128, 128)],
                    QS[64:128, _sl(qc * 512, 512)], start=True, stop=True,
                    tile_position=(64, 0)), inc=("pe", 1))
                qk_marks[i] = pr.val("pe")
                pr.wait("scalar", "pe", qk_marks[i])
                pr.wait("scalar", "pe", state["eb_last"][i % 2])
                pr.emit("scalar", lambda e, SSa=SSa, EB=EB: e.activation(
                    EB[:, 0:512], SSa[:, :], AF.Exp, scale=A["SCL"][:, scale_col]))
                pr.emit("scalar", lambda e, SSb=SSb, EB=EB: e.activation(
                    EB[:, 512:1024], SSb[:, :], AF.Exp, scale=A["SCL"][:, scale_col]),
                    inc=("act", 1))
                exp_marks[i] = pr.val("act")
                state["ss_last"][i % 2] = exp_marks[i]
                if regions[i] == "diag":
                    o = kt - 4 * qc
                    eh = _sl((i % 2) * 1024, 1024)
                    pr.wait("vector", "act", exp_marks[i])
                    pr.wait("vector", "pe", state["em_last"][i % 2])
                    pr.emit("vector", lambda e, EB=EB, o=o, eh=eh: e.tensor_tensor(
                        A["EM"][:, eh][:, 0:512], EB[:, 0:512],
                        A["MSK"][:, _sl(o * 512, 512)], ALU.mult))
                    pr.emit("vector", lambda e, EB=EB, o=o, eh=eh: e.tensor_tensor(
                        A["EM"][:, eh][:, 512:1024], EB[:, 512:1024],
                        A["MSK"][:, _sl(o * 512, 512)], ALU.mult), inc=("dve", 1))
                    msk_marks[i] = pr.val("dve")
                    state["em_last"][i % 2] = 0
                if i >= 1:
                    emit_pv(i - 1)
            emit_pv(n - 1)
            pr.wait("vector", "pe", pv_marks[n - 1])
            pr.emit("vector", lambda e: e.tensor_copy(A["PSTG"][:, 0:512],
                                                      P["ACC"][:, :]))
            pr.emit("vector", lambda e: e.reciprocal(A["RRX"][0:1, :],
                                                     P["DEN"][0:1, :]))
            pr.emit("vector", lambda e: e.reciprocal(A["RRX"][64:65, :],
                                                     P["DEN"][64:65, :]),
                    inc=("dve", 1))
            state["acfree"] = pr.val("dve")
            pr.wait("gpsimd", "dve", state["acfree"])
            pr.wait("gpsimd", "ldw", pr.val("ldw"))
            pr.emit("gpsimd", lambda e: e.dma_start(
                out=tn["rowb"].ap()[0:1, 0:512], in_=A["RRX"][0:1, :]),
                inc=("st", 16))
            pr.emit("gpsimd", lambda e: e.dma_start(
                out=tn["rowb"].ap()[0:1, 512:1024], in_=A["RRX"][64:65, :]),
                inc=("st", 16))
            stm = pr.val("st")
            pr.wait("sync", "st", stm)
            pr.emit("sync", lambda e: e.dma_start(
                out=A["RB"][:, :], in_=tn["rowb"].ap()[0:1, :].broadcast_to([128, 1024])),
                inc=("ldw", 16))
            rbm = pr.val("ldw")
            pr.wait("vector", "ldw", rbm)
            pr.emit("vector", lambda e, p_=p_, qc=qc: e.tensor_tensor(
                A["Z"][0:64, _sl(p_ * S + qc * 512, 512)], A["PSTG"][0:64, 0:512],
                A["RB"][0:64, 0:512], ALU.mult))
            pr.emit("vector", lambda e, p_=p_, qc=qc: e.tensor_tensor(
                A["Z"][64:128, _sl(p_ * S + qc * 512, 512)], A["PSTG"][64:128, 0:512],
                A["RB"][64:128, 512:1024], ALU.mult), inc=("dve", 1))
        zm = pr.val("dve")
        pr.wait("gpsimd", "dve", zm)
        pr.emit("gpsimd", lambda e, p_=p_: e.dma_start(
            out=tn["g3_src"].ap()[_sl(p_ * 128, 128), :],
            in_=A["Z"][:, _sl(p_ * S, S)]), inc=("st", 16))


def emit_l0f(ctx):
    pr, nc, tn, A, P = ctx["pr"], ctx["nc"], ctx["tn"], ctx["A"], ctx["P"]
    emit_ln(ctx, "X", F32, "pre")
    pr.wait("tensor", "dve", pr.val("dve"))

    pr.emit("gpsimd", lambda e: e.dma_start(
        out=A["WGS"].ap().rearrange("p (c e) -> p c e", e=E8),
        in_=tn["wgt"].ap().rearrange("(c p) e -> p c e", p=128)),
        inc=("ldg", 16))
    pr.emit("gpsimd", lambda e: e.dma_start(out=A["B2S"][:, :], in_=tn["b2"].ap()),
            inc=("ldg", 16))
    pr.emit("sync", lambda e: e.dma_start(out=A["bgs"][:, :], in_=tn["bg"].ap()),
            inc=("ldw", 16))
    pr.emit("sync", lambda e: e.dma_start(out=A["IDS"][:, :], in_=tn["ident"].ap()),
            inc=("ldw", 16))
    pr.wait("tensor", "ldg", pr.val("ldg"))
    pr.wait("tensor", "ldw", pr.val("ldw"))
    pr.wait("vector", "ldw", pr.val("ldw"))

    for qc in range(4):
        pr.wait("tensor", "dve", pr.val("dve"))
        for dc in range(DC):
            pr.emit("tensor", lambda e, dc=dc, qc=qc: e.matmul(
                P["PST"][:, :], A["WGS"][:, _sl(dc * 8, 8)],
                A["LNX"][:, _sl(dc * T + qc * 512, 512)],
                start=(dc == 0), stop=(dc == DC - 1)),
                inc=("pe", 1) if dc == DC - 1 else None)
        pr.wait("vector", "pe", pr.val("pe"))
        pr.emit("vector", lambda e, qc=qc: e.tensor_scalar(
            A["GS"][:, _sl(qc * 512, 512)], P["PST"][0:8, :], A["bgs"][:, 0:1],
            None, ALU.add), inc=("dve", 1))
    pr.wait("tensor", "dve", pr.val("dve"))
    for tt in range(16):
        pr.emit("tensor", lambda e, tt=tt: e.matmul(
            P["PTR"][:, _sl(tt * 8, 8)], A["GS"][:, _sl(tt * 128, 128)],
            A["IDS"][0:8, 0:8], start=True, stop=True),
            inc=("pe", 1) if tt == 15 else None)
    pr.wait("vector", "pe", pr.val("pe"))

    def vstep(fn):
        pr.emit("vector", fn, inc=("dve", 1))
        pr.wait("vector", "dve", pr.val("dve"))

    def seg(t, w=8):
        return t.ap().rearrange("p (s e) -> p s e", e=w)

    def segb(t, c0):
        v = t.ap()[:, _sl(c0, 16)].rearrange("p (s o) -> p s o", o=1)
        return v.broadcast_to([128, 16, 8])

    vstep(lambda e: e.tensor_copy(A["GTA"][:, :], P["PTR"][:, :]))
    vstep(lambda e: e.reduce_max(A["REDA"][:, 0:16], seg(A["GTA"]), AX.X))
    vstep(lambda e: e.tensor_tensor(seg(A["GEB"]), seg(A["GTA"]),
                                    segb(A["REDA"], 0), ALU.subtract))
    am0 = pr.val("dve")
    pr.wait("scalar", "dve", am0)
    pr.emit("scalar", lambda e: e.activation(A["GEC"][:, :], A["GEB"][:, :], AF.Exp),
            inc=("act", 1))
    pr.wait("vector", "act", pr.val("act"))
    vstep(lambda e: e.reduce_sum(A["REDA"][:, 16:32], seg(A["GEC"]), AX.X))
    vstep(lambda e: e.reciprocal(A["REDA"][:, 32:48], A["REDA"][:, 16:32]))
    vstep(lambda e: e.tensor_tensor(seg(A["GEC"]), seg(A["GEC"]),
                                    segb(A["REDA"], 32), ALU.mult))
    vstep(lambda e: e.reduce_max(A["REDA"][:, 0:16], seg(A["GEC"]), AX.X))
    vstep(lambda e: e.tensor_tensor(seg(A["GEB"]), seg(A["GEC"]),
                                    segb(A["REDA"], 0), ALU.is_equal))
    vstep(lambda e: e.scalar_tensor_tensor(A["GEB"][:, :], A["GEB"][:, :], -4.0,
                                           A["GEC"][:, :], ALU.mult, ALU.add))
    vstep(lambda e: e.reduce_max(A["REDA"][:, 16:32], seg(A["GEB"]), AX.X))
    vstep(lambda e: e.tensor_tensor(seg(A["GEB"]), seg(A["GEC"]),
                                    segb(A["REDA"], 16), ALU.is_ge))
    vstep(lambda e: e.tensor_tensor(A["GTA"][:, :], A["GEC"][:, :], A["GEB"][:, :],
                                    ALU.mult))
    for tt in range(16):
        pr.wait("tensor", "dve", pr.val("dve"))
        pr.emit("tensor", lambda e, tt=tt: e.matmul(
            P["PTR"][0:8, 0:128], A["GTA"][:, _sl(tt * 8, 8)], A["IDS"][:, :],
            start=True, stop=True), inc=("pe", 1))
        pr.wait("vector", "pe", pr.val("pe"))
        pr.emit("vector", lambda e, tt=tt: e.tensor_copy(
            A["W8B"][:, _sl(tt * 128, 128)], P["PTR"][0:8, 0:128]), inc=("dve", 1))
    wm = pr.val("dve")
    pr.wait("gpsimd", "dve", wm)
    pr.emit("gpsimd", lambda e: e.dma_start(out=tn["w8d"].ap(), in_=A["W8B"][:, :]),
            inc=("st", 16))
    w8st = pr.val("st")
    pr.wait("tensor", "dve", wm)

    marks = []
    for dc in range(DC):
        for qc in range(4):
            half = (dc * 4 + qc) % 4
            ps = [P["B0"][:, :], P["B1"][:, :], P["B2"][:, :],
                  P["B3"][:, :]][half]
            if len(marks) >= 4:
                pr.wait("tensor", "dve", marks[-4])
            pr.emit("tensor", lambda e, ps=ps, dc=dc, qc=qc: e.matmul(
                ps, A["B2S"][:, _sl(dc * 128, 128)], A["W8B"][:, _sl(qc * 512, 512)],
                start=True, stop=True), inc=("pe", 1))
            pr.wait("vector", "pe", pr.val("pe"))
            pr.emit("vector", lambda e, ps=ps, dc=dc, qc=qc: e.tensor_copy(
                A["Z"][:, _sl(dc * T + qc * 512, 512)], ps), inc=("dve", 1))
            marks.append(pr.val("dve"))

    pr.wait("gpsimd", "pe", pr.val("pe"))
    pr.wait("gpsimd", "dve", pr.val("dve"))
    pr.wait("gpsimd", "act", pr.val("act"))
    w1_marks = [None, None]
    w2_marks = [None, None]
    wbc_marks = [None, None]
    b1_mark = None
    for ex in range(E8):
        pr.wait("sync", "hev", pr.val("hev"))
        pr.emit("sync", lambda e, ex=ex: e.dma_start(
            out=A["b1s"][:, :], in_=tn["b1t"].ap()[ex]), inc=("ldw", 16))
        b1m = pr.val("ldw")
        pr.wait("vector", "ldw", b1m)
        pr.wait("scalar", "ldw", b1m)
        for tc in range(4):
            tb = tc % 2
            if wbc_marks[tb] is not None:
                pr.wait("gpsimd", "zev", wbc_marks[tb])
            pr.wait("gpsimd", "st", w8st)
            pr.emit("gpsimd", lambda e, ex=ex, tc=tc, tb=tb: e.dma_start(
                out=A["WBC"][:, _sl(tb * 512, 512)],
                in_=tn["w8d"].ap()[_sl(ex, 1), _sl(tc * 512, 512)].broadcast_to(
                    [128, 512])), inc=("ldg", 16))
            wbcm = pr.val("ldg")
            pr.wait("tensor", "dve", pr.val("dve"))
            pr.wait("tensor", "zev", pr.val("zev"))
            if ex == 0 and tc == 0:
                pr.wait("scalar", "st", w8st)
                pr.wait("vector", "st", w8st)
            hmarks = []
            ldms = {}
            for fc in range(32):
                if fc % 4 == 0:
                    grp = (fc // 4) % 2
                    if w1_marks[grp] is not None:
                        pr.wait("gpsimd", "pe", w1_marks[grp])
                    pr.emit("gpsimd", lambda e, ex=ex, fc=fc, grp=grp: e.dma_start(
                        out=A["W1B"][:, _sl(grp * 4096, 4096)].rearrange(
                            "p (q c) -> p q c", q=4),
                        in_=tn["w1t"].ap()[ex].rearrange(
                            "s p c -> p s c")[:, _sl(fc, 4), :]),
                        inc=("ldg", 16))
                    ldms[fc // 4] = pr.val("ldg")
                grp = (fc // 4) % 2
                wb = grp * 4 + (fc % 4)
                half = fc % 4
                ps = [P["B0"][:, :], P["B1"][:, :], P["B2"][:, :],
                      P["B3"][:, :]][half]
                pr.wait("tensor", "ldg", ldms[fc // 4])
                if len(hmarks) >= 4:
                    pr.wait("tensor", "hev", hmarks[-4])
                for dc in range(DC):
                    pr.emit("tensor", lambda e, ps=ps, wb=wb, dc=dc, tc=tc: e.matmul(
                        ps, A["W1B"][:, _sl(wb * 1024 + dc * 128, 128)],
                        A["LNX"][:, _sl(dc * T + tc * 512, 512)],
                        start=(dc == 0), stop=(dc == DC - 1)),
                        inc=("pe", 1) if dc == DC - 1 else None)
                if fc % 4 == 3:
                    w1_marks[(fc // 4) % 2] = pr.val("pe")
                pm = pr.val("pe")
                if fc % 2 == 0:
                    pr.wait("scalar", "pe", pm)
                    pr.emit("scalar", lambda e, ps=ps, fc=fc: e.activation(
                        A["HSL"][:, _sl(fc * 512, 512)], ps, AF.Relu,
                        bias=A["b1s"][:, _sl(fc, 1)]), inc=("hev", 1))
                else:
                    pr.wait("vector", "pe", pm)
                    pr.emit("vector", lambda e, ps=ps, fc=fc: e.tensor_scalar(
                        A["HSL"][:, _sl(fc * 512, 512)], ps, A["b1s"][:, _sl(fc, 1)],
                        0.0, ALU.add, ALU.max), inc=("hev", 1))
                hmarks.append(pr.val("hev"))
            hall = pr.val("hev")
            pr.wait("tensor", "hev", hall)
            for dc in range(DC):
                wb2 = dc % 2
                if w2_marks[wb2] is not None:
                    pr.wait("gpsimd", "pe", w2_marks[wb2])
                pr.emit("gpsimd", lambda e, ex=ex, dc=dc, wb2=wb2: e.dma_start(
                    out=A["W2B"][:, _sl(wb2 * F1, F1)],
                    in_=tn["w2t"].ap()[ex, dc]), inc=("ldg", 16))
                ldm = pr.val("ldg")
                half = dc % 4
                ps = [P["B0"][:, :], P["B1"][:, :], P["B2"][:, :],
                      P["B3"][:, :]][half]
                pr.wait("tensor", "ldg", ldm)
                pr.wait("tensor", "zev", max(0, pr.val("zev") - 3))
                for fc in range(32):
                    pr.emit("tensor", lambda e, ps=ps, wb2=wb2, fc=fc: e.matmul(
                        ps, A["W2B"][:, _sl(wb2 * F1 + fc * 128, 128)],
                        A["HSL"][:, _sl(fc * 512, 512)],
                        start=(fc == 0), stop=(fc == 31)),
                        inc=("pe", 1) if fc == 31 else None)
                w2_marks[wb2] = pr.val("pe")
                pm = pr.val("pe")
                pr.wait("vector", "pe", pm)
                pr.wait("vector", "ldg", wbcm)
                pr.emit("vector", lambda e, ps=ps, tb=tb: e.tensor_tensor(
                    A["TMI"][:, :], ps, A["WBC"][:, _sl(tb * 512, 512)], ALU.mult))
                pr.emit("vector", lambda e, dc=dc, tc=tc: e.tensor_tensor(
                    A["Z"][:, _sl(dc * T + tc * 512, 512)],
                    A["Z"][:, _sl(dc * T + tc * 512, 512)], A["TMI"][:, :], ALU.add),
                    inc=("zev", 1))
            wbc_marks[tb] = pr.val("zev")

    emit_ln(ctx, "Z", BF16, "res")


def emit_l1a(ctx):
    pr, nc, tn, A, P = ctx["pr"], ctx["nc"], ctx["tn"], ctx["A"], ctx["P"]
    pr.emit("sync", lambda e: e.dma_start(out=A["MSK"][:, :], in_=tn["mask1"].ap()),
            inc=("ldw", 16))

    emit_ln(ctx, "X", F32, "pre")
    pr.wait("tensor", "dve", pr.val("dve"))
    if DEBUG_PHASE < 2:
        return
    pr.wait("sync", "dve", pr.val("dve"))
    pr.emit("sync", lambda e: e.dma_start(out=A["COS"][:, :], in_=tn["cos4"].ap()),
            inc=("ldw", 16))
    pr.emit("sync", lambda e: e.dma_start(out=A["SIN"][:, :], in_=tn["sin4"].ap()),
            inc=("ldw", 16))
    pr.wait("vector", "ldw", pr.val("ldw"))

    stl = {"last": None}

    pr.wait("gpsimd", "pe", pr.val("pe"))
    for wi, wname in enumerate(("wk1a", "wk1b")):
        pr.emit("gpsimd", lambda e, wi=wi, wname=wname: e.dma_start(
            out=A["WB0"][:, _sl(wi * 512, 512)].rearrange("p (c f) -> p c f", f=64),
            in_=tn[wname].ap().rearrange("(c p) f -> p c f", p=128)),
            inc=("ldg", 16))
    pr.wait("tensor", "ldg", pr.val("ldg"))
    pr.wait("tensor", "dve", pr.val("dve"))
    for wi in range(2):
        DST = A["KTA"] if wi == 0 else A["KTB"]
        for qc in range(4):
            ps = P["B0"][:, :] if qc % 2 == 0 else P["B2"][:, :]
            pr.wait("tensor", "dve", max(0, pr.val("dve") - 1))
            for dc in range(DC):
                pr.emit("tensor", lambda e, ps=ps, wi=wi, dc=dc, qc=qc: e.matmul(
                    ps[0:64, :], A["WB0"][:, _sl(wi * 512 + dc * 64, 64)],
                    A["LNX"][:, _sl(dc * T + qc * 512, 512)],
                    start=(dc == 0), stop=(dc == DC - 1)),
                    inc=("pe", 1) if dc == DC - 1 else None)
            pr.wait("vector", "pe", pr.val("pe"))
            pr.emit("vector", lambda e, ps=ps, DST=DST, qc=qc: e.tensor_copy(
                DST[:, _sl(qc * 512, 512)], ps[0:64, :]), inc=("dve", 1))
    pr.emit("vector", lambda e: e.tensor_tensor(
        A["WRKA"][0:64, :], A["KTA"][:, :], A["COS"][0:64, :], ALU.mult))
    pr.emit("vector", lambda e: e.tensor_tensor(
        A["WRKB"][0:64, :], A["KTB"][:, :], A["SIN"][0:64, :], ALU.mult))
    pr.emit("vector", lambda e: e.tensor_tensor(
        A["STG"][0:64, :], A["WRKA"][0:64, :], A["WRKB"][0:64, :], ALU.add),
        inc=("dve", 1))
    km = pr.val("dve")
    pr.wait("gpsimd", "dve", km)
    pr.emit("gpsimd", lambda e: e.dma_start(
        out=tn["g1_src"].ap()[_sl(1024, 64), :], in_=A["STG"][0:64, :]),
        inc=("st", 16))
    stl["last"] = pr.val("st")
    if DEBUG_PHASE < 3:
        return

    pr.wait("gpsimd", "pe", pr.val("pe"))
    pr.wait("gpsimd", "dve", pr.val("dve"))
    pr.emit("gpsimd", lambda e: e.dma_start(
        out=A["WVs"].ap().rearrange("p (c f) -> p c f", f=64),
        in_=tn["wv1"].ap().rearrange("(c p) f -> p c f", p=128)),
        inc=("ldg", 16))
    pr.wait("tensor", "ldg", pr.val("ldg"))
    pr.wait("tensor", "dve", pr.val("dve"))
    for tt in range(16):
        ps = P["B0"][:, :] if tt % 2 == 0 else P["B2"][:, :]
        pr.wait("tensor", "dve", max(0, pr.val("dve") - 1))
        for dc in range(DC):
            pr.emit("tensor", lambda e, ps=ps, dc=dc, tt=tt: e.matmul(
                ps[:, 0:64], A["LNX"][:, _sl(dc * T + tt * 128, 128)],
                A["WVs"][:, _sl(dc * 64, 64)], start=(dc == 0), stop=(dc == DC - 1)),
                inc=("pe", 1) if dc == DC - 1 else None)
        pr.wait("vector", "pe", pr.val("pe"))
        if stl["last"] is not None:
            pr.wait("vector", "st", stl["last"])
        pr.emit("vector", lambda e, ps=ps: e.tensor_copy(A["STG"][:, 0:64], ps[:, 0:64]),
                inc=("dve", 1))
        vm = pr.val("dve")
        pr.wait("gpsimd", "dve", vm)
        pr.emit("gpsimd", lambda e, tt=tt: e.dma_start(
            out=tn["g2_src"].ap()[_sl(tt * 128, 128), 0:64], in_=A["STG"][:, 0:64]),
            inc=("st", 16))
        stl["last"] = pr.val("st")
    pr.wait("gpsimd", "st", pr.val("st"))
    pr.emit("gpsimd", lambda e: e.collective_compute(
        "AllGather", ALU.bypass, replica_groups=GROUPS,
        ins=[tn["g2_src"].ap()], outs=[tn["g2_out"].ap()]), inc=("cc", 1))

    if DEBUG_PHASE < 4:
        return
    pr.wait("gpsimd", "pe", pr.val("pe"))
    for dc in range(DC):
        pr.emit("gpsimd", lambda e, dc=dc: e.dma_start(
            out=A["WB0"][:, _sl(dc * D, D)],
            in_=tn["wqT1a"].ap()[_sl(dc * 128, 128), :]), inc=("ldg", 16))
    qa_m = pr.val("ldg")
    pr.wait("tensor", "ldg", qa_m)
    pr.wait("tensor", "dve", pr.val("dve"))
    marks = []
    for oc in range(8):
        for qc in range(4):
            half = (oc * 4 + qc) % 4
            ps = [P["B0"][:, :], P["B1"][:, :], P["B2"][:, :],
                  P["B3"][:, :]][half]
            if len(marks) >= 4:
                pr.wait("tensor", "dve", marks[-4])
            for dc in range(DC):
                pr.emit("tensor", lambda e, ps=ps, dc=dc, oc=oc, qc=qc: e.matmul(
                    ps, A["WB0"][:, _sl(dc * D + oc * 128, 128)],
                    A["LNX"][:, _sl(dc * T + qc * 512, 512)],
                    start=(dc == 0), stop=(dc == DC - 1)),
                    inc=("pe", 1) if dc == DC - 1 else None)
            pr.wait("vector", "pe", pr.val("pe"))
            pr.emit("vector", lambda e, ps=ps, oc=oc, qc=qc: e.tensor_copy(
                A["Z"][:, _sl(oc * T + qc * 512, 512)], ps), inc=("dve", 1))
            marks.append(pr.val("dve"))
    qa_all = pr.val("dve")
    pr.wait("gpsimd", "pe", pr.val("pe"))
    for dc in range(DC):
        pr.emit("gpsimd", lambda e, dc=dc: e.dma_start(
            out=A["WB0"][:, _sl(dc * D, D)],
            in_=tn["wqT1b"].ap()[_sl(dc * 128, 128), :]), inc=("ldg", 16))
    pr.wait("tensor", "ldg", pr.val("ldg"))
    pr.wait("tensor", "dve", pr.val("dve"))
    for oc in range(8):
        for qc in range(4):
            half = (oc * 4 + qc) % 4
            ps = [P["B0"][:, :], P["B1"][:, :], P["B2"][:, :],
                  P["B3"][:, :]][half]
            if len(marks) >= 4:
                pr.wait("tensor", "dve", marks[-4])
            for dc in range(DC):
                pr.emit("tensor", lambda e, ps=ps, dc=dc, oc=oc, qc=qc: e.matmul(
                    ps, A["WB0"][:, _sl(dc * D + oc * 128, 128)],
                    A["LNX"][:, _sl(dc * T + qc * 512, 512)],
                    start=(dc == 0), stop=(dc == DC - 1)),
                    inc=("pe", 1) if dc == DC - 1 else None)
            pr.wait("vector", "pe", pr.val("pe"))
            if qc == 0 and stl["last"] is not None:
                pr.wait("vector", "st", stl["last"])
            pr.emit("vector", lambda e, ps=ps, qc=qc: e.tensor_copy(
                A["STG"][:, _sl(qc * 512, 512)], ps), inc=("dve", 1))
            marks.append(pr.val("dve"))
        pr.emit("vector", lambda e, oc=oc: e.tensor_tensor(
            A["WRKA"][:, :], A["Z"][:, _sl(oc * T, T)], A["COS"][:, :], ALU.mult))
        pr.emit("vector", lambda e: e.tensor_tensor(
            A["WRKB"][:, :], A["STG"][:, :], A["SIN"][:, :], ALU.mult))
        if stl["last"] is not None:
            pr.wait("vector", "st", stl["last"])
        pr.emit("vector", lambda e: e.tensor_tensor(
            A["STG"][:, :], A["WRKA"][:, :], A["WRKB"][:, :], ALU.add), inc=("dve", 1))
        qm = pr.val("dve")
        pr.wait("gpsimd", "dve", qm)
        pr.emit("gpsimd", lambda e, oc=oc: e.dma_start(
            out=tn["g1_src"].ap()[_sl(oc * 128, 128), :], in_=A["STG"][:, :]),
            inc=("st", 16))
        stl["last"] = pr.val("st")
    pr.wait("gpsimd", "st", pr.val("st"))
    pr.emit("gpsimd", lambda e: e.collective_compute(
        "AllGather", ALU.bypass, replica_groups=GROUPS,
        ins=[tn["g1_src"].ap()], outs=[tn["g1_out"].ap()]), inc=("cc", 1))

    if DEBUG_PHASE < 5:
        return
    emit_attn(ctx, layer=1)
    if DEBUG_PHASE < 6:
        return

    pr.wait("gpsimd", "st", pr.val("st"))
    pr.emit("gpsimd", lambda e: e.collective_compute(
        "AllGather", ALU.bypass, replica_groups=GROUPS,
        ins=[tn["g3_src"].ap()], outs=[tn["g3_out"].ap()]), inc=("cc", 1))
    ccz = pr.val("cc")
    pr.wait("sync", "cc", ccz)
    pr.wait("sync", "dve", pr.val("dve"))
    def ldz1(e):
        c0 = ctx["get_sym"](e, "tcol")
        src = tn["g3_out"].ap().rearrange("(c p) t -> p c t", p=128)
        src = src[:, :, bass.ds(c0, T)]
        return e.dma_start(out=A["Z"].ap().rearrange("p (c t) -> p c t", t=T),
                           in_=src)
    pr.emit("sync", ldz1, inc=("ldw", 16))
    zm = pr.val("ldw")
    pr.wait("tensor", "ldw", zm)
    pr.wait("vector", "ldw", zm)
    def ocb(wi, oc, qc, ps):
        pr.emit("vector", lambda e, ps=ps, oc=oc, qc=qc: e.tensor_copy(
            A["LNX"][:, _sl(oc * T + qc * 512, 512)], ps), inc=("dve", 1))
    emit_proj(ctx, ["woT"], D, ocb, rhs_slab="Z")
    emit_ln(ctx, "LNX", BF16, "res")


def emit_l1f(ctx):
    pr, nc, tn, A, P = ctx["pr"], ctx["nc"], ctx["tn"], ctx["A"], ctx["P"]
    emit_ln(ctx, "X", F32, "pre")
    pr.wait("tensor", "dve", pr.val("dve"))

    cast_marks = {}
    def cast_load(idx, src_ap, ncols, dst, dmark):
        if dmark is not None:
            pr.wait("sync", "pe", dmark)
        pr.wait("sync", "cst", cast_marks.get("wi", 0))
        pr.emit("sync", lambda e, src_ap=src_ap, ncols=ncols: e.dma_start(
            out=A["WI"][:, 0:ncols], in_=src_ap), inc=("ldw", 16))
        lm = pr.val("ldw")
        pr.wait("vector", "ldw", lm)
        pr.emit("vector", lambda e, ncols=ncols, dst=dst: e.tensor_scalar(
            dst[:, 0:ncols], A["WI"][:, 0:ncols], Q127, None, ALU.mult),
            inc=("cst", 1))
        cast_marks["wi"] = pr.val("cst")
        return pr.val("cst")

    wf_mark = [None, None]
    for tc in range(2):
        for hc in range(24):
            m0 = cast_load(0, tn["wgT1"].ap()[hc],
                           1024, A["WF0"], wf_mark[0])
            m1 = cast_load(1, tn["wuT1"].ap()[hc],
                           1024, A["WF1"], wf_mark[1])
            pr.wait("tensor", "cst", m1)
            for tq in range(2):
                psg = [P["B0"], P["B1"]][tq][:, :]
                psu = [P["B2"], P["B3"]][tq][:, :]
                pr.wait("tensor", "hev", max(0, pr.val("hev") - 1))
                for dc in range(DC):
                    rhs = A["LNX"][:, _sl(dc * T + tc * 1024 + tq * 512, 512)]
                    pr.emit("tensor", lambda e, psg=psg, dc=dc, rhs=rhs: e.matmul(
                        psg, A["WF0"][:, _sl(dc * 128, 128)], rhs,
                        start=(dc == 0), stop=(dc == DC - 1)))
                for dc in range(DC):
                    rhs = A["LNX"][:, _sl(dc * T + tc * 1024 + tq * 512, 512)]
                    pr.emit("tensor", lambda e, psu=psu, dc=dc, rhs=rhs: e.matmul(
                        psu, A["WF1"][:, _sl(dc * 128, 128)], rhs,
                        start=(dc == 0), stop=(dc == DC - 1)),
                        inc=("pe", 1) if dc == DC - 1 else None)
                pm = pr.val("pe")
                pr.wait("scalar", "pe", pm)
                pr.emit("scalar", lambda e, psg=psg: e.activation(
                    A["SIL"][:, :], psg, AF.Silu), inc=("act", 1))
                pr.wait("vector", "act", pr.val("act"))
                pr.emit("vector", lambda e, psu=psu, hc=hc, tq=tq: e.tensor_tensor(
                    A["HSL1"][:, _sl(hc * 1024 + tq * 512, 512)], A["SIL"][:, :],
                    psu, ALU.mult), inc=("hev", 1))
            wf_mark[0] = pr.val("pe")
            wf_mark[1] = pr.val("pe")
        hall = pr.val("hev")
        pr.wait("tensor", "hev", hall)
        wd_mark = None
        for dc in range(DC):
            md = cast_load(2, tn["wdT1"].ap()[dc],
                           3072, A["WF0"], wd_mark)
            pr.wait("tensor", "cst", md)
            for tq in range(2):
                ps = [P["B0"], P["B1"]][tq][:, :]
                pr.wait("tensor", "zev", max(0, pr.val("zev") - 1))
                for hc in range(24):
                    pr.emit("tensor", lambda e, ps=ps, hc=hc, tq=tq: e.matmul(
                        ps, A["WF0"][:, _sl(hc * 128, 128)],
                        A["HSL1"][:, _sl(hc * 1024 + tq * 512, 512)],
                        start=(hc == 0), stop=(hc == 23)),
                        inc=("pe", 1) if hc == 23 else None)
                pm = pr.val("pe")
                pr.wait("vector", "pe", pm)
                pr.emit("vector", lambda e, ps=ps, dc=dc, tc=tc, tq=tq: e.tensor_copy(
                    A["Z"][:, _sl(dc * T + tc * 1024 + tq * 512, 512)], ps),
                    inc=("zev", 1))
            wd_mark = pr.val("pe")
        wf_mark[0] = pr.val("pe")

    emit_ln(ctx, "Z", BF16, "res")


def _prep(inputs):
    pc = {}
    x = np.asarray(inputs["L0_attn_embedded"], dtype=np.float32)
    pc["x_in"] = [np.ascontiguousarray(x[c // 4, (c % 4) * T:(c % 4 + 1) * T, :].T)
                  for c in range(NCORE)]

    def rep(a):
        a = np.ascontiguousarray(a)
        return [a] * NCORE

    pc["wqT"] = rep(np.asarray(inputs["L0_attn_Wq"], np.float32).T)
    pc["wkT"] = rep(np.asarray(inputs["L0_attn_Wk"], np.float32).T)
    pc["wvT"] = rep(np.asarray(inputs["L0_attn_Wv"], np.float32).T)
    bq = np.asarray(inputs["L0_attn_bq"], np.float32).reshape(8, 128).T
    bk = np.asarray(inputs["L0_attn_bk"], np.float32).reshape(8, 128).T
    bv = np.asarray(inputs["L0_attn_bv"], np.float32).reshape(8, 128).T
    pc["bqkv"] = rep(np.concatenate([bq, bk, bv], axis=1))
    ga = np.asarray(inputs["L0_attn_gamma"], np.float32).reshape(8, 128).T
    be = np.asarray(inputs["L0_attn_beta"], np.float32).reshape(8, 128).T
    pc["gb"] = rep(np.concatenate([ga, be], axis=1))
    pc["bvrow"] = rep(np.asarray(inputs["L0_attn_bv"], np.float32).reshape(1, D))

    pc["wgt"] = rep(np.asarray(inputs["L0_ffn_Wg"], np.float32))
    pc["bg"] = rep(np.asarray(inputs["L0_ffn_bg"], np.float32).reshape(8, 1))
    w1 = np.asarray(inputs["L0_ffn_W1"], np.float32)
    pc["w1t"] = rep(np.transpose(w1.reshape(E8, DC, 128, 32, 128), (0, 3, 2, 1, 4))
                    .reshape(E8, 32, 128, 1024))
    b1 = np.asarray(inputs["L0_ffn_b1"], np.float32)
    pc["b1t"] = rep(np.transpose(b1.reshape(E8, 32, 128), (0, 2, 1)))
    w2 = np.asarray(inputs["L0_ffn_W2"], np.float32)
    pc["w2t"] = rep(np.transpose(w2.reshape(E8, 32, 128, DC, 128), (0, 3, 2, 1, 4))
                    .reshape(E8, DC, 128, F1))
    pc["b2"] = rep(np.asarray(inputs["L0_ffn_b2"], np.float32))

    perm_a = np.concatenate([np.concatenate([64 * h + np.arange(0, 64, 2),
                                             64 * h + np.arange(1, 64, 2)])
                             for h in range(H)])
    perm_b = np.concatenate([np.concatenate([64 * h + np.arange(1, 64, 2),
                                             64 * h + np.arange(0, 64, 2)])
                             for h in range(H)])
    wq1 = np.asarray(inputs["L1_attn_wq"], np.float32).T
    pc["wqT1a"] = rep(wq1[:, perm_a])
    pc["wqT1b"] = rep(wq1[:, perm_b])
    wk1 = np.asarray(inputs["L1_attn_wk"], np.float32)
    pa64 = np.concatenate([np.arange(0, 64, 2), np.arange(1, 64, 2)])
    pb64 = np.concatenate([np.arange(1, 64, 2), np.arange(0, 64, 2)])
    pc["wk1a"] = rep(wk1[:, pa64])
    pc["wk1b"] = rep(wk1[:, pb64])
    pc["wv1"] = rep(np.asarray(inputs["L1_attn_wv"], np.float32))
    pc["woT"] = rep(np.asarray(inputs["L1_attn_wo"], np.float32).T)
    freqs = 1.0 / (1000.0 ** (np.arange(0, HD, 2, dtype=np.float64) / HD))
    cos_l, sin_l = [], []
    for c in range(NCORE):
        posv = np.arange((c % 4) * T, (c % 4 + 1) * T, dtype=np.float64)
        ang = freqs[:, None] * posv[None, :]
        co = np.cos(ang).astype(np.float32)
        si = np.sin(ang).astype(np.float32)
        cos4 = np.tile(co, (4, 1))
        sin4 = np.concatenate([-si, si, -si, si], axis=0)
        cos_l.append(np.ascontiguousarray(cos4))
        sin_l.append(np.ascontiguousarray(sin4))
    pc["cos4"] = cos_l
    pc["sin4"] = sin_l

    wgq = np.asarray(inputs["L1_ffn_w_gate"], np.int32)
    wuq = np.asarray(inputs["L1_ffn_w_up"], np.int32)
    wdq = np.asarray(inputs["L1_ffn_w_down"], np.int32)
    pc["wgT1"] = rep(np.ascontiguousarray(np.transpose(
        wgq.T.reshape(DC, 128, 24, 128), (2, 1, 0, 3)).reshape(24, 128, 1024)))
    pc["wuT1"] = rep(np.ascontiguousarray(np.transpose(
        wuq.T.reshape(DC, 128, 24, 128), (2, 1, 0, 3)).reshape(24, 128, 1024)))
    pc["wdT1"] = rep(np.ascontiguousarray(np.transpose(
        wdq.T.reshape(24, 128, DC, 128), (2, 1, 0, 3)).reshape(DC, 128, H3)))
    return pc


_IN_NAMES = {
    "l0a": ["wqT", "wkT", "wvT", "bqkv", "gb", "bvrow"],
    "l0f": ["wgt", "bg", "w1t", "b1t", "w2t", "b2"],
    "l1a": ["wqT1a", "wqT1b", "wk1a", "wk1b", "wv1", "woT", "cos4", "sin4"],
    "l1f": ["wgT1", "wuT1", "wdT1"],
}


def kernel(**inputs):
    pc = _prep(inputs)
    xcur = pc["x_in"]
    for group in LAUNCH_PLAN:
        nc = build_program(group)
        in_maps = []
        for c in range(NCORE):
            m = {"x_in": np.ascontiguousarray(xcur[c])}
            for s in group:
                for n in _IN_NAMES[s]:
                    m[n] = pc[n][c]
            in_maps.append(m)
        res = run_bass_kernel_spmd(nc, in_maps, core_ids=list(range(NCORE)))
        xcur = [np.asarray(res.results[c]["x_out"]) for c in range(NCORE)]
    out = np.empty((B, S, D), dtype=np.float32)
    for c in range(NCORE):
        out[c // 4, (c % 4) * T:(c % 4 + 1) * T, :] = xcur[c].T
    return out
